# version 07
# baseline (speedup 1.0000x reference)
import numpy as np
import ml_dtypes
import concourse.bass as bass
import concourse.mybir as mybir
from contextlib import ExitStack
from concourse.bass_utils import run_bass_kernel_spmd

F32 = mybir.dt.float32
BF16 = mybir.dt.bfloat16
AF = mybir.ActivationFunctionType
ALU = mybir.AluOpType
AX = mybir.AxisListType

D = 2048
T = 2048
NT = T // 128
ND = D // 128
QLR, NQ = 1536, 12
KVLR, NKV = 512, 4
H = 16
DQ = 192
INTER = 8192
MOE = 1024
SDIM = INTER + MOE
HID1 = 5632
EPS_RMS = 1e-6
EPS_LN = 1e-5
EPS_GN = 1e-5
EPS_BIT = 1e-8
NEG = -1e30
QSC = float(DQ) ** -0.5

SEMS = ("dS", "dG", "tS", "vS", "cS", "vq", "cq", "gq", "dA0", "dA1", "dW0", "dW1", "dL", "dH0", "dH1", "dP0", "dP1")


class Phase:

    def __init__(self, nc, sems, S):
        self.nc, self.sems, self.S = nc, sems, S
        self.q = {k: [] for k in ("sync", "gpsimd", "tensor", "vector", "scalar")}

    CHAIN = {"vector": "vS", "scalar": "cS", "gpsimd": "gq"}

    def emit(self, eng, fn, wait=(), inc=()):
        wait = list(wait)
        inc = list(inc)
        ch = self.CHAIN.get(eng)
        is_dma = any(c.startswith(("dS", "dG", "dA", "dW")) and k >= 16 for c, k in inc)
        if ch is not None and fn is not None and not is_dma:
            inc = [(c, k) for c, k in inc if c != ch]
            wait.append((ch, self.S[ch]))
            inc.append((ch, 1))
            assert len(inc) == 1, f"multi-inc on {eng}: {inc}"
        for c, k in inc:
            self.S[c] += k
        self.q[eng].append((wait, fn, inc))

    def w(self, *ctrs):
        return [(c, self.S[c]) for c in ctrs]

    def at(self, ctr, val):
        return [(ctr, val)]

    def run(self):
        self.emit("sync", None, wait=self.w("dS", "dL", "dH0", "dH1", "dP0", "dP1"))
        self.emit("gpsimd", None, wait=self.w("dG", "dA0", "dA1", "dW0", "dW1"))
        nc = self.nc
        with nc.Block() as block:
            def mk(name):
                ops = self.q[name]

                def body(eng):
                    for waits, fn, incs in ops:
                        for c, v in waits:
                            if v > 0:
                                eng.wait_ge(self.sems[c], v)
                        if fn is None:
                            continue
                        ins = fn(eng)
                        for c, k in incs:
                            ins.then_inc(self.sems[c], k)
                return body

            block.sync(mk("sync"))
            block.gpsimd(mk("gpsimd"))
            block.tensor(mk("tensor"))
            block.vector(mk("vector"))
            block.scalar(mk("scalar"))
        nc.all_engine_barrier()


def ncj(j):
    return (128 * (j + 1) + 511) // 512


def build_program(upto=99, dbg=()):
    nc = bass.Bass("TRN2", target_bir_lowering=False)
    dbg = set(dbg)

    def din(name, shape, dt=F32):
        return nc.dram_tensor(name, shape, dt, kind="ExternalInput").ap()

    x_in = din("x", [T, D])
    wqaT = din("wqaT", [D, QLR])
    lnq_in = din("lnq", [128, NQ])
    wqbT = din("wqbT", [QLR, H * 256])
    wkvaT = din("wkvaT", [D, 640])
    lnkv_in = din("lnkv", [128, NKV])
    wkvbT = din("wkvbT", [KVLR, H * 256])
    woT = din("woT", [D, D])
    rwT = din("rwT", [D, 8])
    wgT = din("wgT", [D, INTER])
    wuT = din("wuT", [D, INTER])
    wrgT = din("wrgT", [D, MOE])
    wruT = din("wruT", [D, MOE])
    wdsT = din("wdsT", [SDIM, D])
    WQm = din("WQm", [D, D])
    hhb_in = din("hhb", [128, D])
    gnw_in = din("gnw", [128, D])
    gnb_in = din("gnb", [128, D])
    w1T = din("w1T", [D, HID1])
    w2T = din("w2T", [D, HID1])
    w3T = din("w3T", [HID1, D])
    identf = din("identf", [128, 128])
    onescol = din("onescol", [128, 1])
    onesrow = din("onesrow", [1, 128])
    maskd = din("maskd", [128, 128])
    cs_qk_in = din("cs_qk", [128, T])
    c32_in = din("c32", [T, 32])
    s32_in = din("s32", [T, 32])
    identb_in = din("identb", [128, 128], BF16)

    o_out = nc.dram_tensor("o", [T, D], F32, kind="ExternalOutput").ap()
    dbg_out = {}
    for name, shape in [
        ("xh_dbg", [D, T]), ("xn_dbg", [D, T]), ("cqn_dbg", [QLR, T]),
        ("ckv_dbg", [KVLR, T]), ("kpe_dbg", [64, T]), ("cat_dbg", [D, T]),
        ("comb_dbg", [128, NT]), ("thr_dbg", [128, 6]), ("xn1_dbg", [D, T]),
    ]:
        if name in dbg:
            dbg_out[name] = nc.dram_tensor(name, shape, F32, kind="ExternalOutput").ap()
    for name, shape in [("s_dbg", [SDIM, T]), ("hid_dbg", [HID1, T])]:
        if name in dbg:
            dbg_out[name] = nc.dram_tensor(name, shape, BF16, kind="ExternalOutput").ap()

    xh = nc.dram_tensor("xh", [D, T], F32, kind="Internal").ap()
    s_hbm = nc.dram_tensor("s_hbm", [SDIM, T], BF16, kind="Internal").ap()
    hid_hbm = nc.dram_tensor("hid_hbm", [HID1, T], BF16, kind="Internal").ap()
    bounce = nc.dram_tensor("bounce", [16, 128], F32, kind="Internal").ap()

    S = {k: 0 for k in SEMS}

    with ExitStack() as top:
        sems = {k: top.enter_context(nc.semaphore(k)) for k in SEMS}

        def sbuf(stack, name, shape, dt):
            return stack.enter_context(nc.sbuf_tensor(name, shape, dt))

        def psum(stack, name, shape, dt=F32):
            return stack.enter_context(nc.psum_tensor(name, shape, dt))

        arena = sbuf(top, "arena", [128, NT * T], BF16)
        ident_f = sbuf(top, "ident_f", [128, 128], F32)
        ident_b = sbuf(top, "ident_b", [128, 128], BF16)
        ones_c = sbuf(top, "ones_c", [128, 1], F32)
        ones_r = sbuf(top, "ones_r", [1, 128], F32)
        thr = sbuf(top, "thr", [128, 6], F32)
        cst = sbuf(top, "cst", [128, 4], F32)

        def arena_t(i):
            return arena[:, T * i:T * (i + 1)]

        def arena3(i):
            return arena[:, :].rearrange("p (c t) -> p c t", t=T)[:, :, 128 * i:128 * (i + 1)]

        def dbg_dump_fm(ph, name, src_fn, ntiles):
            if name not in dbg_out:
                return
            dst = dbg_out[name]
            for i in range(ntiles):
                ph.emit("gpsimd", lambda e, i=i: e.dma_start(
                    out=dst[128 * i:128 * (i + 1), :], in_=src_fn(i)),
                    wait=ph.w("vS", "tS", "cS"), inc=[("dG", 16)])

        p = Phase(nc, sems, S)
        for dst, src in [(ident_f, identf), (ident_b, identb_in),
                         (ones_c, onescol), (ones_r, onesrow)]:
            p.emit("sync", lambda e, d=dst, s=src: e.dma_start(out=d[:, :], in_=s),
                   inc=[("dS", 16)])
        for col, val in [(0, EPS_LN), (1, EPS_RMS), (2, EPS_GN)]:
            p.emit("vector", lambda e, col=col, val=val: e.memset(cst[:, col:col + 1], val),
                   inc=[("vS", 1)])
        p.run()

        if upto >= 1:
            p = Phase(nc, sems, S)
            with ExitStack() as c1:
                xt = [sbuf(c1, f"p1x{i}", [128, D], F32) for i in range(2)]
                xsq = sbuf(c1, "p1sq", [128, D], F32)
                st = sbuf(c1, "p1st", [128, 8], F32)
                xn_tm = sbuf(c1, "p1xn", [128, D], BF16)
                xh_str = sbuf(c1, "p1xh", [128, D], F32)
                wb = [sbuf(c1, f"p0w{i}", [128, HID1], F32) for i in range(2)]
                trash = sbuf(c1, "p0tr", [128, HID1], F32)
                racc3 = sbuf(c1, "p0acc", [128, 80], F32)
                rtmp = sbuf(c1, "p0tmp", [128, 1], F32)
                sc1 = sbuf(c1, "p0s1", [1, 1], F32)
                ps_b = psum(c1, "p1psb", [128, D], BF16)
                ps_f = psum(c1, "p1psf", [128, D])
                ps1 = psum(c1, "p0ps", [1, 1])
                psr = psum(c1, "p0psr", [128, 1])
                strips = [(w1T, i, HID1, i) for i in range(16)]
                strips += [(w2T, i, HID1, 16 + i) for i in range(16)]
                strips += [(w3T, i, D, 32 + i) for i in range(44)]
                abs_marks = {}

                def emit_strip(s):
                    wt, i, cols, col = strips[s]
                    spar = s % 2
                    p.emit("sync", lambda e, wt=wt, i=i, cols=cols, spar=spar: e.dma_start(
                        out=wb[spar][:, 0:cols], in_=wt[128 * i:128 * (i + 1), :]),
                        wait=(p.at("cS", abs_marks[s - 2]) if s >= 2 else ()),
                        inc=[(f"dP{spar}", 16)])
                    dlm = S[f"dP{spar}"]
                    p.emit("scalar", lambda e, cols=cols, col=col, spar=spar: e.activation(
                        trash[:, 0:cols], wb[spar][:, 0:cols], AF.Abs,
                        accum_out=racc3[:, col:col + 1]),
                        wait=p.at(f"dP{spar}", dlm), inc=[("cS", 1)])
                    abs_marks[s] = S["cS"]

                marks = {}
                for i in range(NT):
                    for s in range(5 * i, min(5 * (i + 1), len(strips))):
                        emit_strip(s)
                    par = i % 2
                    xv = xt[par]
                    wv = (p.at("tS", marks[i - 2][0]) + p.at("vS", marks[i - 2][1])
                          if i >= 2 else ())
                    p.emit("sync", lambda e, i=i, xv=xv: e.dma_start(
                        out=xv[:, :], in_=x_in[128 * i:128 * (i + 1), :]),
                        wait=wv, inc=[(f"dH{par}", 16)])
                    ld_mark = S[f"dH{par}"]
                    p.emit("scalar", lambda e, xv=xv: e.square(xsq[:, :], xv[:, :]),
                           wait=p.at(f"dH{par}", ld_mark) + p.w("vS"), inc=[("cS", 1)])
                    p.emit("vector", lambda e, xv=xv: e.reduce_sum(
                        st[:, 0:1], xv[:, :], AX.X), wait=p.at(f"dH{par}", ld_mark))
                    p.emit("vector", lambda e: e.reduce_sum(st[:, 1:2], xsq[:, :], AX.X),
                           wait=p.w("cS"))
                    p.emit("vector", lambda e: e.tensor_scalar(
                        st[:, 0:1], st[:, 0:1], 1.0 / D, None, ALU.mult))
                    p.emit("vector", lambda e: e.tensor_tensor(
                        st[:, 2:3], st[:, 0:1], st[:, 0:1], ALU.mult))
                    p.emit("vector", lambda e: e.tensor_scalar(
                        st[:, 3:4], st[:, 1:2], 1.0 / D, None, ALU.mult))
                    p.emit("vector", lambda e: e.tensor_tensor(
                        st[:, 4:5], st[:, 3:4], st[:, 2:3], ALU.subtract), inc=[("vS", 1)])
                    p.emit("scalar", lambda e: e.activation(
                        st[:, 5:6], st[:, 4:5], AF.Sqrt, bias=cst[:, 0:1]),
                        wait=p.w("vS"), inc=[("cS", 1)])
                    p.emit("vector", lambda e: e.reciprocal(st[:, 6:7], st[:, 5:6]),
                           wait=p.w("cS"))
                    p.emit("vector", lambda e, xv=xv: e.tensor_scalar(
                        xn_tm[:, :], xv[:, :], st[:, 0:1], st[:, 6:7],
                        ALU.subtract, ALU.mult), inc=[("vS", 1)])
                    xn_mark = S["vS"]
                    for c in range(ND):
                        p.emit("tensor", lambda e, c=c: e.transpose(
                            ps_b[:, 128 * c:128 * (c + 1)], xn_tm[:, 128 * c:128 * (c + 1)],
                            ident_b[:, :]),
                            wait=(p.at("vS", xn_mark) if c == 0 else ()),
                            inc=([("tS", 1)] if c == ND - 1 else ()))
                    bT = S["tS"]
                    for c in range(ND):
                        p.emit("tensor", lambda e, c=c, xv=xv: e.transpose(
                            ps_f[:, 128 * c:128 * (c + 1)], xv[:, 128 * c:128 * (c + 1)],
                            ident_f[:, :]),
                            inc=([("tS", 1)] if c == ND - 1 else ()))
                    fT = S["tS"]
                    p.emit("vector", lambda e, i=i: e.tensor_copy(
                        arena3(i), ps_b[:, :].rearrange("p (c n) -> p c n", n=128)),
                        wait=p.at("tS", bT), inc=[("vS", 1)])
                    p.emit("vector", lambda e: e.tensor_copy(xh_str[:, :], ps_f[:, :]),
                           wait=p.at("tS", fT) + (p.w("dS") if i > 0 else []),
                           inc=[("vS", 1)])
                    fEv = S["vS"]
                    p.emit("sync", lambda e, i=i: e.dma_start(
                        out=xh[:, 128 * i:128 * (i + 1)].rearrange("(c p) n -> p c n", p=128),
                        in_=xh_str[:, :].rearrange("p (c n) -> p c n", n=128)),
                        wait=p.at("vS", fEv), inc=[("dS", 16)])
                    marks[i] = (fT, fEv)
                for s in range(5 * NT, len(strips)):
                    emit_strip(s)
                for mi, (a, b, nelem) in enumerate([(0, 16, D * HID1), (16, 32, D * HID1),
                                                    (32, 76, HID1 * D)]):
                    p.emit("vector", lambda e, a=a, b=b: e.tensor_reduce(
                        rtmp[:, :], racc3[:, a:b], AX.X, ALU.add),
                        wait=p.at("cS", abs_marks[len(strips) - 1]))
                    p.emit("vector", lambda e: e.engine_nop(), inc=[("vS", 1)])
                    p.emit("tensor", lambda e: e.matmul(
                        ps1[:, :], ones_c[:, :], rtmp[:, :], start=True, stop=True),
                        wait=p.w("vS"), inc=[("tS", 1)])
                    p.emit("vector", lambda e, nelem=nelem: e.tensor_scalar(
                        sc1[:, :], ps1[:, :], 0.5 / nelem, 0.5 * EPS_BIT, ALU.mult, ALU.add),
                        wait=p.w("tS"), inc=[("vS", 1)])
                    p.emit("tensor", lambda e: e.matmul(
                        psr[:, :], ones_r[:, :], sc1[:, :], start=True, stop=True),
                        wait=p.w("vS"), inc=[("tS", 1)])
                    p.emit("vector", lambda e, mi=mi: e.tensor_copy(
                        thr[:, 2 * mi:2 * mi + 1], psr[:, :]), wait=p.w("tS"))
                    p.emit("vector", lambda e, mi=mi: e.tensor_scalar(
                        thr[:, 2 * mi + 1:2 * mi + 2], psr[:, :], -1.0, None, ALU.mult),
                        inc=[("vS", 1)])
                if "thr_dbg" in dbg_out:
                    p.emit("sync", lambda e: e.dma_start(
                        out=dbg_out["thr_dbg"], in_=thr[:, :]),
                        wait=p.w("vS"), inc=[("dS", 16)])
                if "xh_dbg" in dbg_out and upto == 1:
                    p.emit("sync", lambda e: e.dma_start(out=dbg_out["xh_dbg"], in_=xh),
                           wait=p.w("dS"), inc=[("dS", 16)])
                dbg_dump_fm(p, "xn_dbg", arena_t, ND)
                p.run()

        if upto >= 2:
            p = Phase(nc, sems, S)
            with ExitStack() as c2:
                comb_bc = sbuf(c2, "comb_bc", [128, T], F32)
                with ExitStack() as cr:
                    rw_sb = sbuf(cr, "rw_sb", [128, NT * 8], BF16)
                    t8 = sbuf(cr, "p2t8", [128, 8], F32)
                    cb = sbuf(cr, "p2cb", [128, 8], F32)
                    combc = sbuf(cr, "p2cc", [128, NT], F32)
                    ct16 = sbuf(cr, "p2ct", [16, 128], F32)
                    crow = sbuf(cr, "p2crow", [1, T], F32)
                    ps_r = psum(cr, "p2psr", [128, 8])
                    ps_t = psum(cr, "p2pst", [16, 128])
                    ps_c = psum(cr, "p2psc", [128, T])
                    for k in range(ND):
                        p.emit("gpsimd", lambda e, k=k: e.dma_start(
                            out=rw_sb[:, 8 * k:8 * (k + 1)], in_=rwT[128 * k:128 * (k + 1), :]),
                            inc=[("dG", 16)])
                    for i in range(NT):
                        for k in range(ND):
                            p.emit("tensor", lambda e, i=i, k=k: e.matmul(
                                ps_r[:, :],
                                arena_t(k)[:, 128 * i:128 * (i + 1)],
                                rw_sb[:, 8 * k:8 * (k + 1)],
                                start=(k == 0), stop=(k == ND - 1)),
                                wait=(p.w("dG", "vS") if k == 0 else ()),
                                inc=([("tS", 1)] if k == ND - 1 else ()))
                        p.emit("vector", lambda e: e.reduce_max(cb[:, 0:1], ps_r[:, :], AX.X),
                               wait=p.w("tS"))
                        p.emit("vector", lambda e: e.tensor_scalar(
                            t8[:, :], ps_r[:, :], cb[:, 0:1], NEG, ALU.is_ge, ALU.mult))
                        p.emit("vector", lambda e: e.tensor_tensor(
                            t8[:, :], t8[:, :], ps_r[:, :], ALU.add))
                        p.emit("vector", lambda e: e.reduce_max(cb[:, 1:2], t8[:, :], AX.X))
                        p.emit("vector", lambda e: e.tensor_tensor(
                            cb[:, 2:3], cb[:, 1:2], cb[:, 0:1], ALU.subtract), inc=[("vS", 1)])
                        p.emit("scalar", lambda e: e.activation(
                            cb[:, 3:4], cb[:, 2:3], AF.Exp), wait=p.w("vS"), inc=[("cS", 1)])
                        p.emit("vector", lambda e: e.tensor_scalar(
                            cb[:, 4:5], cb[:, 3:4], 1.0, None, ALU.add), wait=p.w("cS"))
                        p.emit("vector", lambda e: e.reciprocal(cb[:, 5:6], cb[:, 4:5]))
                        p.emit("vector", lambda e, i=i: e.tensor_tensor(
                            combc[:, i:i + 1], cb[:, 4:5], cb[:, 5:6], ALU.mult),
                            inc=[("vS", 1)])
                    p.emit("tensor", lambda e: e.transpose(
                        ps_t[:, :], combc[:, :], ident_f[:, :]),
                        wait=p.w("vS"), inc=[("tS", 1)])
                    p.emit("vector", lambda e: e.tensor_copy(ct16[:, :], ps_t[:, :]),
                           wait=p.w("tS"), inc=[("vS", 1)])
                    p.emit("sync", lambda e: e.dma_start(out=bounce, in_=ct16[:, :]),
                           wait=p.w("vS"), inc=[("dS", 16)])
                    p.emit("sync", lambda e: e.dma_start(
                        out=crow[:, :], in_=bounce.rearrange("a b -> (a b)").rearrange("(o n) -> o n", o=1)),
                        wait=p.w("dS"), inc=[("dS", 16)])
                    for c in range(4):
                        p.emit("tensor", lambda e, c=c: e.matmul(
                            ps_c[:, 512 * c:512 * (c + 1)], ones_r[:, :],
                            crow[:, 512 * c:512 * (c + 1)], start=True, stop=True),
                            wait=(p.w("dS") if c == 0 else ()),
                            inc=([("tS", 1)] if c == 3 else ()))
                    p.emit("vector", lambda e: e.tensor_copy(comb_bc[:, :], ps_c[:, :]),
                           wait=p.w("tS"), inc=[("vS", 1)])
                    if "comb_dbg" in dbg_out:
                        p.emit("sync", lambda e: e.dma_start(
                            out=dbg_out["comb_dbg"], in_=combc[:, :]),
                            wait=p.w("vS"), inc=[("dS", 16)])
                wg2 = [sbuf(c2, f"wg2_{i}", [128, ND * 512], BF16) for i in range(2)]
                wu2 = [sbuf(c2, f"wu2_{i}", [128, ND * 512], BF16) for i in range(2)]
                t1 = [sbuf(c2, f"p2t1{i}", [128, 1024], BF16) for i in range(2)]
                s_str = [sbuf(c2, f"p2s{i}", [128, T], BF16) for i in range(2)]
                psG = [psum(c2, f"p2g{i}", [128, 1024]) for i in range(2)]
                psU = [psum(c2, f"p2u{i}", [128, 1024]) for i in range(2)]
                ogs = [(wgT, wuT, 512 * og, 512 * og, False) for og in range(16)]
                ogs += [(wrgT, wruT, 512 * r, INTER + 512 * r, True) for r in range(2)]
                gi = 0
                ev_mark = {}
                og_mm_marks = {}
                for og, (gsrc, usrc, col0, srow0, routed) in enumerate(ogs):
                    wpar = og % 2
                    lw = p.w("tS", "vS", "cS", "dS") if og == 0 else (
                        p.at("tS", og_mm_marks[og - 2]) if og >= 2 else ())
                    p.emit("gpsimd", lambda e, gsrc=gsrc, col0=col0, wpar=wpar: e.dma_start(
                        out=wg2[wpar][:, :].rearrange("p (k c) -> p k c", c=512),
                        in_=gsrc[:, col0:col0 + 512].rearrange("(k p) c -> p k c", p=128)),
                        wait=lw, inc=[(f"dW{wpar}", 16)])
                    p.emit("gpsimd", lambda e, usrc=usrc, col0=col0, wpar=wpar: e.dma_start(
                        out=wu2[wpar][:, :].rearrange("p (k c) -> p k c", c=512),
                        in_=usrc[:, col0:col0 + 512].rearrange("(k p) c -> p k c", p=128)),
                        inc=[(f"dW{wpar}", 16)])
                    dg_mark = S[f"dW{wpar}"]
                    for m in range(4):
                        s_par = m % 2
                        for hf in range(2):
                            par = gi % 2
                            gi += 1
                            wv = p.at(f"dW{wpar}", dg_mark)
                            if gi > 2:
                                wv = wv + p.at("vS", ev_mark[gi - 2])
                            for k in range(ND):
                                for c in range(2):
                                    p.emit("tensor", lambda e, k=k, c=c, m=m, hf=hf, par=par, wpar=wpar: e.matmul(
                                        psG[par][:, 512 * c:512 * (c + 1)],
                                        wg2[wpar][:, 512 * k + 128 * m:512 * k + 128 * (m + 1)],
                                        arena_t(k)[:, 1024 * hf + 512 * c:1024 * hf + 512 * (c + 1)],
                                        start=(k == 0), stop=(k == ND - 1)),
                                        wait=(wv if (k == 0 and c == 0) else ()),
                                        inc=([("tS", 1)] if (k == ND - 1 and c == 1) else ()))
                            g_mark = S["tS"]
                            for k in range(ND):
                                for c in range(2):
                                    p.emit("tensor", lambda e, k=k, c=c, m=m, hf=hf, par=par, wpar=wpar: e.matmul(
                                        psU[par][:, 512 * c:512 * (c + 1)],
                                        wu2[wpar][:, 512 * k + 128 * m:512 * k + 128 * (m + 1)],
                                        arena_t(k)[:, 1024 * hf + 512 * c:1024 * hf + 512 * (c + 1)],
                                        start=(k == 0), stop=(k == ND - 1)),
                                        inc=([("tS", 1)] if (k == ND - 1 and c == 1) else ()))
                            u_mark = S["tS"]
                            p.emit("scalar", lambda e, par=par: e.activation(
                                t1[par][:, :], psG[par][:, :], AF.Silu),
                                wait=p.at("tS", g_mark), inc=[("cS", 1)])
                            if routed:
                                p.emit("vector", lambda e, par=par, hf=hf: e.tensor_tensor(
                                    t1[par][:, :], t1[par][:, :],
                                    comb_bc[:, 1024 * hf:1024 * (hf + 1)], ALU.mult),
                                    wait=p.w("cS"))
                            p.emit("vector", lambda e, par=par, s_par=s_par, hf=hf: e.tensor_tensor(
                                s_str[s_par][:, 1024 * hf:1024 * (hf + 1)],
                                t1[par][:, :], psU[par][:, :], ALU.mult),
                                wait=p.at("tS", u_mark) + p.w("cS") +
                                (p.w("dS") if (hf == 0 and gi > 4) else []),
                                inc=[("vS", 1)])
                            ev_mark[gi] = S["vS"]
                        p.emit("sync", lambda e, srow0=srow0, m=m, s_par=s_par: e.dma_start(
                            out=s_hbm[srow0 + 128 * m:srow0 + 128 * (m + 1), :],
                            in_=s_str[s_par][:, :]),
                            wait=p.w("vS"), inc=[("dS", 16)])
                    og_mm_marks[og] = S["tS"]
                if "s_dbg" in dbg_out:
                    p.emit("sync", lambda e: e.dma_start(out=dbg_out["s_dbg"], in_=s_hbm),
                           wait=p.w("dS"), inc=[("dS", 16)])
                p.run()

        if upto >= 3:
            p = Phase(nc, sems, S)
            with ExitStack() as c3:
                ss_sb = sbuf(c3, "ss_sb", [128, ND * T], BF16)
                wd_sb = [sbuf(c3, f"wd{i}", [128, ND * 128], BF16) for i in range(2)]
                dtmp = [sbuf(c3, f"dt{i}", [128, T], F32) for i in range(2)]
                psD = [psum(c3, f"p3d{i}", [128, T]) for i in range(2)]
                slabs = [(0, 16), (2048, 16), (4096, 16), (6144, 16), (8192, 8)]
                acc_done = {}
                ev_mark = {}
                mm_marks = {}
                gi = 0
                units = []
                for si, (r0, nk) in enumerate(slabs):
                    for mo in range(ND):
                        units.append((si, r0, nk, mo))

                def emit_band(u):
                    si, r0, nk, mo = units[u]
                    par = (u + 1) % 2
                    p.emit("gpsimd", lambda e, r0=r0, nk=nk, mo=mo, par=par: e.dma_start(
                        out=wd_sb[par][:, 0:nk * 128].rearrange("p (k c) -> p k c", c=128),
                        in_=wdsT[r0:r0 + nk * 128, 128 * mo:128 * (mo + 1)].rearrange(
                            "(k p) c -> p k c", p=128)),
                        wait=(p.at("tS", mm_marks[u - 2]) if u >= 2 else ()),
                        inc=[(f"dW{par}", 16)])
                    return S[f"dW{par}"]

                band_mark = {}
                band_mark[0] = emit_band(0)
                band_mark[1] = emit_band(1)
                cur_slab = -1
                for u, (si, r0, nk, mo) in enumerate(units):
                    gi += 1
                    par = gi % 2
                    if si != cur_slab:
                        cur_slab = si
                        p.emit("sync", None, wait=p.w("tS"))
                        for k in range(nk):
                            p.emit("sync", lambda e, r0=r0, k=k: e.dma_start(
                                out=ss_sb[:, T * k:T * (k + 1)],
                                in_=s_hbm[r0 + 128 * k:r0 + 128 * (k + 1), :]),
                                inc=[("dS", 16)])
                        ds_mark = S["dS"]
                    wv = p.at("dS", ds_mark) + p.at(f"dW{par}", band_mark[u])
                    if gi > 2:
                        wv = wv + p.at("vS", ev_mark[gi - 2])
                    for k in range(nk):
                        for c in range(4):
                            p.emit("tensor", lambda e, k=k, c=c, par=par, nk=nk: e.matmul(
                                psD[par][:, 512 * c:512 * (c + 1)],
                                wd_sb[par][:, 128 * k:128 * (k + 1)],
                                ss_sb[:, T * k + 512 * c:T * k + 512 * (c + 1)],
                                start=(k == 0), stop=(k == nk - 1)),
                                wait=(wv if (k == 0 and c == 0) else ()),
                                inc=([("tS", 1)] if (k == nk - 1 and c == 3) else ()))
                    mm_marks[u] = S["tS"]
                    if u + 2 < len(units):
                        band_mark[u + 2] = emit_band(u + 2)
                    p.emit("vector", lambda e, par=par: e.tensor_copy(
                        dtmp[par][:, :], psD[par][:, :]),
                        wait=p.w("tS") + (p.at(f"dA{par}", acc_done[gi - 2]) if gi > 2 else []),
                        inc=[("vS", 1)])
                    ev_mark[gi] = S["vS"]
                    p.emit("gpsimd", lambda e, mo=mo, par=par: e.dma_start(
                        out=xh[128 * mo:128 * (mo + 1), :], in_=dtmp[par][:, :],
                        accum_op=ALU.add),
                        wait=p.w("vS") + (p.at(f"dA{par}", acc_done.get(gi - 16, 0))
                                          if gi > 16 else []),
                        inc=[(f"dA{par}", 16)])
                    acc_done[gi] = S[f"dA{par}"]
                if "xh_dbg" in dbg_out and upto == 3:
                    p.emit("sync", lambda e: e.dma_start(out=dbg_out["xh_dbg"], in_=xh),
                           wait=p.w("dG", "dA0", "dA1"), inc=[("dS", 16)])
                p.run()

        b_scope = ExitStack()
        cqn = sbuf(b_scope, "cqn", [128, NQ * T], BF16)
        ckv = sbuf(b_scope, "ckv", [128, NKV * T], BF16)
        kpe = sbuf(b_scope, "kpe", [64, T], BF16)
        cs_sb = sbuf(b_scope, "cs_sb", [128, T], BF16)

        def cqn_t(i):
            return cqn[:, T * i:T * (i + 1)]

        def ckv_t(i):
            return ckv[:, T * i:T * (i + 1)]

        if upto >= 4:
            p = Phase(nc, sems, S)
            with ExitStack() as c4:
                wa_sb = [sbuf(c4, f"p4w{i}", [128, ND * 128], BF16) for i in range(2)]
                lnq_sb = sbuf(c4, "lnq_sb", [128, NQ], F32)
                lnkv_sb = sbuf(c4, "lnkv_sb", [128, NKV], F32)
                sqf = sbuf(c4, "p4sq", [128, T], F32)
                srow = sbuf(c4, "p4srow", [1, T], F32)
                rstd_bc = sbuf(c4, "p4rstd", [128, T], F32)
                t1f = sbuf(c4, "p4t1", [128, T], F32)
                t64 = sbuf(c4, "p4t64", [64, T], F32)
                psC2 = [psum(c4, f"p4c{i}", [128, 1024]) for i in range(2)]
                psS = psum(c4, "p4s", [1, T])

                p.emit("sync", lambda e: e.dma_start(out=lnq_sb[:, :], in_=lnq_in),
                       inc=[("dS", 16)])
                p.emit("sync", lambda e: e.dma_start(out=lnkv_sb[:, :], in_=lnkv_in),
                       inc=[("dS", 16)])
                p.emit("gpsimd", lambda e: e.dma_start(out=cs_sb[:, :], in_=cs_qk_in),
                       inc=[("dG", 16)])
                pmm_marks = []

                def proj_rms(wsrcT, nm, n_rms, dst_t, ln_sb, scale):
                    ev_marks = {}
                    for m in range(nm):
                        par = m % 2
                        p.emit("gpsimd", lambda e, m=m, par=par, wsrcT=wsrcT: e.dma_start(
                            out=wa_sb[par][:, :].rearrange("p (k c) -> p k c", c=128),
                            in_=wsrcT[:, 128 * m:128 * (m + 1)].rearrange(
                                "(k p) c -> p k c", p=128)),
                            wait=(p.at("tS", pmm_marks[-2]) if len(pmm_marks) >= 2
                                  else ()),
                            inc=[(f"dW{par}", 16)])
                        dwm = S[f"dW{par}"]
                        for h2 in range(2):
                            hpar = (2 * m + h2) % 2
                            wv = p.at(f"dW{par}", dwm) + \
                                p.at("vS", ev_marks.get(2 * m + h2 - 2, 0)) + \
                                p.at("cS", ev_marks.get((2 * m + h2 - 2, "c"), 0))
                            for k in range(ND):
                                for c in range(2):
                                    p.emit("tensor", lambda e, k=k, c=c, h2=h2, par=par,
                                           hpar=hpar: e.matmul(
                                        psC2[hpar][:, 512 * c:512 * (c + 1)],
                                        wa_sb[par][:, 128 * k:128 * (k + 1)],
                                        arena_t(k)[:, 1024 * h2 + 512 * c:1024 * h2 + 512 * (c + 1)],
                                        start=(k == 0), stop=(k == ND - 1)),
                                        wait=(wv if (k == 0 and c == 0) else ()),
                                        inc=([("tS", 1)] if (k == ND - 1 and c == 1) else ()))
                            hm = S["tS"]
                            if m < n_rms:
                                p.emit("vector", lambda e, m=m, h2=h2, hpar=hpar: e.tensor_copy(
                                    dst_t(m)[:, 1024 * h2:1024 * (h2 + 1)], psC2[hpar][:, :]),
                                    wait=p.at("tS", hm), inc=[("vS", 1)])
                                ev_marks[2 * m + h2] = S["vS"]
                                p.emit("scalar", lambda e, m=m, h2=h2: e.square(
                                    sqf[:, 1024 * h2:1024 * (h2 + 1)],
                                    dst_t(m)[:, 1024 * h2:1024 * (h2 + 1)]),
                                    wait=p.w("vS"), inc=[("cS", 1)])
                                ev_marks[(2 * m + h2, "c")] = S["cS"]
                                for c in range(2):
                                    p.emit("tensor", lambda e, c=c, h2=h2, m=m,
                                           n_rms=n_rms: e.matmul(
                                        psS[:, 1024 * h2 + 512 * c:1024 * h2 + 512 * (c + 1)],
                                        ones_c[:, :],
                                        sqf[:, 1024 * h2 + 512 * c:1024 * h2 + 512 * (c + 1)],
                                        start=(m == 0), stop=(m == n_rms - 1)),
                                        wait=(p.w("cS") if c == 0 else ()),
                                        inc=([("tS", 1)] if c == 1 else ()))
                            else:
                                p.emit("vector", lambda e, h2=h2, hpar=hpar: e.tensor_tensor(
                                    t1f[:, 1024 * h2:1024 * (h2 + 1)], psC2[hpar][:, :],
                                    cs_sb[:, 1024 * h2:1024 * (h2 + 1)], ALU.mult),
                                    wait=p.at("tS", hm) + p.w("dG"), inc=[("vS", 1)])
                                ev_marks[2 * m + h2] = S["vS"]
                        pmm_marks.append(S["tS"])
                        if m >= n_rms:
                            p.emit("gpsimd", lambda e: e.dma_start(
                                out=t64[:, :], in_=t1f[64:128, :]),
                                wait=p.w("vS"), inc=[("dG", 16)])
                            p.emit("vector", lambda e: e.tensor_tensor(
                                kpe[:, :], t1f[0:64, :], t64[:, :], ALU.add),
                                wait=p.w("dG"), inc=[("vS", 1)])
                    p.emit("vector", lambda e: e.tensor_copy(srow[:, :], psS[:, :]),
                           wait=p.w("tS", "dS"))
                    p.emit("vector", lambda e, n_rms=n_rms: e.tensor_scalar(
                        srow[:, :], srow[:, :], 1.0 / (n_rms * 128), None, ALU.mult),
                        inc=[("vS", 1)])
                    p.emit("scalar", lambda e: e.activation(
                        srow[:, :], srow[:, :], AF.Sqrt, bias=cst[0:1, 1:2]),
                        wait=p.w("vS"), inc=[("cS", 1)])
                    p.emit("vector", lambda e: e.reciprocal(srow[:, :], srow[:, :]),
                           wait=p.w("cS"))
                    if scale != 1.0:
                        p.emit("vector", lambda e, scale=scale: e.tensor_scalar(
                            srow[:, :], srow[:, :], scale, None, ALU.mult))
                    p.emit("vector", lambda e: e.engine_nop(), inc=[("vS", 1)])
                    for h2 in range(2):
                        for c in range(2):
                            p.emit("tensor", lambda e, c=c, h2=h2: e.matmul(
                                psC2[h2][:, 512 * c:512 * (c + 1)], ones_r[:, :],
                                srow[:, 1024 * h2 + 512 * c:1024 * h2 + 512 * (c + 1)],
                                start=True, stop=True),
                                wait=(p.w("vS") if c == 0 else ()),
                                inc=([("tS", 1)] if c == 1 else ()))
                        p.emit("vector", lambda e, h2=h2: e.tensor_copy(
                            rstd_bc[:, 1024 * h2:1024 * (h2 + 1)], psC2[h2][:, :]),
                            wait=p.w("tS"), inc=[("vS", 1)])
                    for m in range(n_rms):
                        p.emit("vector", lambda e, m=m, ln_sb=ln_sb: e.scalar_tensor_tensor(
                            dst_t(m)[:, :], dst_t(m)[:, :], ln_sb[:, m:m + 1],
                            rstd_bc[:, :], ALU.mult, ALU.mult))
                    p.emit("vector", lambda e: e.engine_nop(), inc=[("vS", 1)])

                proj_rms(wqaT, NQ, NQ, cqn_t, lnq_sb, QSC)
                proj_rms(wkvaT, 5, NKV, ckv_t, lnkv_sb, 1.0)
                dbg_dump_fm(p, "cqn_dbg", cqn_t, NQ)
                dbg_dump_fm(p, "ckv_dbg", ckv_t, NKV)
                if "kpe_dbg" in dbg_out:
                    p.emit("gpsimd", lambda e: e.dma_start(
                        out=dbg_out["kpe_dbg"], in_=kpe[:, :]),
                        wait=p.w("vS"), inc=[("dG", 16)])
                p.run()


        if upto >= 6:
            p = Phase(nc, sems, S)
            with ExitStack() as c6:
                wqb_sb = [sbuf(c6, f"wqb{i}", [128, NQ * 256], BF16) for i in range(2)]
                wkvb_sb = [sbuf(c6, f"wkvb{i}", [128, NKV * 256], BF16) for i in range(2)]
                qn2 = [sbuf(c6, f"qn{i}", [128, T], BF16) for i in range(2)]
                qpe2 = [sbuf(c6, f"qpe{i}", [64, T], BF16) for i in range(2)]
                kn2 = [sbuf(c6, f"kn{i}", [128, T], BF16) for i in range(2)]
                vfm1 = sbuf(c6, "vfm1", [128, T], BF16)
                vfm2 = [vfm1, vfm1]
                vtm2 = [sbuf(c6, f"vtm{i}", [128, T], BF16) for i in range(2)]
                probs = [sbuf(c6, f"pr{i}", [128, T], BF16) for i in range(2)]
                probsT = sbuf(c6, "prT", [128, T], BF16)
                atm = sbuf(c6, "atm", [128, T], BF16)
                t1f = sbuf(c6, "p6t1", [128, 256], F32)
                t64 = sbuf(c6, "p6t64", [64, 256], F32)
                mask_sb = sbuf(c6, "mask_sb", [128, 128], BF16)
                cmax = sbuf(c6, "cmax", [128, 4], F32)
                nmax = sbuf(c6, "nmax", [128, 2], F32)
                sums = sbuf(c6, "sums", [128, 1], F32)
                recip = sbuf(c6, "recip", [128, 1], F32)
                psA = psum(c6, "p6a", [128, T])
                psT = psum(c6, "p6t", [128, 1024], BF16)
                psO = psum(c6, "p6o", [128, 128])
                psP = psum(c6, "p6p", [128, 1024])

                p.emit("gpsimd", lambda e: e.dma_start(out=mask_sb[:, :], in_=maskd),
                       inc=[("dG", 16)])
                mask_mark = S["dG"]

                psT_guard = [[("vS", 0)]]
                slice_mark = {}
                hend_mark = {-1: 0, -2: 0}

                def emit_slices(h):
                    hp = h % 2
                    for k in range(NQ):
                        p.emit("gpsimd", lambda e, k=k, h=h, hp=hp: e.dma_start(
                            out=wqb_sb[hp][:, 256 * k:256 * (k + 1)],
                            in_=wqbT[128 * k:128 * (k + 1), 256 * h:256 * (h + 1)]),
                            wait=(p.at("tS", hend_mark[h - 2]) if k == 0 and h >= 2 else ()),
                            inc=[(f"dW{hp}", 16)])
                    for k in range(NKV):
                        p.emit("gpsimd", lambda e, k=k, h=h, hp=hp: e.dma_start(
                            out=wkvb_sb[hp][:, 256 * k:256 * (k + 1)],
                            in_=wkvbT[128 * k:128 * (k + 1), 256 * h:256 * (h + 1)]),
                            inc=[(f"dW{hp}", 16)])
                    slice_mark[h] = S[f"dW{hp}"]

                def proj_units(h):
                    hp = h % 2
                    units = []
                    half_ev = [[("cS", 0)], [("cS", 0)]]
                    uidx = [0]

                    def mm_unit(lhs_off, nkk, rhs_t, dst, tq8, post=None):
                        def go():
                            hh = uidx[0] % 2
                            uidx[0] += 1
                            pslice = psP[:, 512 * hh:512 * hh + 256]
                            wv = p.at(f"dW{h % 2}", slice_mark[h]) + \
                                p.at("tS", hend_mark[h - 2]) + half_ev[hh]
                            for k in range(nkk):
                                p.emit("tensor", lambda e, k=k, lhs_off=lhs_off, nkk=nkk,
                                       rhs_t=rhs_t, tq8=tq8, hp=hp, pslice=pslice: e.matmul(
                                    pslice, wqb_sb[hp][:, 256 * k + lhs_off:256 * k + lhs_off + 128]
                                    if nkk == NQ else
                                    wkvb_sb[hp][:, 256 * k + lhs_off:256 * k + lhs_off + 128],
                                    rhs_t(k)[:, 256 * tq8:256 * (tq8 + 1)],
                                    start=(k == 0), stop=(k == nkk - 1)),
                                    wait=(wv if k == 0 else ()),
                                    inc=([("tS", 1)] if k == nkk - 1 else ()))
                            um = S["tS"]
                            if post is None:
                                p.emit("scalar", lambda e, dst=dst, tq8=tq8, pslice=pslice: e.copy(
                                    dst[:, 256 * tq8:256 * (tq8 + 1)], pslice),
                                    wait=p.at("tS", um), inc=[("cS", 1)])
                                half_ev[hh] = p.at("cS", S["cS"])
                            else:
                                post(pslice)
                                half_ev[hh] = p.at("vS", S["vS"])
                        return go

                    for tq8 in range(8):
                        units.append(mm_unit(0, NQ, cqn_t, qn2[hp], tq8))
                    for tq8 in range(8):
                        def rope_post(pslice, tq8=tq8, hp=hp):
                            p.emit("vector", lambda e, tq8=tq8, pslice=pslice: e.tensor_tensor(
                                t1f[:, :], pslice,
                                cs_sb[:, 256 * tq8:256 * (tq8 + 1)], ALU.mult),
                                wait=p.w("tS", "dG"), inc=[("vS", 1)])
                            p.emit("gpsimd", lambda e: e.dma_start(
                                out=t64[:, :], in_=t1f[64:128, :]),
                                wait=p.w("vS"), inc=[("dG", 16)])
                            p.emit("vector", lambda e, tq8=tq8, hp=hp: e.tensor_tensor(
                                qpe2[hp][:, 256 * tq8:256 * (tq8 + 1)],
                                t1f[0:64, :],
                                t64[0:64, :], ALU.add),
                                wait=p.w("dG"), inc=[("vS", 1)])
                        units.append(mm_unit(128, NQ, cqn_t, None, tq8, post=rope_post))
                    for tq8 in range(8):
                        units.append(mm_unit(0, NKV, ckv_t, kn2[hp], tq8))
                    for tq8 in range(8):
                        units.append(mm_unit(128, NKV, ckv_t, vfm2[hp], tq8))

                    def vtmT(batch, hp=hp):
                        def go():
                            for kt in range(8 * batch, 8 * batch + 8):
                                p.emit("tensor", lambda e, kt=kt, batch=batch, hp=hp: e.transpose(
                                    psT[:, 128 * (kt - 8 * batch):128 * (kt - 8 * batch + 1)],
                                    vfm2[hp][:, 128 * kt:128 * (kt + 1)], ident_b[:, :]),
                                    wait=(psT_guard[0] + p.w("vS", "cS") if kt == 8 * batch else ()),
                                    inc=([("tS", 1)] if kt == 8 * batch + 7 else ()))
                            p.emit("vector", lambda e, batch=batch, hp=hp: e.tensor_copy(
                                vtm2[hp][:, 1024 * batch:1024 * (batch + 1)],
                                psT[:, 0:1024]),
                                wait=p.w("tS"), inc=[("vS", 1)])
                            psT_guard[0] = p.at("vS", S["vS"])
                        return go
                    units.append(vtmT(0))
                    units.append(vtmT(1))
                    return units

                emit_slices(0)
                for u in proj_units(0):
                    u()
                emit_slices(1)

                for h in range(H):
                    hp = h % 2
                    pend = list(proj_units(h + 1)) if h + 1 < H else []
                    if h + 2 < H:
                        pass
                    exp_mark, ptc_mark, av_mark, tT_mark, nrm_mark = {}, {}, {}, {}, {}
                    sc_marks = {}

                    def emit_scores(j, h=h, hp=hp):
                        wv = p.w("vS", "cS")
                        if j > 0:
                            wv = p.at("cS", exp_mark[j - 1])
                        for kc in range(ncj(j)):
                            n = min(512, 128 * (j + 1) - 512 * kc)
                            p.emit("tensor", lambda e, j=j, kc=kc, n=n, hp=hp: e.matmul(
                                psA[:, 512 * kc:512 * kc + n],
                                qn2[hp][:, 128 * j:128 * (j + 1)],
                                kn2[hp][:, 512 * kc:512 * kc + n],
                                start=True, stop=False),
                                wait=(wv if kc == 0 else ()))
                            p.emit("tensor", lambda e, j=j, kc=kc, n=n, hp=hp: e.matmul(
                                psA[:, 512 * kc:512 * kc + n],
                                qpe2[hp][:, 128 * j:128 * (j + 1)],
                                kpe[:, 512 * kc:512 * kc + n],
                                start=False, stop=True),
                                inc=[("tS", 1)])
                            sc_marks[(j, kc)] = S["tS"]

                    def emit_vec(j):
                        for kc in range(ncj(j)):
                            n = min(512, 128 * (j + 1) - 512 * kc)
                            if kc == ncj(j) - 1:
                                p.emit("vector", lambda e, j=j: e.tensor_tensor(
                                    psA[:, 128 * j:128 * (j + 1)],
                                    psA[:, 128 * j:128 * (j + 1)], mask_sb[:, :], ALU.add),
                                    wait=p.at("tS", sc_marks[(j, kc)]) + p.at("dG", mask_mark))
                                p.emit("vector", lambda e, kc=kc, n=n: e.reduce_max(
                                    cmax[:, kc:kc + 1], psA[:, 512 * kc:512 * kc + n], AX.X))
                            else:
                                p.emit("vector", lambda e, kc=kc, n=n: e.reduce_max(
                                    cmax[:, kc:kc + 1], psA[:, 512 * kc:512 * kc + n], AX.X),
                                    wait=p.at("tS", sc_marks[(j, kc)]))
                        p.emit("vector", lambda e, j=j: e.reduce_max(
                            nmax[:, j % 2:j % 2 + 1], cmax[:, 0:ncj(j)], AX.X, negate=True),
                            inc=[("vS", 1)])

                    def emit_T_ptc(j, h=h):
                        nb = (j + 8) // 8
                        for b in range(nb):
                            k0, k1 = 8 * b, min(8 * b + 8, j + 1)
                            for kt in range(k0, k1):
                                p.emit("tensor", lambda e, j=j, kt=kt, k0=k0: e.transpose(
                                    psT[:, 128 * (kt - k0):128 * (kt - k0 + 1)],
                                    probs[j % 2][:, 128 * kt:128 * (kt + 1)], ident_b[:, :]),
                                    wait=((p.at("cS", exp_mark[j]) if b == 0 else
                                           p.at("cS", ptc_mark[j])) + psT_guard[0]
                                          if kt == k0 else ()),
                                    inc=([("tS", 1)] if kt == k1 - 1 else ()))
                            tT_mark[j] = S["tS"]
                            p.emit("scalar", lambda e, k0=k0, k1=k1: e.copy(
                                probsT[:, 128 * k0:128 * k1], psT[:, 0:128 * (k1 - k0)]),
                                wait=p.at("tS", tT_mark[j]), inc=[("cS", 1)])
                            ptc_mark[j] = S["cS"]
                            psT_guard[0] = p.at("cS", S["cS"])

                    def emit_AV(j, hp=hp):
                        wv = p.at("cS", ptc_mark[j])
                        if j >= 1:
                            wv = wv + p.at("vS", nrm_mark[j - 1])
                        for kt in range(j + 1):
                            p.emit("tensor", lambda e, j=j, kt=kt, hp=hp: e.matmul(
                                psO[:, :], probsT[:, 128 * kt:128 * (kt + 1)],
                                vtm2[hp][:, 128 * kt:128 * (kt + 1)],
                                start=(kt == 0), stop=(kt == j)),
                                wait=(wv if kt == 0 else ()),
                                inc=([("tS", 1)] if kt == j else ()))
                        av_mark[j] = S["tS"]

                    def emit_recip_norm(j):
                        p.emit("vector", lambda e: e.reciprocal(recip[:, :], sums[:, :]),
                               wait=p.at("cS", exp_mark[j]), inc=[("vS", 1)])
                        p.emit("vector", lambda e, j=j: e.tensor_scalar(
                            atm[:, 128 * j:128 * (j + 1)], psO[:, :], recip[:, 0:1], None,
                            ALU.mult), wait=p.at("tS", av_mark[j]), inc=[("vS", 1)])
                        nrm_mark[j] = S["vS"]

                    def emit_exp(j):
                        kj = 128 * (j + 1)
                        wv = p.w("vS")
                        if j >= 2:
                            wv = wv + p.at("tS", tT_mark[j - 2])
                        p.emit("scalar", lambda e, j=j, kj=kj: e.activation(
                            probs[j % 2][:, 0:kj], psA[:, 0:kj], AF.Exp,
                            bias=nmax[:, j % 2:j % 2 + 1], accum_out=sums[:, 0:1]),
                            wait=wv, inc=[("cS", 1)])
                        exp_mark[j] = S["cS"]

                    ui = 0
                    for j in range(NT):
                        emit_scores(j)
                        emit_vec(j)
                        if j > 0:
                            emit_T_ptc(j - 1)
                            emit_AV(j - 1)
                            emit_recip_norm(j - 1)
                        want = ((j + 1) * len(pend)) // NT
                        while ui < want:
                            pend[ui]()
                            ui += 1
                        emit_exp(j)
                    emit_T_ptc(NT - 1)
                    emit_AV(NT - 1)
                    emit_recip_norm(NT - 1)
                    while ui < len(pend):
                        pend[ui]()
                        ui += 1

                    nm15 = nrm_mark[NT - 1]
                    for b in range(2):
                        for j in range(8 * b, 8 * b + 8):
                            p.emit("tensor", lambda e, j=j, b=b: e.transpose(
                                psT[:, 128 * (j - 8 * b):128 * (j - 8 * b + 1)],
                                atm[:, 128 * j:128 * (j + 1)], ident_b[:, :]),
                                wait=((p.at("vS", nm15) if b == 0 else p.w("vS")) +
                                      psT_guard[0] if j == 8 * b else ()),
                                inc=([("tS", 1)] if j == 8 * b + 7 else ()))
                        p.emit("vector", lambda e, h=h, b=b: e.tensor_copy(
                            arena_t(h)[:, 1024 * b:1024 * (b + 1)], psT[:, 0:1024]),
                            wait=p.w("tS"), inc=[("vS", 1)])
                        psT_guard[0] = p.at("vS", S["vS"])
                    hend_mark[h] = S["tS"]
                    if h + 2 < H:
                        emit_slices(h + 2)
                dbg_dump_fm(p, "cat_dbg", arena_t, ND)
                p.run()

        b_scope.close()

        if upto >= 7:
            p = Phase(nc, sems, S)
            with ExitStack() as c7:
                wo_sb = [sbuf(c7, f"wo{i}", [128, ND * 128], BF16) for i in range(2)]
                dtmp = [sbuf(c7, f"p7dt{i}", [128, T], F32) for i in range(2)]
                psW = [psum(c7, f"p7w{i}", [128, T]) for i in range(2)]
                acc_done = {}
                ev_mark = {}
                mm_marks = {}
                band_mark = {}

                def emit_band7(u):
                    par = u % 2
                    p.emit("gpsimd", lambda e, u=u, par=par: e.dma_start(
                        out=wo_sb[par][:, :].rearrange("p (k c) -> p k c", c=128),
                        in_=woT[:, 128 * u:128 * (u + 1)].rearrange(
                            "(k p) c -> p k c", p=128)),
                        wait=(p.at("tS", mm_marks[u - 2]) if u >= 2 else ()),
                        inc=[(f"dW{par}", 16)])
                    return S[f"dW{par}"]

                band_mark[0] = emit_band7(0)
                band_mark[1] = emit_band7(1)
                for mo in range(ND):
                    par = mo % 2
                    wv = p.at(f"dW{par}", band_mark[mo]) + (
                        p.at("vS", ev_mark[mo - 2]) if mo >= 2 else [])
                    for k in range(ND):
                        for c in range(4):
                            p.emit("tensor", lambda e, k=k, c=c, par=par: e.matmul(
                                psW[par][:, 512 * c:512 * (c + 1)],
                                wo_sb[par][:, 128 * k:128 * (k + 1)],
                                arena_t(k)[:, 512 * c:512 * (c + 1)],
                                start=(k == 0), stop=(k == ND - 1)),
                                wait=(wv if (k == 0 and c == 0) else ()),
                                inc=([("tS", 1)] if (k == ND - 1 and c == 3) else ()))
                    mm_marks[mo] = S["tS"]
                    if mo + 2 < ND:
                        band_mark[mo + 2] = emit_band7(mo + 2)
                    p.emit("vector", lambda e, par=par: e.tensor_copy(
                        dtmp[par][:, :], psW[par][:, :]),
                        wait=p.w("tS") + (p.at(f"dA{par}", acc_done[mo - 2]) if mo >= 2 else []),
                        inc=[("vS", 1)])
                    ev_mark[mo] = S["vS"]
                    p.emit("gpsimd", lambda e, mo=mo, par=par: e.dma_start(
                        out=xh[128 * mo:128 * (mo + 1), :], in_=dtmp[par][:, :],
                        accum_op=ALU.add),
                        wait=p.w("vS"), inc=[(f"dA{par}", 16)])
                    acc_done[mo] = S[f"dA{par}"]
                if "xh_dbg" in dbg_out and upto == 7:
                    p.emit("sync", lambda e: e.dma_start(out=dbg_out["xh_dbg"], in_=xh),
                           wait=p.w("dG", "dA0", "dA1"), inc=[("dS", 16)])
                p.run()

        if upto >= 8:
            p = Phase(nc, sems, S)
            with ExitStack() as c8:
                xt = [sbuf(c8, f"p8x{i}", [128, T], F32) for i in range(2)]
                sqf = sbuf(c8, "p8sq", [128, T], F32)
                r1 = sbuf(c8, "p8r1", [1, T], F32)
                r2 = sbuf(c8, "p8r2", [1, T], F32)
                nrbc = sbuf(c8, "p8nr", [128, T], F32)
                nmbc = sbuf(c8, "p8nm", [128, T], F32)
                tmpf = sbuf(c8, "p8tmp", [128, T], F32)
                with ExitStack() as c8s:
                    psS1 = psum(c8s, "p8s1", [1, T])
                    psS2 = psum(c8s, "p8s2", [1, T])
                    for i in range(NT):
                        par = i % 2
                        p.emit("sync", lambda e, i=i, par=par: e.dma_start(
                            out=xt[par][:, :], in_=xh[128 * i:128 * (i + 1), :]),
                            wait=(p.at("tS", S["tS"] - 1) if i >= 2 else ()),
                            inc=[(f"dH{par}", 16)])
                        ld_mark = S[f"dH{par}"]
                        p.emit("scalar", lambda e, par=par: e.square(
                            sqf[:, :], xt[par][:, :]),
                            wait=p.at(f"dH{par}", ld_mark) + (p.w("tS") if i > 0 else []),
                            inc=[("cS", 1)])
                        for c in range(4):
                            p.emit("tensor", lambda e, c=c, i=i, par=par: e.matmul(
                                psS1[:, 512 * c:512 * (c + 1)], ones_c[:, :],
                                xt[par][:, 512 * c:512 * (c + 1)],
                                start=(i == 0), stop=(i == NT - 1)),
                                wait=(p.at(f"dH{par}", ld_mark) if c == 0 else ()),
                                inc=([("tS", 1)] if c == 3 else ()))
                        for c in range(4):
                            p.emit("tensor", lambda e, c=c, i=i: e.matmul(
                                psS2[:, 512 * c:512 * (c + 1)], ones_c[:, :],
                                sqf[:, 512 * c:512 * (c + 1)],
                                start=(i == 0), stop=(i == NT - 1)),
                                wait=(p.w("cS") if c == 0 else ()),
                                inc=([("tS", 1)] if c == 3 else ()))
                    p.emit("vector", lambda e: e.tensor_copy(r1[:, :], psS1[:, :]),
                           wait=p.w("tS"))
                    p.emit("vector", lambda e: e.tensor_copy(r2[:, :], psS2[:, :]))
                    p.emit("vector", lambda e: e.tensor_scalar(
                        r1[:, :], r1[:, :], 1.0 / D, None, ALU.mult))
                    p.emit("vector", lambda e: e.tensor_scalar(
                        r2[:, :], r2[:, :], 1.0 / D, None, ALU.mult))
                    p.emit("vector", lambda e: e.tensor_tensor(
                        sqf[0:1, 0:T], r1[:, :], r1[:, :], ALU.mult))
                    p.emit("vector", lambda e: e.tensor_tensor(
                        r2[:, :], r2[:, :], sqf[0:1, 0:T], ALU.subtract), inc=[("vS", 1)])
                    p.emit("scalar", lambda e: e.activation(
                        r2[:, :], r2[:, :], AF.Sqrt, bias=cst[0:1, 0:1]),
                        wait=p.w("vS"), inc=[("cS", 1)])
                    p.emit("vector", lambda e: e.reciprocal(r2[:, :], r2[:, :]),
                           wait=p.w("cS"))
                    p.emit("vector", lambda e: e.tensor_tensor(
                        r1[:, :], r1[:, :], r2[:, :], ALU.mult), inc=[("vS", 1)])
                with ExitStack() as c8r:
                    psR = psum(c8r, "p8pr", [128, T])
                    for c in range(4):
                        p.emit("tensor", lambda e, c=c: e.matmul(
                            psR[:, 512 * c:512 * (c + 1)], ones_r[:, :],
                            r2[:, 512 * c:512 * (c + 1)], start=True, stop=True),
                            wait=(p.w("vS") if c == 0 else ()),
                            inc=([("tS", 1)] if c == 3 else ()))
                    p.emit("vector", lambda e: e.tensor_copy(nrbc[:, :], psR[:, :]),
                           wait=p.w("tS"), inc=[("vS", 1)])
                    for c in range(4):
                        p.emit("tensor", lambda e, c=c: e.matmul(
                            psR[:, 512 * c:512 * (c + 1)], ones_r[:, :],
                            r1[:, 512 * c:512 * (c + 1)], start=True, stop=True),
                            wait=(p.w("vS") if c == 0 else ()),
                            inc=([("tS", 1)] if c == 3 else ()))
                    p.emit("vector", lambda e: e.tensor_copy(nmbc[:, :], psR[:, :]),
                           wait=p.w("tS"), inc=[("vS", 1)])
                    vmarks = {}
                    for i in range(NT):
                        par = i % 2
                        p.emit("sync", lambda e, i=i, par=par: e.dma_start(
                            out=xt[par][:, :], in_=xh[128 * i:128 * (i + 1), :]),
                            wait=(p.at("vS", vmarks[i - 2]) if i >= 2 else p.w("tS", "cS")),
                            inc=[(f"dH{par}", 16)])
                        ld_mark = S[f"dH{par}"]
                        p.emit("vector", lambda e, par=par: e.tensor_tensor(
                            tmpf[:, :], xt[par][:, :], nrbc[:, :], ALU.mult),
                            wait=p.at(f"dH{par}", ld_mark), inc=[("vS", 1)])
                        p.emit("vector", lambda e, i=i: e.tensor_tensor(
                            arena_t(i)[:, :], tmpf[:, :], nmbc[:, :], ALU.subtract),
                            inc=[("vS", 1)])
                        vmarks[i] = S["vS"]
                dbg_dump_fm(p, "xn1_dbg", arena_t, ND)
                p.run()

        if upto >= 9:
            p = Phase(nc, sems, S)
            with ExitStack() as c9:
                wq_sb2 = [sbuf(c9, f"p9wq{i}", [128, ND * 1024], BF16) for i in range(2)]
                hh_sb = sbuf(c9, "p9hh", [128, D], F32)
                gw_sb = sbuf(c9, "p9gw", [128, D], F32)
                gb_sb = sbuf(c9, "p9gb", [128, D], F32)
                c32_sb = sbuf(c9, "p9c32", [128, NT * 32], F32)
                s32_sb = sbuf(c9, "p9s32", [128, NT * 32], F32)
                h_sb = [sbuf(c9, f"p9h{i}", [128, 1024], F32) for i in range(2)]
                h2_sb = [sbuf(c9, f"p9h2{i}", [128, 1024], F32) for i in range(2)]
                gn_f = [sbuf(c9, f"p9gn{i}", [128, 1024], F32) for i in range(2)]
                sq_f = sbuf(c9, "p9sqf", [128, 1024], F32)
                gst = sbuf(c9, "p9gst", [128, 64], F32)
                rt = sbuf(c9, "p9rt", [128, 64], F32)
                psG = [psum(c9, f"p9g{i}", [128, 1024]) for i in range(2)]

                for dst, src in [(hh_sb, hhb_in), (gw_sb, gnw_in), (gb_sb, gnb_in)]:
                    p.emit("sync", lambda e, d=dst, s=src: e.dma_start(out=d[:, :], in_=s),
                           inc=[("dS", 16)])
                for i in range(NT):
                    p.emit("sync", lambda e, i=i: e.dma_start(
                        out=c32_sb[:, 32 * i:32 * (i + 1)],
                        in_=c32_in[128 * i:128 * (i + 1), :]), inc=[("dS", 16)])
                    p.emit("sync", lambda e, i=i: e.dma_start(
                        out=s32_sb[:, 32 * i:32 * (i + 1)],
                        in_=s32_in[128 * i:128 * (i + 1), :]), inc=[("dS", 16)])
                gi = 0
                relu_mark = {}
                acc_done = {}
                og_dgm = {}
                for og in range(2):
                    p.emit("gpsimd", lambda e, og=og: e.dma_start(
                        out=wq_sb2[og][:, :].rearrange("p (k c) -> p k c", c=1024),
                        in_=WQm[:, 1024 * og:1024 * (og + 1)].rearrange(
                            "(k p) c -> p k c", p=128)),
                        inc=[(f"dW{og}", 16)])
                    og_dgm[og] = S[f"dW{og}"]
                for og in range(2):
                    wq_sb = wq_sb2[og]
                    dgm = og_dgm[og]
                    for t in range(NT):
                        gi += 1
                        par = gi % 2
                        wv = p.at(f"dW{og}", dgm) + p.w("dS")
                        if gi > 2:
                            wv = wv + p.at("vS", relu_mark[gi - 2])
                        for k in range(ND):
                            for c in range(2):
                                p.emit("tensor", lambda e, k=k, c=c, t=t, par=par, wq_sb=wq_sb: e.matmul(
                                    psG[par][:, 512 * c:512 * (c + 1)],
                                    arena_t(k)[:, 128 * t:128 * (t + 1)],
                                    wq_sb[:, 1024 * k + 512 * c:1024 * k + 512 * (c + 1)],
                                    start=(k == 0), stop=(k == ND - 1)),
                                    wait=(wv if (k == 0 and c == 0) else ()),
                                    inc=([("tS", 1)] if (k == ND - 1 and c == 1) else ()))
                        tm = S["tS"]
                        hv, h2v, gv = h_sb[par], h2_sb[par], gn_f[par]
                        if og == 0:
                            def rview(par=par, t=t, hv=hv):
                                pg = psG[par]
                                A = pg[:, 0:64].rearrange("p (a b) -> p a b", b=2)[:, :, 0:1]
                                B = pg[:, 0:64].rearrange("p (a b) -> p a b", b=2)[:, :, 1:2]
                                ct = c32_sb[:, 32 * t:32 * (t + 1)].rearrange("p (a o) -> p a o", o=1)
                                st_ = s32_sb[:, 32 * t:32 * (t + 1)].rearrange("p (a o) -> p a o", o=1)
                                hA = hv[:, 0:64].rearrange("p (a b) -> p a b", b=2)[:, :, 0:1]
                                hB = hv[:, 0:64].rearrange("p (a b) -> p a b", b=2)[:, :, 1:2]
                                r0 = rt[:, 0:32].rearrange("p (a o) -> p a o", o=1)
                                r1_ = rt[:, 32:64].rearrange("p (a o) -> p a o", o=1)
                                return A, B, ct, st_, hA, hB, r0, r1_
                            A, B, ct_, st_, hA, hB, r0, r1_ = rview()
                            p.emit("vector", lambda e, A=A, ct_=ct_, r0=r0: e.tensor_tensor(
                                r0, A, ct_, ALU.mult), wait=p.at("tS", tm))
                            p.emit("vector", lambda e, B=B, st_=st_, r1_=r1_: e.tensor_tensor(
                                r1_, B, st_, ALU.mult))
                            p.emit("vector", lambda e, hA=hA, r0=r0, r1_=r1_: e.tensor_tensor(
                                hA, r0, r1_, ALU.subtract))
                            p.emit("vector", lambda e, B=B, ct_=ct_, r0=r0: e.tensor_tensor(
                                r0, B, ct_, ALU.mult))
                            p.emit("vector", lambda e, A=A, st_=st_, r1_=r1_: e.tensor_tensor(
                                r1_, A, st_, ALU.mult))
                            p.emit("vector", lambda e, hB=hB, r0=r0, r1_=r1_: e.tensor_tensor(
                                hB, r0, r1_, ALU.add))
                            p.emit("vector", lambda e, hv=hv: e.tensor_relu(
                                hv[:, 0:64], hv[:, 0:64]))
                            p.emit("vector", lambda e, par=par, hv=hv: e.tensor_relu(
                                hv[:, 64:1024], psG[par][:, 64:1024]), inc=[("vS", 1)])
                        else:
                            p.emit("vector", lambda e, par=par, hv=hv: e.tensor_relu(
                                hv[:, :], psG[par][:, :]), wait=p.at("tS", tm),
                                inc=[("vS", 1)])
                        relu_mark[gi] = S["vS"]
                        p.emit("vector", lambda e, og=og, hv=hv: e.tensor_tensor(
                            hv[:, :], hv[:, :], hh_sb[:, 1024 * og:1024 * (og + 1)], ALU.add),
                            inc=[("vS", 1)])
                        p.emit("scalar", lambda e, hv=hv, h2v=h2v: e.activation(
                            h2v[:, :], hv[:, :], AF.Tanh), wait=p.w("vS"), inc=[("cS", 1)])
                        p.emit("scalar", lambda e, h2v=h2v: e.square(sq_f[:, :], h2v[:, :]),
                               inc=[("cS", 1)])
                        p.emit("vector", lambda e, h2v=h2v: e.tensor_reduce(
                            gst[:, 0:16].rearrange("p (g o) -> p g o", o=1),
                            h2v[:, :].rearrange("p (g d) -> p g d", d=64), AX.X, ALU.add),
                            wait=p.w("cS"))
                        p.emit("vector", lambda e: e.tensor_reduce(
                            gst[:, 16:32].rearrange("p (g o) -> p g o", o=1),
                            sq_f[:, :].rearrange("p (g d) -> p g d", d=64), AX.X, ALU.add))
                        p.emit("vector", lambda e: e.tensor_scalar(
                            gst[:, 0:16], gst[:, 0:16], 1.0 / 64, None, ALU.mult))
                        p.emit("vector", lambda e: e.tensor_scalar(
                            gst[:, 16:32], gst[:, 16:32], 1.0 / 64, None, ALU.mult))
                        p.emit("vector", lambda e: e.tensor_tensor(
                            gst[:, 32:48], gst[:, 0:16], gst[:, 0:16], ALU.mult))
                        p.emit("vector", lambda e: e.tensor_tensor(
                            gst[:, 16:32], gst[:, 16:32], gst[:, 32:48], ALU.subtract),
                            inc=[("vS", 1)])
                        p.emit("scalar", lambda e: e.activation(
                            gst[:, 16:32], gst[:, 16:32], AF.Sqrt, bias=cst[:, 2:3]),
                            wait=p.w("vS"), inc=[("cS", 1)])
                        p.emit("vector", lambda e: e.reciprocal(gst[:, 16:32], gst[:, 16:32]),
                               wait=p.w("cS"))
                        for g in range(16):
                            p.emit("vector", lambda e, g=g, h2v=h2v, gv=gv: e.tensor_scalar(
                                gv[:, 64 * g:64 * (g + 1)], h2v[:, 64 * g:64 * (g + 1)],
                                gst[:, g:g + 1], gst[:, 16 + g:17 + g],
                                ALU.subtract, ALU.mult),
                                wait=(p.at(f"dA{par}", acc_done[gi - 2]) if (g == 0 and gi > 2)
                                      else ()))
                        p.emit("vector", lambda e, og=og, gv=gv: e.tensor_tensor(
                            gv[:, :], gv[:, :], gw_sb[:, 1024 * og:1024 * (og + 1)], ALU.mult))
                        p.emit("vector", lambda e, og=og, gv=gv: e.tensor_tensor(
                            gv[:, :], gv[:, :], gb_sb[:, 1024 * og:1024 * (og + 1)], ALU.add),
                            inc=[("vS", 1)])
                        vmark = S["vS"]
                        p.emit("gpsimd", lambda e, t=t, og=og, gv=gv: e.dma_start(
                            out=xh[128 * t:128 * (t + 1), 1024 * og:1024 * (og + 1)],
                            in_=gv[:, :], accum_op=ALU.add),
                            wait=p.at("vS", vmark), inc=[(f"dA{par}", 16)])
                        acc_done[gi] = S[f"dA{par}"]
                p.run()

        if upto >= 10:
            p = Phase(nc, sems, S)
            with ExitStack() as ca:
                wf_sb = sbuf(ca, "pawf", [128, ND * 512], F32)
                wq1 = [sbuf(ca, f"pawq1{i}", [128, ND * 512], BF16) for i in range(2)]
                wq2 = [sbuf(ca, f"pawq2{i}", [128, ND * 512], BF16) for i in range(2)]
                sgt = sbuf(ca, "pasg", [128, T], BF16)
                gt1 = sbuf(ca, "pagt1", [128, 512], BF16)
                hstr = [sbuf(ca, f"pah{i}", [128, T], BF16) for i in range(2)]
                psB1 = psum(ca, "pab1", [128, T])
                psB2 = psum(ca, "pab2", [128, T])
                mm_marks = {}
                q_marks = {}
                for og in range(11):
                    c0 = 512 * og
                    par = og % 2
                    lw = p.w("tS", "vS") if og == 0 else (
                        (p.at("tS", mm_marks[og - 2]) if og >= 2 else []) +
                        p.at("gq", q_marks[og - 1]))
                    p.emit("sync", lambda e, c0=c0: e.dma_start(
                        out=wf_sb[:, :].rearrange("p (k c) -> p k c", c=512),
                        in_=w1T[:, c0:c0 + 512].rearrange("(k p) c -> p k c", p=128)),
                        wait=lw, inc=[("dL", 16)])
                    dlm = S["dL"]
                    for k in range(ND):
                        p.emit("gpsimd", lambda e, k=k, par=par: e.tensor_scalar(
                            wq1[par][:, 512 * k:512 * (k + 1)], wf_sb[:, 512 * k:512 * (k + 1)],
                            thr[:, 0:1], 1.0, ALU.is_ge, ALU.subtract),
                            wait=(p.at("dL", dlm) + lw if k == 0 else ()))
                        p.emit("gpsimd", lambda e, k=k, par=par: e.tensor_scalar(
                            gt1[:, :], wf_sb[:, 512 * k:512 * (k + 1)],
                            thr[:, 1:2], None, ALU.is_gt))
                        p.emit("gpsimd", lambda e, k=k, par=par: e.tensor_tensor(
                            wq1[par][:, 512 * k:512 * (k + 1)],
                            wq1[par][:, 512 * k:512 * (k + 1)], gt1[:, :], ALU.add))
                    p.emit("gpsimd", lambda e: e.nop(), inc=[("gq", 1)])
                    q1m = S["gq"]
                    p.emit("sync", lambda e, c0=c0: e.dma_start(
                        out=wf_sb[:, :].rearrange("p (k c) -> p k c", c=512),
                        in_=w2T[:, c0:c0 + 512].rearrange("(k p) c -> p k c", p=128)),
                        wait=p.at("gq", q1m), inc=[("dL", 16)])
                    dlm = S["dL"]
                    for k in range(ND):
                        p.emit("gpsimd", lambda e, k=k, par=par: e.tensor_scalar(
                            wq2[par][:, 512 * k:512 * (k + 1)], wf_sb[:, 512 * k:512 * (k + 1)],
                            thr[:, 2:3], 1.0, ALU.is_ge, ALU.subtract),
                            wait=(p.at("dL", dlm) if k == 0 else ()))
                        p.emit("gpsimd", lambda e, k=k, par=par: e.tensor_scalar(
                            gt1[:, :], wf_sb[:, 512 * k:512 * (k + 1)],
                            thr[:, 3:4], None, ALU.is_gt))
                        p.emit("gpsimd", lambda e, k=k, par=par: e.tensor_tensor(
                            wq2[par][:, 512 * k:512 * (k + 1)],
                            wq2[par][:, 512 * k:512 * (k + 1)], gt1[:, :], ALU.add))
                    p.emit("gpsimd", lambda e: e.nop(), inc=[("gq", 1)])
                    q_marks[og] = S["gq"]
                    for m in range(4):
                        s_par = m % 2
                        wv = p.at("gq", q_marks[og])
                        if not (og == 0 and m == 0):
                            wv = wv + p.w("cS")
                        for k in range(ND):
                            for c in range(4):
                                p.emit("tensor", lambda e, k=k, c=c, m=m, par=par: e.matmul(
                                    psB1[:, 512 * c:512 * (c + 1)],
                                    wq1[par][:, 512 * k + 128 * m:512 * k + 128 * (m + 1)],
                                    arena_t(k)[:, 512 * c:512 * (c + 1)],
                                    start=(k == 0), stop=(k == ND - 1)),
                                    wait=(wv if (k == 0 and c == 0) else ()),
                                    inc=([("tS", 1)] if (k == ND - 1 and c == 3) else ()))
                        b1m = S["tS"]
                        wv2 = p.w("vS") if not (og == 0 and m == 0) else ()
                        for k in range(ND):
                            for c in range(4):
                                p.emit("tensor", lambda e, k=k, c=c, m=m, par=par: e.matmul(
                                    psB2[:, 512 * c:512 * (c + 1)],
                                    wq2[par][:, 512 * k + 128 * m:512 * k + 128 * (m + 1)],
                                    arena_t(k)[:, 512 * c:512 * (c + 1)],
                                    start=(k == 0), stop=(k == ND - 1)),
                                    wait=(wv2 if (k == 0 and c == 0) else ()),
                                    inc=([("tS", 1)] if (k == ND - 1 and c == 3) else ()))
                        b2m = S["tS"]
                        p.emit("scalar", lambda e: e.activation(
                            sgt[:, :], psB1[:, :], AF.Sigmoid),
                            wait=p.at("tS", b1m) + p.w("vS"), inc=[("cS", 1)])
                        p.emit("vector", lambda e, s_par=s_par: e.tensor_tensor(
                            hstr[s_par][:, :], sgt[:, :], psB2[:, :], ALU.mult),
                            wait=p.at("tS", b2m) + p.w("cS") +
                            (p.w("dS") if not (og == 0 and m <= 1) else []),
                            inc=[("vS", 1)])
                        p.emit("sync", lambda e, c0=c0, m=m, s_par=s_par: e.dma_start(
                            out=hid_hbm[c0 + 128 * m:c0 + 128 * (m + 1), :],
                            in_=hstr[s_par][:, :]),
                            wait=p.w("vS"), inc=[("dS", 16)])
                    mm_marks[og] = S["tS"]
                if "hid_dbg" in dbg_out:
                    p.emit("sync", lambda e: e.dma_start(
                        out=dbg_out["hid_dbg"], in_=hid_hbm),
                        wait=p.w("dS"), inc=[("dS", 16)])
                p.run()

            p = Phase(nc, sems, S)
            with ExitStack() as cb:
                w3f = [sbuf(cb, f"pbw3f{i}", [128, ND * 128], F32) for i in range(2)]
                w3q = [sbuf(cb, f"pbw3q{i}", [128, ND * 128], BF16) for i in range(2)]
                dtmp = [sbuf(cb, f"pbdt{i}", [128, T], F32) for i in range(2)]
                xr = sbuf(cb, "pbxr", [128, T], F32)
                fin = sbuf(cb, "pbfin", [128, T], F32)
                otmp = sbuf(cb, "pbot", [128, T], F32)
                slabs = [(0, 16), (2048, 16), (4096, 12)]
                acc_done = {}
                ev_mark = {}
                mm_marks = {}
                fin_marks = {}
                tr_marks = {}
                ot_marks = {}
                gi = 0
                units = []
                for si, (r0, nk) in enumerate(slabs):
                    for mo in range(ND):
                        units.append((si, r0, nk, mo))
                q_marks = {}
                band_mark = {}

                def emit_band10(u):
                    si, r0, nk, mo = units[u]
                    par = (u + 1) % 2
                    p.emit("gpsimd", lambda e, r0=r0, nk=nk, mo=mo, par=par: e.dma_start(
                        out=w3f[par][:, 0:nk * 128].rearrange("p (k c) -> p k c", c=128),
                        in_=w3T[r0:r0 + nk * 128, 128 * mo:128 * (mo + 1)].rearrange(
                            "(k p) c -> p k c", p=128)),
                        wait=(p.at("vS", q_marks[u - 2]) if u >= 2 else ()),
                        inc=[(f"dW{par}", 16)])
                    return S[f"dW{par}"]

                def emit_quant10(u):
                    si, r0, nk, mo = units[u]
                    par = (u + 1) % 2
                    p.emit("vector", lambda e, nk=nk, par=par: e.tensor_scalar(
                        w3q[par][:, 0:nk * 128], w3f[par][:, 0:nk * 128],
                        thr[:, 4:5], 1.0, ALU.is_ge, ALU.subtract),
                        wait=p.at(f"dW{par}", band_mark[u]) +
                        (p.at("tS", mm_marks[u - 2]) if u >= 2 else []))
                    p.emit("vector", lambda e, nk=nk, par=par: e.scalar_tensor_tensor(
                        w3q[par][:, 0:nk * 128], w3f[par][:, 0:nk * 128],
                        thr[:, 5:6], w3q[par][:, 0:nk * 128], ALU.is_gt, ALU.add),
                        inc=[("vS", 1)])
                    q_marks[u] = S["vS"]

                cur_slab = -1
                band_mark[0] = emit_band10(0)
                band_mark[1] = emit_band10(1)
                emit_quant10(0)
                emit_quant10(1)
                with ExitStack() as cb1:
                    psD = [psum(cb1, f"pbd{i}", [128, T]) for i in range(2)]
                    for u, (si, r0, nk, mo) in enumerate(units):
                        last = (si == len(slabs) - 1)
                        gi += 1
                        par = gi % 2
                        if si != cur_slab:
                            cur_slab = si
                            p.emit("sync", None, wait=p.w("tS"))
                            for k in range(nk):
                                p.emit("sync", lambda e, r0=r0, k=k: e.dma_start(
                                    out=arena[:, T * k:T * (k + 1)],
                                    in_=hid_hbm[r0 + 128 * k:r0 + 128 * (k + 1), :]),
                                    inc=[("dL", 16)])
                            ds_mark = S["dL"]
                        wv = p.at("dL", ds_mark) + p.at("vS", q_marks[u])
                        if last:
                            wv = wv + p.at("vS", max(fin_marks.get(gi - 1, 0),
                                                     ev_mark.get(31, 0), ev_mark.get(32, 0)))
                        elif gi > 2:
                            wv = wv + p.at("vS", ev_mark.get(gi - 2, 0))
                        pd = psD[par if not last else 0]
                        for k in range(nk):
                            for c in range(4):
                                p.emit("tensor", lambda e, k=k, c=c, pd=pd, nk=nk, par=par: e.matmul(
                                    pd[:, 512 * c:512 * (c + 1)],
                                    w3q[par][:, 128 * k:128 * (k + 1)],
                                    arena[:, T * k + 512 * c:T * k + 512 * (c + 1)],
                                    start=(k == 0), stop=(k == nk - 1)),
                                    wait=(wv if (k == 0 and c == 0) else ()),
                                    inc=([("tS", 1)] if (k == nk - 1 and c == 3) else ()))
                        mm_mark = S["tS"]
                        mm_marks[u] = mm_mark
                        if u + 2 < len(units):
                            band_mark[u + 2] = emit_band10(u + 2)
                            emit_quant10(u + 2)
                        if not last:
                            p.emit("vector", lambda e, par=par, pd=pd: e.tensor_copy(
                                dtmp[par][:, :], pd[:, :]),
                                wait=p.at("tS", mm_mark) +
                                (p.at(f"dA{par}", acc_done[gi - 2]) if gi > 2 else []),
                                inc=[("vS", 1)])
                            ev_mark[gi] = S["vS"]
                            p.emit("gpsimd", lambda e, mo=mo, par=par: e.dma_start(
                                out=xh[128 * mo:128 * (mo + 1), :], in_=dtmp[par][:, :],
                                accum_op=ALU.add),
                                wait=p.w("vS"), inc=[(f"dA{par}", 16)])
                            acc_done[gi] = S[f"dA{par}"]
                        else:
                            p.emit("sync", lambda e, mo=mo: e.dma_start(
                                out=xr[:, :], in_=xh[128 * mo:128 * (mo + 1), :]),
                                wait=p.at(f"dA{(mo + 17) % 2}", acc_done[16 + mo + 1]) + p.w("vS"),
                                inc=[("dS", 16)])
                            p.emit("vector", lambda e, pd0=psD[0]: e.tensor_tensor(
                                fin[:, :], pd0[:, :], xr[:, :], ALU.add),
                                wait=p.at("tS", mm_mark) + p.w("dS") +
                                (p.at("tS", tr_marks.get(gi - 1, 0))), inc=[("vS", 1)])
                            fmark = S["vS"]
                            fin_marks[gi] = fmark
                            for tb in range(NT):
                                p.emit("tensor", lambda e, tb=tb, pF=psD[1]: e.transpose(
                                    pF[:, 128 * tb:128 * (tb + 1)],
                                    fin[:, 128 * tb:128 * (tb + 1)], ident_f[:, :]),
                                    wait=(p.at("vS", fmark) +
                                          p.at("vS", ot_marks.get(gi - 1, 0))
                                          if tb == 0 else ()),
                                    inc=([("tS", 1)] if tb == NT - 1 else ()))
                            tr_marks[gi] = S["tS"]
                            p.emit("vector", lambda e, pF=psD[1]: e.tensor_copy(
                                otmp[:, :], pF[:, :]),
                                wait=p.at("tS", tr_marks[gi]) + (p.w("dS") if mo > 0 else []),
                                inc=[("vS", 1)])
                            ot_marks[gi] = S["vS"]
                            p.emit("sync", lambda e, mo=mo: e.dma_start(
                                out=o_out[:, 128 * mo:128 * (mo + 1)].rearrange(
                                    "(tb p) n -> p tb n", p=128),
                                in_=otmp[:, :].rearrange("p (tb n) -> p tb n", n=128)),
                                wait=p.w("vS"), inc=[("dS", 16)])
                p.run()

    return nc



def _host_prep(inputs):
    f32 = np.float32
    asf = lambda a: np.ascontiguousarray(np.asarray(a), dtype=f32)

    _ = inputs["L1_attn_S_n"]
    w_qb = np.asarray(inputs["L0_attn_w_qb"])
    perm_q = []
    for h in range(H):
        base = h * 192
        perm_q += list(range(base, base + 128))
        perm_q += list(range(base + 128, base + 192))
        perm_q += [base + 128 + ((j + 32) % 64) for j in range(64)]
    wqbT = asf(w_qb[perm_q].T)

    w_kva = np.asarray(inputs["L0_attn_w_kva"])
    perm_kv = list(range(512)) + list(range(512, 576)) + \
        [512 + ((j + 32) % 64) for j in range(64)]
    wkvaT = asf(w_kva[perm_kv].T)

    inv = 1.0 / (10000.0 ** (np.arange(0, 64, 2, dtype=np.float64) / 64))
    frq = np.outer(np.arange(T, dtype=np.float64), inv)
    emb = np.concatenate([frq, frq], axis=1)
    cosT = np.cos(emb).T
    sinT = np.sin(emb).T
    sgn = np.where(np.arange(64) < 32, -1.0, 1.0)[:, None]
    cs_qk = asf(np.concatenate([cosT, sgn * sinT], axis=0))

    f1 = np.arange(T, dtype=np.float64)[:, None] * inv[None, :]
    c32 = asf(np.cos(f1))
    s32 = asf(np.sin(f1))

    mask = np.where(np.arange(128)[None, :] <= np.arange(128)[:, None], 0.0, NEG)

    return {
        "wqaT": asf(np.asarray(inputs["L0_attn_w_qa"]).T),
        "lnq": asf(np.asarray(inputs["L0_attn_q_a_ln_w"]).reshape(NQ, 128).T),
        "wqbT": wqbT,
        "wkvaT": wkvaT,
        "lnkv": asf(np.asarray(inputs["L0_attn_kv_a_ln_w"]).reshape(NKV, 128).T),
        "wkvbT": asf(np.asarray(inputs["L0_attn_w_kvb"]).T),
        "woT": asf(np.asarray(inputs["L0_attn_w_o"]).T),
        "rwT": asf(np.asarray(inputs["L0_ffn_router_weight"]).T),
        "wgT": asf(np.asarray(inputs["L0_ffn_sh_gate"]).T),
        "wuT": asf(np.asarray(inputs["L0_ffn_sh_up"]).T),
        "wrgT": asf(np.asarray(inputs["L0_ffn_r_gate"]).T),
        "wruT": asf(np.asarray(inputs["L0_ffn_r_up"]).T),
        "wdsT": asf(np.concatenate([np.asarray(inputs["L0_ffn_sh_down"]).T,
                                    np.asarray(inputs["L0_ffn_r_down"]).T], axis=0)),
        "WQm": asf(np.asarray(inputs["L1_attn_W_Q"])),
        "hhb": asf(np.broadcast_to(np.asarray(inputs["L1_attn_hh_b"]), (128, D))),
        "gnw": asf(np.broadcast_to(np.asarray(inputs["L1_attn_gn_weight"]), (128, D))),
        "gnb": asf(np.broadcast_to(np.asarray(inputs["L1_attn_gn_bias"]), (128, D))),
        "w1T": asf(np.asarray(inputs["L1_ffn_w1"]).T),
        "w2T": asf(np.asarray(inputs["L1_ffn_w2"]).T),
        "w3T": asf(np.asarray(inputs["L1_ffn_w3"]).T),
        "identf": np.eye(128, dtype=f32),
        "identb": np.eye(128).astype(ml_dtypes.bfloat16),
        "onescol": np.ones((128, 1), dtype=f32),
        "onesrow": np.ones((1, 128), dtype=f32),
        "maskd": asf(mask),
        "cs_qk": cs_qk,
        "c32": c32,
        "s32": s32,
    }


def kernel(**inputs):
    common = _host_prep(inputs)
    x = np.asarray(inputs["L0_attn_hidden_states"], dtype=np.float32)
    nc = build_program()
    in_maps = []
    for i in range(8):
        m = dict(common)
        m["x"] = np.ascontiguousarray(x[i])
        in_maps.append(m)
    res = run_bass_kernel_spmd(nc, in_maps, core_ids=list(range(8)))
    return np.stack([res.results[i]["o"] for i in range(8)], axis=0)


if __name__ == "__main__":
    import time
    t0 = time.time()
    nc = build_program()
    print(f"built ok in {time.time() - t0:.1f}s")


# version 08
# speedup vs baseline: 1.4425x; 1.4425x over previous
import numpy as np
import ml_dtypes
import concourse.bass as bass
import concourse.mybir as mybir
from contextlib import ExitStack
from concourse.bass_utils import run_bass_kernel_spmd

F32 = mybir.dt.float32
BF16 = mybir.dt.bfloat16
AF = mybir.ActivationFunctionType
ALU = mybir.AluOpType
AX = mybir.AxisListType

D = 2048
T = 2048
NT = T // 128
ND = D // 128
QLR, NQ = 1536, 12
KVLR, NKV = 512, 4
H = 16
DQ = 192
INTER = 8192
MOE = 1024
SDIM = INTER + MOE
HID1 = 5632
EPS_RMS = 1e-6
EPS_LN = 1e-5
EPS_GN = 1e-5
EPS_BIT = 1e-8
NEG = -1e30
QSC = float(DQ) ** -0.5

SEMS = ("dS", "dG", "tS", "vS", "cS", "vq", "cq", "gq", "dA0", "dA1", "dW0", "dW1", "dL", "dH0", "dH1", "dP0", "dP1")


class Phase:

    def __init__(self, nc, sems, S):
        self.nc, self.sems, self.S = nc, sems, S
        self.q = {k: [] for k in ("sync", "gpsimd", "tensor", "vector", "scalar")}

    CHAIN = {"vector": "vS", "scalar": "cS", "gpsimd": "gq"}

    def emit(self, eng, fn, wait=(), inc=()):
        wait = list(wait)
        inc = list(inc)
        ch = self.CHAIN.get(eng)
        is_dma = any(c.startswith(("dS", "dG", "dA", "dW")) and k >= 16 for c, k in inc)
        if ch is not None and fn is not None and not is_dma:
            inc = [(c, k) for c, k in inc if c != ch]
            wait.append((ch, self.S[ch]))
            inc.append((ch, 1))
            assert len(inc) == 1, f"multi-inc on {eng}: {inc}"
        for c, k in inc:
            self.S[c] += k
        self.q[eng].append((wait, fn, inc))

    def w(self, *ctrs):
        return [(c, self.S[c]) for c in ctrs]

    def at(self, ctr, val):
        return [(ctr, val)]

    def run(self):
        self.emit("sync", None, wait=self.w("dS", "dL", "dH0", "dH1", "dP0", "dP1"))
        self.emit("gpsimd", None, wait=self.w("dG", "dA0", "dA1", "dW0", "dW1"))
        nc = self.nc
        with nc.Block() as block:
            def mk(name):
                ops = self.q[name]

                def body(eng):
                    for waits, fn, incs in ops:
                        for c, v in waits:
                            if v > 0:
                                eng.wait_ge(self.sems[c], v)
                        if fn is None:
                            continue
                        ins = fn(eng)
                        for c, k in incs:
                            ins.then_inc(self.sems[c], k)
                return body

            block.sync(mk("sync"))
            block.gpsimd(mk("gpsimd"))
            block.tensor(mk("tensor"))
            block.vector(mk("vector"))
            block.scalar(mk("scalar"))
        nc.all_engine_barrier()


def ncj(j):
    return (128 * (j + 1) + 511) // 512


def build_program(upto=99, dbg=()):
    nc = bass.Bass("TRN2", target_bir_lowering=False)
    dbg = set(dbg)

    def din(name, shape, dt=F32):
        return nc.dram_tensor(name, shape, dt, kind="ExternalInput").ap()

    x_in = din("x", [T, D])
    wqaT = din("wqaT", [D, QLR])
    lnq_in = din("lnq", [128, NQ])
    wqbT = din("wqbT", [QLR, H * 256])
    wkvaT = din("wkvaT", [D, 640])
    lnkv_in = din("lnkv", [128, NKV])
    wkvbT = din("wkvbT", [KVLR, H * 256])
    woT = din("woT", [D, D])
    rwT = din("rwT", [D, 8])
    wgT = din("wgT", [D, INTER])
    wuT = din("wuT", [D, INTER])
    wrgT = din("wrgT", [D, MOE])
    wruT = din("wruT", [D, MOE])
    wdsT = din("wdsT", [SDIM, D])
    WQm = din("WQm", [D, D])
    hhb_in = din("hhb", [128, D])
    gnw_in = din("gnw", [128, D])
    gnb_in = din("gnb", [128, D])
    w1T = din("w1T", [D, HID1])
    w2T = din("w2T", [D, HID1])
    w3T = din("w3T", [HID1, D])
    identf = din("identf", [128, 128])
    onescol = din("onescol", [128, 1])
    onesrow = din("onesrow", [1, 128])
    maskd = din("maskd", [128, 128])
    cs_qk_in = din("cs_qk", [128, T])
    c32_in = din("c32", [T, 32])
    s32_in = din("s32", [T, 32])
    identb_in = din("identb", [128, 128], BF16)

    o_out = nc.dram_tensor("o", [T, D], F32, kind="ExternalOutput").ap()
    dbg_out = {}
    for name, shape in [
        ("xh_dbg", [D, T]), ("xn_dbg", [D, T]), ("cqn_dbg", [QLR, T]),
        ("ckv_dbg", [KVLR, T]), ("kpe_dbg", [64, T]), ("cat_dbg", [D, T]),
        ("comb_dbg", [128, NT]), ("thr_dbg", [128, 6]), ("xn1_dbg", [D, T]),
    ]:
        if name in dbg:
            dbg_out[name] = nc.dram_tensor(name, shape, F32, kind="ExternalOutput").ap()
    for name, shape in [("s_dbg", [SDIM, T]), ("hid_dbg", [HID1, T])]:
        if name in dbg:
            dbg_out[name] = nc.dram_tensor(name, shape, BF16, kind="ExternalOutput").ap()

    xh = nc.dram_tensor("xh", [D, T], F32, kind="Internal").ap()
    s_hbm = nc.dram_tensor("s_hbm", [SDIM, T], BF16, kind="Internal").ap()
    hid_hbm = nc.dram_tensor("hid_hbm", [HID1, T], BF16, kind="Internal").ap()
    bounce = nc.dram_tensor("bounce", [16, 128], F32, kind="Internal").ap()

    S = {k: 0 for k in SEMS}

    with ExitStack() as top:
        sems = {k: top.enter_context(nc.semaphore(k)) for k in SEMS}

        def sbuf(stack, name, shape, dt):
            return stack.enter_context(nc.sbuf_tensor(name, shape, dt))

        def psum(stack, name, shape, dt=F32):
            return stack.enter_context(nc.psum_tensor(name, shape, dt))

        arena = sbuf(top, "arena", [128, NT * T], BF16)
        ident_f = sbuf(top, "ident_f", [128, 128], F32)
        ident_b = sbuf(top, "ident_b", [128, 128], BF16)
        ones_c = sbuf(top, "ones_c", [128, 1], F32)
        ones_r = sbuf(top, "ones_r", [1, 128], F32)
        thr = sbuf(top, "thr", [128, 6], F32)
        cst = sbuf(top, "cst", [128, 4], F32)

        def arena_t(i):
            return arena[:, T * i:T * (i + 1)]

        def arena3(i):
            return arena[:, :].rearrange("p (c t) -> p c t", t=T)[:, :, 128 * i:128 * (i + 1)]

        def dbg_dump_fm(ph, name, src_fn, ntiles):
            if name not in dbg_out:
                return
            dst = dbg_out[name]
            for i in range(ntiles):
                ph.emit("gpsimd", lambda e, i=i: e.dma_start(
                    out=dst[128 * i:128 * (i + 1), :], in_=src_fn(i)),
                    wait=ph.w("vS", "tS", "cS"), inc=[("dG", 16)])

        p = Phase(nc, sems, S)
        for dst, src in [(ident_f, identf), (ident_b, identb_in),
                         (ones_c, onescol), (ones_r, onesrow)]:
            p.emit("sync", lambda e, d=dst, s=src: e.dma_start(out=d[:, :], in_=s),
                   inc=[("dS", 16)])
        for col, val in [(0, EPS_LN), (1, EPS_RMS), (2, EPS_GN)]:
            p.emit("vector", lambda e, col=col, val=val: e.memset(cst[:, col:col + 1], val),
                   inc=[("vS", 1)])
        p.run()

        if upto >= 1:
            p = Phase(nc, sems, S)
            with ExitStack() as c1:
                xt = [sbuf(c1, f"p1x{i}", [128, D], F32) for i in range(2)]
                xsq = sbuf(c1, "p1sq", [128, D], F32)
                st = sbuf(c1, "p1st", [128, 8], F32)
                xn_tm = sbuf(c1, "p1xn", [128, D], BF16)
                xh_str = sbuf(c1, "p1xh", [128, D], F32)
                wb = [sbuf(c1, f"p0w{i}", [128, HID1], F32) for i in range(2)]
                trash = sbuf(c1, "p0tr", [128, HID1], F32)
                racc3 = sbuf(c1, "p0acc", [128, 80], F32)
                rtmp = sbuf(c1, "p0tmp", [128, 1], F32)
                sc1 = sbuf(c1, "p0s1", [1, 1], F32)
                ps_b = psum(c1, "p1psb", [128, D], BF16)
                ps_f = psum(c1, "p1psf", [128, D])
                ps1 = psum(c1, "p0ps", [1, 1])
                psr = psum(c1, "p0psr", [128, 1])
                strips = [(w1T, i, HID1, i) for i in range(16)]
                strips += [(w2T, i, HID1, 16 + i) for i in range(16)]
                strips += [(w3T, i, D, 32 + i) for i in range(44)]
                abs_marks = {}

                def emit_strip(s):
                    wt, i, cols, col = strips[s]
                    spar = s % 2
                    p.emit("sync", lambda e, wt=wt, i=i, cols=cols, spar=spar: e.dma_start(
                        out=wb[spar][:, 0:cols], in_=wt[128 * i:128 * (i + 1), :]),
                        wait=(p.at("cS", abs_marks[s - 2]) if s >= 2 else ()),
                        inc=[(f"dP{spar}", 16)])
                    dlm = S[f"dP{spar}"]
                    p.emit("scalar", lambda e, cols=cols, col=col, spar=spar: e.activation(
                        trash[:, 0:cols], wb[spar][:, 0:cols], AF.Abs,
                        accum_out=racc3[:, col:col + 1]),
                        wait=p.at(f"dP{spar}", dlm), inc=[("cS", 1)])
                    abs_marks[s] = S["cS"]

                marks = {}
                for i in range(NT):
                    for s in range(5 * i, min(5 * (i + 1), len(strips))):
                        emit_strip(s)
                    par = i % 2
                    xv = xt[par]
                    wv = (p.at("tS", marks[i - 2][0]) + p.at("vS", marks[i - 2][1])
                          if i >= 2 else ())
                    p.emit("sync", lambda e, i=i, xv=xv: e.dma_start(
                        out=xv[:, :], in_=x_in[128 * i:128 * (i + 1), :]),
                        wait=wv, inc=[(f"dH{par}", 16)])
                    ld_mark = S[f"dH{par}"]
                    p.emit("scalar", lambda e, xv=xv: e.square(xsq[:, :], xv[:, :]),
                           wait=p.at(f"dH{par}", ld_mark) + p.w("vS"), inc=[("cS", 1)])
                    p.emit("vector", lambda e, xv=xv: e.reduce_sum(
                        st[:, 0:1], xv[:, :], AX.X), wait=p.at(f"dH{par}", ld_mark))
                    p.emit("vector", lambda e: e.reduce_sum(st[:, 1:2], xsq[:, :], AX.X),
                           wait=p.w("cS"))
                    p.emit("vector", lambda e: e.tensor_scalar(
                        st[:, 0:1], st[:, 0:1], 1.0 / D, None, ALU.mult))
                    p.emit("vector", lambda e: e.tensor_tensor(
                        st[:, 2:3], st[:, 0:1], st[:, 0:1], ALU.mult))
                    p.emit("vector", lambda e: e.tensor_scalar(
                        st[:, 3:4], st[:, 1:2], 1.0 / D, None, ALU.mult))
                    p.emit("vector", lambda e: e.tensor_tensor(
                        st[:, 4:5], st[:, 3:4], st[:, 2:3], ALU.subtract), inc=[("vS", 1)])
                    p.emit("scalar", lambda e: e.activation(
                        st[:, 5:6], st[:, 4:5], AF.Sqrt, bias=cst[:, 0:1]),
                        wait=p.w("vS"), inc=[("cS", 1)])
                    p.emit("vector", lambda e: e.reciprocal(st[:, 6:7], st[:, 5:6]),
                           wait=p.w("cS"))
                    p.emit("vector", lambda e, xv=xv: e.tensor_scalar(
                        xn_tm[:, :], xv[:, :], st[:, 0:1], st[:, 6:7],
                        ALU.subtract, ALU.mult), inc=[("vS", 1)])
                    xn_mark = S["vS"]
                    for c in range(ND):
                        p.emit("tensor", lambda e, c=c: e.transpose(
                            ps_b[:, 128 * c:128 * (c + 1)], xn_tm[:, 128 * c:128 * (c + 1)],
                            ident_b[:, :]),
                            wait=(p.at("vS", xn_mark) if c == 0 else ()),
                            inc=([("tS", 1)] if c == ND - 1 else ()))
                    bT = S["tS"]
                    for c in range(ND):
                        p.emit("tensor", lambda e, c=c, xv=xv: e.transpose(
                            ps_f[:, 128 * c:128 * (c + 1)], xv[:, 128 * c:128 * (c + 1)],
                            ident_f[:, :]),
                            inc=([("tS", 1)] if c == ND - 1 else ()))
                    fT = S["tS"]
                    p.emit("vector", lambda e, i=i: e.tensor_copy(
                        arena3(i), ps_b[:, :].rearrange("p (c n) -> p c n", n=128)),
                        wait=p.at("tS", bT), inc=[("vS", 1)])
                    p.emit("vector", lambda e: e.tensor_copy(xh_str[:, :], ps_f[:, :]),
                           wait=p.at("tS", fT) + (p.w("dS") if i > 0 else []),
                           inc=[("vS", 1)])
                    fEv = S["vS"]
                    p.emit("sync", lambda e, i=i: e.dma_start(
                        out=xh[:, 128 * i:128 * (i + 1)].rearrange("(c p) n -> p c n", p=128),
                        in_=xh_str[:, :].rearrange("p (c n) -> p c n", n=128)),
                        wait=p.at("vS", fEv), inc=[("dS", 16)])
                    marks[i] = (fT, fEv)
                for s in range(5 * NT, len(strips)):
                    emit_strip(s)
                for mi, (a, b, nelem) in enumerate([(0, 16, D * HID1), (16, 32, D * HID1),
                                                    (32, 76, HID1 * D)]):
                    p.emit("vector", lambda e, a=a, b=b: e.tensor_reduce(
                        rtmp[:, :], racc3[:, a:b], AX.X, ALU.add),
                        wait=p.at("cS", abs_marks[len(strips) - 1]))
                    p.emit("vector", lambda e: e.engine_nop(), inc=[("vS", 1)])
                    p.emit("tensor", lambda e: e.matmul(
                        ps1[:, :], ones_c[:, :], rtmp[:, :], start=True, stop=True),
                        wait=p.w("vS"), inc=[("tS", 1)])
                    p.emit("vector", lambda e, nelem=nelem: e.tensor_scalar(
                        sc1[:, :], ps1[:, :], 0.5 / nelem, 0.5 * EPS_BIT, ALU.mult, ALU.add),
                        wait=p.w("tS"), inc=[("vS", 1)])
                    p.emit("tensor", lambda e: e.matmul(
                        psr[:, :], ones_r[:, :], sc1[:, :], start=True, stop=True),
                        wait=p.w("vS"), inc=[("tS", 1)])
                    p.emit("vector", lambda e, mi=mi: e.tensor_copy(
                        thr[:, 2 * mi:2 * mi + 1], psr[:, :]), wait=p.w("tS"))
                    p.emit("vector", lambda e, mi=mi: e.tensor_scalar(
                        thr[:, 2 * mi + 1:2 * mi + 2], psr[:, :], -1.0, None, ALU.mult),
                        inc=[("vS", 1)])
                if "thr_dbg" in dbg_out:
                    p.emit("sync", lambda e: e.dma_start(
                        out=dbg_out["thr_dbg"], in_=thr[:, :]),
                        wait=p.w("vS"), inc=[("dS", 16)])
                if "xh_dbg" in dbg_out and upto == 1:
                    p.emit("sync", lambda e: e.dma_start(out=dbg_out["xh_dbg"], in_=xh),
                           wait=p.w("dS"), inc=[("dS", 16)])
                dbg_dump_fm(p, "xn_dbg", arena_t, ND)
                p.run()

        if upto >= 2:
            p = Phase(nc, sems, S)
            with ExitStack() as c2:
                comb_bc = sbuf(c2, "comb_bc", [128, T], F32)
                with ExitStack() as cr:
                    rw_sb = sbuf(cr, "rw_sb", [128, NT * 8], BF16)
                    t8 = sbuf(cr, "p2t8", [128, 8], F32)
                    cb = sbuf(cr, "p2cb", [128, 8], F32)
                    combc = sbuf(cr, "p2cc", [128, NT], F32)
                    ct16 = sbuf(cr, "p2ct", [16, 128], F32)
                    crow = sbuf(cr, "p2crow", [1, T], F32)
                    ps_r = psum(cr, "p2psr", [128, 8])
                    ps_t = psum(cr, "p2pst", [16, 128])
                    ps_c = psum(cr, "p2psc", [128, T])
                    for k in range(ND):
                        p.emit("gpsimd", lambda e, k=k: e.dma_start(
                            out=rw_sb[:, 8 * k:8 * (k + 1)], in_=rwT[128 * k:128 * (k + 1), :]),
                            inc=[("dG", 16)])
                    for i in range(NT):
                        for k in range(ND):
                            p.emit("tensor", lambda e, i=i, k=k: e.matmul(
                                ps_r[:, :],
                                arena_t(k)[:, 128 * i:128 * (i + 1)],
                                rw_sb[:, 8 * k:8 * (k + 1)],
                                start=(k == 0), stop=(k == ND - 1)),
                                wait=(p.w("dG", "vS") if k == 0 else ()),
                                inc=([("tS", 1)] if k == ND - 1 else ()))
                        p.emit("vector", lambda e: e.reduce_max(cb[:, 0:1], ps_r[:, :], AX.X),
                               wait=p.w("tS"))
                        p.emit("vector", lambda e: e.tensor_scalar(
                            t8[:, :], ps_r[:, :], cb[:, 0:1], NEG, ALU.is_ge, ALU.mult))
                        p.emit("vector", lambda e: e.tensor_tensor(
                            t8[:, :], t8[:, :], ps_r[:, :], ALU.add))
                        p.emit("vector", lambda e: e.reduce_max(cb[:, 1:2], t8[:, :], AX.X))
                        p.emit("vector", lambda e: e.tensor_tensor(
                            cb[:, 2:3], cb[:, 1:2], cb[:, 0:1], ALU.subtract), inc=[("vS", 1)])
                        p.emit("scalar", lambda e: e.activation(
                            cb[:, 3:4], cb[:, 2:3], AF.Exp), wait=p.w("vS"), inc=[("cS", 1)])
                        p.emit("vector", lambda e: e.tensor_scalar(
                            cb[:, 4:5], cb[:, 3:4], 1.0, None, ALU.add), wait=p.w("cS"))
                        p.emit("vector", lambda e: e.reciprocal(cb[:, 5:6], cb[:, 4:5]))
                        p.emit("vector", lambda e, i=i: e.tensor_tensor(
                            combc[:, i:i + 1], cb[:, 4:5], cb[:, 5:6], ALU.mult),
                            inc=[("vS", 1)])
                    p.emit("tensor", lambda e: e.transpose(
                        ps_t[:, :], combc[:, :], ident_f[:, :]),
                        wait=p.w("vS"), inc=[("tS", 1)])
                    p.emit("vector", lambda e: e.tensor_copy(ct16[:, :], ps_t[:, :]),
                           wait=p.w("tS"), inc=[("vS", 1)])
                    p.emit("sync", lambda e: e.dma_start(out=bounce, in_=ct16[:, :]),
                           wait=p.w("vS"), inc=[("dS", 16)])
                    p.emit("sync", lambda e: e.dma_start(
                        out=crow[:, :], in_=bounce.rearrange("a b -> (a b)").rearrange("(o n) -> o n", o=1)),
                        wait=p.w("dS"), inc=[("dS", 16)])
                    for c in range(4):
                        p.emit("tensor", lambda e, c=c: e.matmul(
                            ps_c[:, 512 * c:512 * (c + 1)], ones_r[:, :],
                            crow[:, 512 * c:512 * (c + 1)], start=True, stop=True),
                            wait=(p.w("dS") if c == 0 else ()),
                            inc=([("tS", 1)] if c == 3 else ()))
                    p.emit("vector", lambda e: e.tensor_copy(comb_bc[:, :], ps_c[:, :]),
                           wait=p.w("tS"), inc=[("vS", 1)])
                    if "comb_dbg" in dbg_out:
                        p.emit("sync", lambda e: e.dma_start(
                            out=dbg_out["comb_dbg"], in_=combc[:, :]),
                            wait=p.w("vS"), inc=[("dS", 16)])
                wg2 = [sbuf(c2, f"wg2_{i}", [128, ND * 512], BF16) for i in range(2)]
                wu2 = [sbuf(c2, f"wu2_{i}", [128, ND * 512], BF16) for i in range(2)]
                t1 = [sbuf(c2, f"p2t1{i}", [128, 1024], BF16) for i in range(2)]
                s_str = [sbuf(c2, f"p2s{i}", [128, T], BF16) for i in range(2)]
                psG = [psum(c2, f"p2g{i}", [128, 1024]) for i in range(2)]
                psU = [psum(c2, f"p2u{i}", [128, 1024]) for i in range(2)]
                ogs = [(wgT, wuT, 512 * og, 512 * og, False) for og in range(16)]
                ogs += [(wrgT, wruT, 512 * r, INTER + 512 * r, True) for r in range(2)]
                gi = 0
                ev_mark = {}
                og_mm_marks = {}
                for og, (gsrc, usrc, col0, srow0, routed) in enumerate(ogs):
                    wpar = og % 2
                    lw = p.w("tS", "vS", "cS", "dS") if og == 0 else (
                        p.at("tS", og_mm_marks[og - 2]) if og >= 2 else ())
                    p.emit("gpsimd", lambda e, gsrc=gsrc, col0=col0, wpar=wpar: e.dma_start(
                        out=wg2[wpar][:, :].rearrange("p (k c) -> p k c", c=512),
                        in_=gsrc[:, col0:col0 + 512].rearrange("(k p) c -> p k c", p=128)),
                        wait=lw, inc=[(f"dW{wpar}", 16)])
                    p.emit("gpsimd", lambda e, usrc=usrc, col0=col0, wpar=wpar: e.dma_start(
                        out=wu2[wpar][:, :].rearrange("p (k c) -> p k c", c=512),
                        in_=usrc[:, col0:col0 + 512].rearrange("(k p) c -> p k c", p=128)),
                        inc=[(f"dW{wpar}", 16)])
                    dg_mark = S[f"dW{wpar}"]
                    for m in range(4):
                        s_par = m % 2
                        for hf in range(2):
                            par = gi % 2
                            gi += 1
                            wv = p.at(f"dW{wpar}", dg_mark)
                            if gi > 2:
                                wv = wv + p.at("vS", ev_mark[gi - 2])
                            for k in range(ND):
                                for c in range(2):
                                    p.emit("tensor", lambda e, k=k, c=c, m=m, hf=hf, par=par, wpar=wpar: e.matmul(
                                        psG[par][:, 512 * c:512 * (c + 1)],
                                        wg2[wpar][:, 512 * k + 128 * m:512 * k + 128 * (m + 1)],
                                        arena_t(k)[:, 1024 * hf + 512 * c:1024 * hf + 512 * (c + 1)],
                                        start=(k == 0), stop=(k == ND - 1)),
                                        wait=(wv if (k == 0 and c == 0) else ()),
                                        inc=([("tS", 1)] if (k == ND - 1 and c == 1) else ()))
                            g_mark = S["tS"]
                            for k in range(ND):
                                for c in range(2):
                                    p.emit("tensor", lambda e, k=k, c=c, m=m, hf=hf, par=par, wpar=wpar: e.matmul(
                                        psU[par][:, 512 * c:512 * (c + 1)],
                                        wu2[wpar][:, 512 * k + 128 * m:512 * k + 128 * (m + 1)],
                                        arena_t(k)[:, 1024 * hf + 512 * c:1024 * hf + 512 * (c + 1)],
                                        start=(k == 0), stop=(k == ND - 1)),
                                        inc=([("tS", 1)] if (k == ND - 1 and c == 1) else ()))
                            u_mark = S["tS"]
                            p.emit("scalar", lambda e, par=par: e.activation(
                                t1[par][:, :], psG[par][:, :], AF.Silu),
                                wait=p.at("tS", g_mark), inc=[("cS", 1)])
                            if routed:
                                p.emit("vector", lambda e, par=par, hf=hf: e.tensor_tensor(
                                    t1[par][:, :], t1[par][:, :],
                                    comb_bc[:, 1024 * hf:1024 * (hf + 1)], ALU.mult),
                                    wait=p.w("cS"))
                            p.emit("vector", lambda e, par=par, s_par=s_par, hf=hf: e.tensor_tensor(
                                s_str[s_par][:, 1024 * hf:1024 * (hf + 1)],
                                t1[par][:, :], psU[par][:, :], ALU.mult),
                                wait=p.at("tS", u_mark) + p.w("cS") +
                                (p.w("dS") if (hf == 0 and gi > 4) else []),
                                inc=[("vS", 1)])
                            ev_mark[gi] = S["vS"]
                        p.emit("sync", lambda e, srow0=srow0, m=m, s_par=s_par: e.dma_start(
                            out=s_hbm[srow0 + 128 * m:srow0 + 128 * (m + 1), :],
                            in_=s_str[s_par][:, :]),
                            wait=p.w("vS"), inc=[("dS", 16)])
                    og_mm_marks[og] = S["tS"]
                if "s_dbg" in dbg_out:
                    p.emit("sync", lambda e: e.dma_start(out=dbg_out["s_dbg"], in_=s_hbm),
                           wait=p.w("dS"), inc=[("dS", 16)])
                p.run()

        if upto >= 3:
            p = Phase(nc, sems, S)
            with ExitStack() as c3:
                ss_sb = sbuf(c3, "ss_sb", [128, ND * T], BF16)
                wd_sb = [sbuf(c3, f"wd{i}", [128, ND * 128], BF16) for i in range(2)]
                dtmp = [sbuf(c3, f"dt{i}", [128, T], F32) for i in range(2)]
                psD = [psum(c3, f"p3d{i}", [128, T]) for i in range(2)]
                slabs = [(0, 16), (2048, 16), (4096, 16), (6144, 16), (8192, 8)]
                acc_done = {}
                ev_mark = {}
                mm_marks = {}
                gi = 0
                units = []
                for si, (r0, nk) in enumerate(slabs):
                    for mo in range(ND):
                        units.append((si, r0, nk, mo))

                def emit_band(u):
                    si, r0, nk, mo = units[u]
                    par = (u + 1) % 2
                    p.emit("gpsimd", lambda e, r0=r0, nk=nk, mo=mo, par=par: e.dma_start(
                        out=wd_sb[par][:, 0:nk * 128].rearrange("p (k c) -> p k c", c=128),
                        in_=wdsT[r0:r0 + nk * 128, 128 * mo:128 * (mo + 1)].rearrange(
                            "(k p) c -> p k c", p=128)),
                        wait=(p.at("tS", mm_marks[u - 2]) if u >= 2 else ()),
                        inc=[(f"dW{par}", 16)])
                    return S[f"dW{par}"]

                band_mark = {}
                band_mark[0] = emit_band(0)
                band_mark[1] = emit_band(1)
                cur_slab = -1
                for u, (si, r0, nk, mo) in enumerate(units):
                    gi += 1
                    par = gi % 2
                    if si != cur_slab:
                        cur_slab = si
                        p.emit("sync", None, wait=p.w("tS"))
                        for k in range(nk):
                            p.emit("sync", lambda e, r0=r0, k=k: e.dma_start(
                                out=ss_sb[:, T * k:T * (k + 1)],
                                in_=s_hbm[r0 + 128 * k:r0 + 128 * (k + 1), :]),
                                inc=[("dS", 16)])
                        ds_mark = S["dS"]
                    wv = p.at("dS", ds_mark) + p.at(f"dW{par}", band_mark[u])
                    if gi > 2:
                        wv = wv + p.at("vS", ev_mark[gi - 2])
                    for k in range(nk):
                        for c in range(4):
                            p.emit("tensor", lambda e, k=k, c=c, par=par, nk=nk: e.matmul(
                                psD[par][:, 512 * c:512 * (c + 1)],
                                wd_sb[par][:, 128 * k:128 * (k + 1)],
                                ss_sb[:, T * k + 512 * c:T * k + 512 * (c + 1)],
                                start=(k == 0), stop=(k == nk - 1)),
                                wait=(wv if (k == 0 and c == 0) else ()),
                                inc=([("tS", 1)] if (k == nk - 1 and c == 3) else ()))
                    mm_marks[u] = S["tS"]
                    if u + 2 < len(units):
                        band_mark[u + 2] = emit_band(u + 2)
                    p.emit("vector", lambda e, par=par: e.tensor_copy(
                        dtmp[par][:, :], psD[par][:, :]),
                        wait=p.w("tS") + (p.at(f"dA{par}", acc_done[gi - 2]) if gi > 2 else []),
                        inc=[("vS", 1)])
                    ev_mark[gi] = S["vS"]
                    p.emit("gpsimd", lambda e, mo=mo, par=par: e.dma_start(
                        out=xh[128 * mo:128 * (mo + 1), :], in_=dtmp[par][:, :],
                        accum_op=ALU.add),
                        wait=p.w("vS") + (p.at(f"dA{par}", acc_done.get(gi - 16, 0))
                                          if gi > 16 else []),
                        inc=[(f"dA{par}", 16)])
                    acc_done[gi] = S[f"dA{par}"]
                if "xh_dbg" in dbg_out and upto == 3:
                    p.emit("sync", lambda e: e.dma_start(out=dbg_out["xh_dbg"], in_=xh),
                           wait=p.w("dG", "dA0", "dA1"), inc=[("dS", 16)])
                p.run()

        b_scope = ExitStack()
        cqn = sbuf(b_scope, "cqn", [128, NQ * T], BF16)
        ckv = sbuf(b_scope, "ckv", [128, NKV * T], BF16)
        kpe = sbuf(b_scope, "kpe", [64, T], BF16)
        cs_sb = sbuf(b_scope, "cs_sb", [128, T], BF16)

        def cqn_t(i):
            return cqn[:, T * i:T * (i + 1)]

        def ckv_t(i):
            return ckv[:, T * i:T * (i + 1)]

        if upto >= 4:
            p = Phase(nc, sems, S)
            with ExitStack() as c4:
                wa_sb = [sbuf(c4, f"p4w{i}", [128, ND * 128], BF16) for i in range(2)]
                lnq_sb = sbuf(c4, "lnq_sb", [128, NQ], F32)
                lnkv_sb = sbuf(c4, "lnkv_sb", [128, NKV], F32)
                sqf = sbuf(c4, "p4sq", [128, T], F32)
                srow = sbuf(c4, "p4srow", [1, T], F32)
                rstd_bc = sbuf(c4, "p4rstd", [128, T], F32)
                t1f = sbuf(c4, "p4t1", [128, T], F32)
                t64 = sbuf(c4, "p4t64", [64, T], F32)
                psC2 = [psum(c4, f"p4c{i}", [128, 1024]) for i in range(2)]
                psS = psum(c4, "p4s", [1, T])

                p.emit("sync", lambda e: e.dma_start(out=lnq_sb[:, :], in_=lnq_in),
                       inc=[("dS", 16)])
                p.emit("sync", lambda e: e.dma_start(out=lnkv_sb[:, :], in_=lnkv_in),
                       inc=[("dS", 16)])
                p.emit("gpsimd", lambda e: e.dma_start(out=cs_sb[:, :], in_=cs_qk_in),
                       inc=[("dG", 16)])
                pmm_marks = []

                def proj_rms(wsrcT, nm, n_rms, dst_t, ln_sb, scale):
                    ev_marks = {}
                    for m in range(nm):
                        par = m % 2
                        p.emit("gpsimd", lambda e, m=m, par=par, wsrcT=wsrcT: e.dma_start(
                            out=wa_sb[par][:, :].rearrange("p (k c) -> p k c", c=128),
                            in_=wsrcT[:, 128 * m:128 * (m + 1)].rearrange(
                                "(k p) c -> p k c", p=128)),
                            wait=(p.at("tS", pmm_marks[-2]) if len(pmm_marks) >= 2
                                  else ()),
                            inc=[(f"dW{par}", 16)])
                        dwm = S[f"dW{par}"]
                        for h2 in range(2):
                            hpar = (2 * m + h2) % 2
                            wv = p.at(f"dW{par}", dwm) + \
                                p.at("vS", ev_marks.get(2 * m + h2 - 2, 0)) + \
                                p.at("cS", ev_marks.get((2 * m + h2 - 2, "c"), 0))
                            for k in range(ND):
                                for c in range(2):
                                    p.emit("tensor", lambda e, k=k, c=c, h2=h2, par=par,
                                           hpar=hpar: e.matmul(
                                        psC2[hpar][:, 512 * c:512 * (c + 1)],
                                        wa_sb[par][:, 128 * k:128 * (k + 1)],
                                        arena_t(k)[:, 1024 * h2 + 512 * c:1024 * h2 + 512 * (c + 1)],
                                        start=(k == 0), stop=(k == ND - 1)),
                                        wait=(wv if (k == 0 and c == 0) else ()),
                                        inc=([("tS", 1)] if (k == ND - 1 and c == 1) else ()))
                            hm = S["tS"]
                            if m < n_rms:
                                p.emit("vector", lambda e, m=m, h2=h2, hpar=hpar: e.tensor_copy(
                                    dst_t(m)[:, 1024 * h2:1024 * (h2 + 1)], psC2[hpar][:, :]),
                                    wait=p.at("tS", hm), inc=[("vS", 1)])
                                ev_marks[2 * m + h2] = S["vS"]
                                p.emit("scalar", lambda e, m=m, h2=h2: e.square(
                                    sqf[:, 1024 * h2:1024 * (h2 + 1)],
                                    dst_t(m)[:, 1024 * h2:1024 * (h2 + 1)]),
                                    wait=p.w("vS"), inc=[("cS", 1)])
                                ev_marks[(2 * m + h2, "c")] = S["cS"]
                                for c in range(2):
                                    p.emit("tensor", lambda e, c=c, h2=h2, m=m,
                                           n_rms=n_rms: e.matmul(
                                        psS[:, 1024 * h2 + 512 * c:1024 * h2 + 512 * (c + 1)],
                                        ones_c[:, :],
                                        sqf[:, 1024 * h2 + 512 * c:1024 * h2 + 512 * (c + 1)],
                                        start=(m == 0), stop=(m == n_rms - 1)),
                                        wait=(p.w("cS") if c == 0 else ()),
                                        inc=([("tS", 1)] if c == 1 else ()))
                            else:
                                p.emit("vector", lambda e, h2=h2, hpar=hpar: e.tensor_tensor(
                                    t1f[:, 1024 * h2:1024 * (h2 + 1)], psC2[hpar][:, :],
                                    cs_sb[:, 1024 * h2:1024 * (h2 + 1)], ALU.mult),
                                    wait=p.at("tS", hm) + p.w("dG"), inc=[("vS", 1)])
                                ev_marks[2 * m + h2] = S["vS"]
                        pmm_marks.append(S["tS"])
                        if m >= n_rms:
                            p.emit("gpsimd", lambda e: e.dma_start(
                                out=t64[:, :], in_=t1f[64:128, :]),
                                wait=p.w("vS"), inc=[("dG", 16)])
                            p.emit("vector", lambda e: e.tensor_tensor(
                                kpe[:, :], t1f[0:64, :], t64[:, :], ALU.add),
                                wait=p.w("dG"), inc=[("vS", 1)])
                    p.emit("vector", lambda e: e.tensor_copy(srow[:, :], psS[:, :]),
                           wait=p.w("tS", "dS"))
                    p.emit("vector", lambda e, n_rms=n_rms: e.tensor_scalar(
                        srow[:, :], srow[:, :], 1.0 / (n_rms * 128), None, ALU.mult),
                        inc=[("vS", 1)])
                    p.emit("scalar", lambda e: e.activation(
                        srow[:, :], srow[:, :], AF.Sqrt, bias=cst[0:1, 1:2]),
                        wait=p.w("vS"), inc=[("cS", 1)])
                    p.emit("vector", lambda e: e.reciprocal(srow[:, :], srow[:, :]),
                           wait=p.w("cS"))
                    if scale != 1.0:
                        p.emit("vector", lambda e, scale=scale: e.tensor_scalar(
                            srow[:, :], srow[:, :], scale, None, ALU.mult))
                    p.emit("vector", lambda e: e.engine_nop(), inc=[("vS", 1)])
                    for h2 in range(2):
                        for c in range(2):
                            p.emit("tensor", lambda e, c=c, h2=h2: e.matmul(
                                psC2[h2][:, 512 * c:512 * (c + 1)], ones_r[:, :],
                                srow[:, 1024 * h2 + 512 * c:1024 * h2 + 512 * (c + 1)],
                                start=True, stop=True),
                                wait=(p.w("vS") if c == 0 else ()),
                                inc=([("tS", 1)] if c == 1 else ()))
                        p.emit("vector", lambda e, h2=h2: e.tensor_copy(
                            rstd_bc[:, 1024 * h2:1024 * (h2 + 1)], psC2[h2][:, :]),
                            wait=p.w("tS"), inc=[("vS", 1)])
                    for m in range(n_rms):
                        p.emit("vector", lambda e, m=m, ln_sb=ln_sb: e.scalar_tensor_tensor(
                            dst_t(m)[:, :], dst_t(m)[:, :], ln_sb[:, m:m + 1],
                            rstd_bc[:, :], ALU.mult, ALU.mult))
                    p.emit("vector", lambda e: e.engine_nop(), inc=[("vS", 1)])

                proj_rms(wqaT, NQ, NQ, cqn_t, lnq_sb, QSC)
                proj_rms(wkvaT, 5, NKV, ckv_t, lnkv_sb, 1.0)
                dbg_dump_fm(p, "cqn_dbg", cqn_t, NQ)
                dbg_dump_fm(p, "ckv_dbg", ckv_t, NKV)
                if "kpe_dbg" in dbg_out:
                    p.emit("gpsimd", lambda e: e.dma_start(
                        out=dbg_out["kpe_dbg"], in_=kpe[:, :]),
                        wait=p.w("vS"), inc=[("dG", 16)])
                p.run()


        if upto >= 6:
            p = Phase(nc, sems, S)
            with ExitStack() as c6:
                wqb_sb = [sbuf(c6, f"wqb{i}", [128, NQ * 256], BF16) for i in range(2)]
                wkvb_sb = [sbuf(c6, f"wkvb{i}", [128, NKV * 256], BF16) for i in range(2)]
                qn2 = [sbuf(c6, f"qn{i}", [128, T], BF16) for i in range(2)]
                qpe2 = [sbuf(c6, f"qpe{i}", [64, T], BF16) for i in range(2)]
                kn2 = [sbuf(c6, f"kn{i}", [128, T], BF16) for i in range(2)]
                vfm1 = sbuf(c6, "vfm1", [128, T], BF16)
                vfm2 = [vfm1, vfm1]
                vtm2 = [sbuf(c6, f"vtm{i}", [128, T], BF16) for i in range(2)]
                probs = [sbuf(c6, f"pr{i}", [128, T], BF16) for i in range(2)]
                probsT = sbuf(c6, "prT", [128, T], BF16)
                atm = sbuf(c6, "atm", [128, T], BF16)
                t1f = sbuf(c6, "p6t1", [128, 256], F32)
                t64 = sbuf(c6, "p6t64", [64, 256], F32)
                mask_sb = sbuf(c6, "mask_sb", [128, 128], BF16)
                cmax = sbuf(c6, "cmax", [128, 4], F32)
                nmax = sbuf(c6, "nmax", [128, 2], F32)
                sums = sbuf(c6, "sums", [128, 1], F32)
                recip = sbuf(c6, "recip", [128, 1], F32)
                psA = psum(c6, "p6a", [128, T])
                psT = psum(c6, "p6t", [128, 1024], BF16)
                psO = psum(c6, "p6o", [128, 128])
                psP = psum(c6, "p6p", [128, 1024])

                p.emit("gpsimd", lambda e: e.dma_start(out=mask_sb[:, :], in_=maskd),
                       inc=[("dG", 16)])
                mask_mark = S["dG"]

                psT_guard = [[("vS", 0)]]
                slice_mark = {}
                hend_mark = {-1: 0, -2: 0}

                def emit_slices(h):
                    hp = h % 2
                    for k in range(NQ):
                        p.emit("gpsimd", lambda e, k=k, h=h, hp=hp: e.dma_start(
                            out=wqb_sb[hp][:, 256 * k:256 * (k + 1)],
                            in_=wqbT[128 * k:128 * (k + 1), 256 * h:256 * (h + 1)]),
                            wait=(p.at("tS", hend_mark[h - 2]) if k == 0 and h >= 2 else ()),
                            inc=[(f"dW{hp}", 16)])
                    for k in range(NKV):
                        p.emit("gpsimd", lambda e, k=k, h=h, hp=hp: e.dma_start(
                            out=wkvb_sb[hp][:, 256 * k:256 * (k + 1)],
                            in_=wkvbT[128 * k:128 * (k + 1), 256 * h:256 * (h + 1)]),
                            inc=[(f"dW{hp}", 16)])
                    slice_mark[h] = S[f"dW{hp}"]

                def proj_units(h):
                    hp = h % 2
                    units = []
                    half_ev = [[("cS", 0)], [("cS", 0)]]
                    uidx = [0]

                    def mm_unit(lhs_off, nkk, rhs_t, dst, tq8, post=None):
                        def go():
                            hh = uidx[0] % 2
                            uidx[0] += 1
                            pslice = psP[:, 512 * hh:512 * hh + 256]
                            wv = p.at(f"dW{h % 2}", slice_mark[h]) + \
                                p.at("tS", hend_mark[h - 2]) + half_ev[hh]
                            for k in range(nkk):
                                p.emit("tensor", lambda e, k=k, lhs_off=lhs_off, nkk=nkk,
                                       rhs_t=rhs_t, tq8=tq8, hp=hp, pslice=pslice: e.matmul(
                                    pslice, wqb_sb[hp][:, 256 * k + lhs_off:256 * k + lhs_off + 128]
                                    if nkk == NQ else
                                    wkvb_sb[hp][:, 256 * k + lhs_off:256 * k + lhs_off + 128],
                                    rhs_t(k)[:, 256 * tq8:256 * (tq8 + 1)],
                                    start=(k == 0), stop=(k == nkk - 1)),
                                    wait=(wv if k == 0 else ()),
                                    inc=([("tS", 1)] if k == nkk - 1 else ()))
                            um = S["tS"]
                            if post is None:
                                p.emit("scalar", lambda e, dst=dst, tq8=tq8, pslice=pslice: e.copy(
                                    dst[:, 256 * tq8:256 * (tq8 + 1)], pslice),
                                    wait=p.at("tS", um), inc=[("cS", 1)])
                                half_ev[hh] = p.at("cS", S["cS"])
                            else:
                                post(pslice)
                                half_ev[hh] = p.at("vS", S["vS"])
                        return go

                    for tq8 in range(8):
                        units.append(mm_unit(0, NQ, cqn_t, qn2[hp], tq8))
                    for tq8 in range(8):
                        def rope_post(pslice, tq8=tq8, hp=hp):
                            p.emit("vector", lambda e, tq8=tq8, pslice=pslice: e.tensor_tensor(
                                t1f[:, :], pslice,
                                cs_sb[:, 256 * tq8:256 * (tq8 + 1)], ALU.mult),
                                wait=p.w("tS", "dG"), inc=[("vS", 1)])
                            p.emit("gpsimd", lambda e: e.dma_start(
                                out=t64[:, :], in_=t1f[64:128, :]),
                                wait=p.w("vS"), inc=[("dG", 16)])
                            p.emit("vector", lambda e, tq8=tq8, hp=hp: e.tensor_tensor(
                                qpe2[hp][:, 256 * tq8:256 * (tq8 + 1)],
                                t1f[0:64, :],
                                t64[0:64, :], ALU.add),
                                wait=p.w("dG"), inc=[("vS", 1)])
                        units.append(mm_unit(128, NQ, cqn_t, None, tq8, post=rope_post))
                    for tq8 in range(8):
                        units.append(mm_unit(0, NKV, ckv_t, kn2[hp], tq8))
                    for tq8 in range(8):
                        units.append(mm_unit(128, NKV, ckv_t, vfm2[hp], tq8))

                    def vtmT(batch, hp=hp):
                        def go():
                            for kt in range(8 * batch, 8 * batch + 8):
                                p.emit("tensor", lambda e, kt=kt, batch=batch, hp=hp: e.transpose(
                                    psT[:, 128 * (kt - 8 * batch):128 * (kt - 8 * batch + 1)],
                                    vfm2[hp][:, 128 * kt:128 * (kt + 1)], ident_b[:, :]),
                                    wait=(psT_guard[0] + p.w("vS", "cS") if kt == 8 * batch else ()),
                                    inc=([("tS", 1)] if kt == 8 * batch + 7 else ()))
                            p.emit("vector", lambda e, batch=batch, hp=hp: e.tensor_copy(
                                vtm2[hp][:, 1024 * batch:1024 * (batch + 1)],
                                psT[:, 0:1024]),
                                wait=p.w("tS"), inc=[("vS", 1)])
                            psT_guard[0] = p.at("vS", S["vS"])
                        return go
                    units.append(vtmT(0))
                    units.append(vtmT(1))
                    return units

                emit_slices(0)
                for u in proj_units(0):
                    u()
                emit_slices(1)

                for h in range(H):
                    hp = h % 2
                    pend = list(proj_units(h + 1)) if h + 1 < H else []
                    if h + 2 < H:
                        pass
                    exp_mark, ptc_mark, av_mark, tT_mark, nrm_mark = {}, {}, {}, {}, {}
                    sc_marks = {}

                    def emit_scores(j, h=h, hp=hp):
                        wv = p.w("vS", "cS")
                        if j > 0:
                            wv = p.at("cS", exp_mark[j - 1])
                        for kc in range(ncj(j)):
                            n = min(512, 128 * (j + 1) - 512 * kc)
                            p.emit("tensor", lambda e, j=j, kc=kc, n=n, hp=hp: e.matmul(
                                psA[:, 512 * kc:512 * kc + n],
                                qn2[hp][:, 128 * j:128 * (j + 1)],
                                kn2[hp][:, 512 * kc:512 * kc + n],
                                start=True, stop=False),
                                wait=(wv if kc == 0 else ()))
                            p.emit("tensor", lambda e, j=j, kc=kc, n=n, hp=hp: e.matmul(
                                psA[:, 512 * kc:512 * kc + n],
                                qpe2[hp][:, 128 * j:128 * (j + 1)],
                                kpe[:, 512 * kc:512 * kc + n],
                                start=False, stop=True),
                                inc=[("tS", 1)])
                            sc_marks[(j, kc)] = S["tS"]

                    def emit_vec(j):
                        for kc in range(ncj(j)):
                            n = min(512, 128 * (j + 1) - 512 * kc)
                            if kc == ncj(j) - 1:
                                p.emit("vector", lambda e, j=j: e.tensor_tensor(
                                    psA[:, 128 * j:128 * (j + 1)],
                                    psA[:, 128 * j:128 * (j + 1)], mask_sb[:, :], ALU.add),
                                    wait=p.at("tS", sc_marks[(j, kc)]) + p.at("dG", mask_mark))
                                p.emit("vector", lambda e, kc=kc, n=n: e.reduce_max(
                                    cmax[:, kc:kc + 1], psA[:, 512 * kc:512 * kc + n], AX.X))
                            else:
                                p.emit("vector", lambda e, kc=kc, n=n: e.reduce_max(
                                    cmax[:, kc:kc + 1], psA[:, 512 * kc:512 * kc + n], AX.X),
                                    wait=p.at("tS", sc_marks[(j, kc)]))
                        p.emit("vector", lambda e, j=j: e.reduce_max(
                            nmax[:, j % 2:j % 2 + 1], cmax[:, 0:ncj(j)], AX.X, negate=True),
                            inc=[("vS", 1)])

                    def emit_T_ptc(j, h=h):
                        nb = (j + 8) // 8
                        for b in range(nb):
                            k0, k1 = 8 * b, min(8 * b + 8, j + 1)
                            for kt in range(k0, k1):
                                p.emit("tensor", lambda e, j=j, kt=kt, k0=k0: e.transpose(
                                    psT[:, 128 * (kt - k0):128 * (kt - k0 + 1)],
                                    probs[j % 2][:, 128 * kt:128 * (kt + 1)], ident_b[:, :]),
                                    wait=((p.at("cS", exp_mark[j]) if b == 0 else
                                           p.at("cS", ptc_mark[j])) + psT_guard[0]
                                          if kt == k0 else ()),
                                    inc=([("tS", 1)] if kt == k1 - 1 else ()))
                            tT_mark[j] = S["tS"]
                            p.emit("scalar", lambda e, k0=k0, k1=k1: e.copy(
                                probsT[:, 128 * k0:128 * k1], psT[:, 0:128 * (k1 - k0)]),
                                wait=p.at("tS", tT_mark[j]), inc=[("cS", 1)])
                            ptc_mark[j] = S["cS"]
                            psT_guard[0] = p.at("cS", S["cS"])

                    def emit_AV(j, hp=hp):
                        wv = p.at("cS", ptc_mark[j])
                        if j >= 1:
                            wv = wv + p.at("vS", nrm_mark[j - 1])
                        for kt in range(j + 1):
                            p.emit("tensor", lambda e, j=j, kt=kt, hp=hp: e.matmul(
                                psO[:, :], probsT[:, 128 * kt:128 * (kt + 1)],
                                vtm2[hp][:, 128 * kt:128 * (kt + 1)],
                                start=(kt == 0), stop=(kt == j)),
                                wait=(wv if kt == 0 else ()),
                                inc=([("tS", 1)] if kt == j else ()))
                        av_mark[j] = S["tS"]

                    def emit_recip_norm(j):
                        p.emit("vector", lambda e: e.reciprocal(recip[:, :], sums[:, :]),
                               wait=p.at("cS", exp_mark[j]), inc=[("vS", 1)])
                        p.emit("vector", lambda e, j=j: e.tensor_scalar(
                            atm[:, 128 * j:128 * (j + 1)], psO[:, :], recip[:, 0:1], None,
                            ALU.mult), wait=p.at("tS", av_mark[j]), inc=[("vS", 1)])
                        nrm_mark[j] = S["vS"]

                    def emit_exp(j):
                        kj = 128 * (j + 1)
                        wv = p.w("vS")
                        if j >= 2:
                            wv = wv + p.at("tS", tT_mark[j - 2])
                        p.emit("scalar", lambda e, j=j, kj=kj: e.activation(
                            probs[j % 2][:, 0:kj], psA[:, 0:kj], AF.Exp,
                            bias=nmax[:, j % 2:j % 2 + 1], accum_out=sums[:, 0:1]),
                            wait=wv, inc=[("cS", 1)])
                        exp_mark[j] = S["cS"]

                    ui = 0
                    for j in range(NT):
                        emit_scores(j)
                        emit_vec(j)
                        if j > 0:
                            emit_T_ptc(j - 1)
                            emit_AV(j - 1)
                            emit_recip_norm(j - 1)
                        want = ((j + 1) * len(pend)) // NT
                        while ui < want:
                            pend[ui]()
                            ui += 1
                        emit_exp(j)
                    emit_T_ptc(NT - 1)
                    emit_AV(NT - 1)
                    emit_recip_norm(NT - 1)
                    while ui < len(pend):
                        pend[ui]()
                        ui += 1

                    nm15 = nrm_mark[NT - 1]
                    for b in range(2):
                        for j in range(8 * b, 8 * b + 8):
                            p.emit("tensor", lambda e, j=j, b=b: e.transpose(
                                psT[:, 128 * (j - 8 * b):128 * (j - 8 * b + 1)],
                                atm[:, 128 * j:128 * (j + 1)], ident_b[:, :]),
                                wait=((p.at("vS", nm15) if b == 0 else p.w("vS")) +
                                      psT_guard[0] if j == 8 * b else ()),
                                inc=([("tS", 1)] if j == 8 * b + 7 else ()))
                        p.emit("vector", lambda e, h=h, b=b: e.tensor_copy(
                            arena_t(h)[:, 1024 * b:1024 * (b + 1)], psT[:, 0:1024]),
                            wait=p.w("tS"), inc=[("vS", 1)])
                        psT_guard[0] = p.at("vS", S["vS"])
                    hend_mark[h] = S["tS"]
                    if h + 2 < H:
                        emit_slices(h + 2)
                dbg_dump_fm(p, "cat_dbg", arena_t, ND)
                p.run()

        b_scope.close()

        if upto >= 7:
            p = Phase(nc, sems, S)
            with ExitStack() as c7:
                wo_sb = [sbuf(c7, f"wo{i}", [128, ND * 128], BF16) for i in range(2)]
                dtmp = [sbuf(c7, f"p7dt{i}", [128, T], F32) for i in range(2)]
                psW = [psum(c7, f"p7w{i}", [128, T]) for i in range(2)]
                acc_done = {}
                ev_mark = {}
                mm_marks = {}
                band_mark = {}

                def emit_band7(u):
                    par = u % 2
                    p.emit("gpsimd", lambda e, u=u, par=par: e.dma_start(
                        out=wo_sb[par][:, :].rearrange("p (k c) -> p k c", c=128),
                        in_=woT[:, 128 * u:128 * (u + 1)].rearrange(
                            "(k p) c -> p k c", p=128)),
                        wait=(p.at("tS", mm_marks[u - 2]) if u >= 2 else ()),
                        inc=[(f"dW{par}", 16)])
                    return S[f"dW{par}"]

                band_mark[0] = emit_band7(0)
                band_mark[1] = emit_band7(1)
                for mo in range(ND):
                    par = mo % 2
                    wv = p.at(f"dW{par}", band_mark[mo]) + (
                        p.at("vS", ev_mark[mo - 2]) if mo >= 2 else [])
                    for k in range(ND):
                        for c in range(4):
                            p.emit("tensor", lambda e, k=k, c=c, par=par: e.matmul(
                                psW[par][:, 512 * c:512 * (c + 1)],
                                wo_sb[par][:, 128 * k:128 * (k + 1)],
                                arena_t(k)[:, 512 * c:512 * (c + 1)],
                                start=(k == 0), stop=(k == ND - 1)),
                                wait=(wv if (k == 0 and c == 0) else ()),
                                inc=([("tS", 1)] if (k == ND - 1 and c == 3) else ()))
                    mm_marks[mo] = S["tS"]
                    if mo + 2 < ND:
                        band_mark[mo + 2] = emit_band7(mo + 2)
                    p.emit("vector", lambda e, par=par: e.tensor_copy(
                        dtmp[par][:, :], psW[par][:, :]),
                        wait=p.w("tS") + (p.at(f"dA{par}", acc_done[mo - 2]) if mo >= 2 else []),
                        inc=[("vS", 1)])
                    ev_mark[mo] = S["vS"]
                    p.emit("gpsimd", lambda e, mo=mo, par=par: e.dma_start(
                        out=xh[128 * mo:128 * (mo + 1), :], in_=dtmp[par][:, :],
                        accum_op=ALU.add),
                        wait=p.w("vS"), inc=[(f"dA{par}", 16)])
                    acc_done[mo] = S[f"dA{par}"]
                if "xh_dbg" in dbg_out and upto == 7:
                    p.emit("sync", lambda e: e.dma_start(out=dbg_out["xh_dbg"], in_=xh),
                           wait=p.w("dG", "dA0", "dA1"), inc=[("dS", 16)])
                p.run()

        if upto >= 8:
            p = Phase(nc, sems, S)
            with ExitStack() as c8:
                xt = [sbuf(c8, f"p8x{i}", [128, T], F32) for i in range(2)]
                sqf = sbuf(c8, "p8sq", [128, T], F32)
                r1 = sbuf(c8, "p8r1", [1, T], F32)
                r2 = sbuf(c8, "p8r2", [1, T], F32)
                nrbc = sbuf(c8, "p8nr", [128, T], F32)
                nmbc = sbuf(c8, "p8nm", [128, T], F32)
                tmpf = sbuf(c8, "p8tmp", [128, T], F32)
                with ExitStack() as c8s:
                    psS1 = psum(c8s, "p8s1", [1, T])
                    psS2 = psum(c8s, "p8s2", [1, T])
                    for i in range(NT):
                        par = i % 2
                        p.emit("sync", lambda e, i=i, par=par: e.dma_start(
                            out=xt[par][:, :], in_=xh[128 * i:128 * (i + 1), :]),
                            wait=(p.at("tS", S["tS"] - 1) if i >= 2 else ()),
                            inc=[(f"dH{par}", 16)])
                        ld_mark = S[f"dH{par}"]
                        p.emit("scalar", lambda e, par=par: e.square(
                            sqf[:, :], xt[par][:, :]),
                            wait=p.at(f"dH{par}", ld_mark) + (p.w("tS") if i > 0 else []),
                            inc=[("cS", 1)])
                        for c in range(4):
                            p.emit("tensor", lambda e, c=c, i=i, par=par: e.matmul(
                                psS1[:, 512 * c:512 * (c + 1)], ones_c[:, :],
                                xt[par][:, 512 * c:512 * (c + 1)],
                                start=(i == 0), stop=(i == NT - 1)),
                                wait=(p.at(f"dH{par}", ld_mark) if c == 0 else ()),
                                inc=([("tS", 1)] if c == 3 else ()))
                        for c in range(4):
                            p.emit("tensor", lambda e, c=c, i=i: e.matmul(
                                psS2[:, 512 * c:512 * (c + 1)], ones_c[:, :],
                                sqf[:, 512 * c:512 * (c + 1)],
                                start=(i == 0), stop=(i == NT - 1)),
                                wait=(p.w("cS") if c == 0 else ()),
                                inc=([("tS", 1)] if c == 3 else ()))
                    p.emit("vector", lambda e: e.tensor_copy(r1[:, :], psS1[:, :]),
                           wait=p.w("tS"))
                    p.emit("vector", lambda e: e.tensor_copy(r2[:, :], psS2[:, :]))
                    p.emit("vector", lambda e: e.tensor_scalar(
                        r1[:, :], r1[:, :], 1.0 / D, None, ALU.mult))
                    p.emit("vector", lambda e: e.tensor_scalar(
                        r2[:, :], r2[:, :], 1.0 / D, None, ALU.mult))
                    p.emit("vector", lambda e: e.tensor_tensor(
                        sqf[0:1, 0:T], r1[:, :], r1[:, :], ALU.mult))
                    p.emit("vector", lambda e: e.tensor_tensor(
                        r2[:, :], r2[:, :], sqf[0:1, 0:T], ALU.subtract), inc=[("vS", 1)])
                    p.emit("scalar", lambda e: e.activation(
                        r2[:, :], r2[:, :], AF.Sqrt, bias=cst[0:1, 0:1]),
                        wait=p.w("vS"), inc=[("cS", 1)])
                    p.emit("vector", lambda e: e.reciprocal(r2[:, :], r2[:, :]),
                           wait=p.w("cS"))
                    p.emit("vector", lambda e: e.tensor_tensor(
                        r1[:, :], r1[:, :], r2[:, :], ALU.mult), inc=[("vS", 1)])
                with ExitStack() as c8r:
                    psR = psum(c8r, "p8pr", [128, T])
                    for c in range(4):
                        p.emit("tensor", lambda e, c=c: e.matmul(
                            psR[:, 512 * c:512 * (c + 1)], ones_r[:, :],
                            r2[:, 512 * c:512 * (c + 1)], start=True, stop=True),
                            wait=(p.w("vS") if c == 0 else ()),
                            inc=([("tS", 1)] if c == 3 else ()))
                    p.emit("vector", lambda e: e.tensor_copy(nrbc[:, :], psR[:, :]),
                           wait=p.w("tS"), inc=[("vS", 1)])
                    for c in range(4):
                        p.emit("tensor", lambda e, c=c: e.matmul(
                            psR[:, 512 * c:512 * (c + 1)], ones_r[:, :],
                            r1[:, 512 * c:512 * (c + 1)], start=True, stop=True),
                            wait=(p.w("vS") if c == 0 else ()),
                            inc=([("tS", 1)] if c == 3 else ()))
                    p.emit("vector", lambda e: e.tensor_copy(nmbc[:, :], psR[:, :]),
                           wait=p.w("tS"), inc=[("vS", 1)])
                    vmarks = {}
                    for i in range(NT):
                        par = i % 2
                        p.emit("sync", lambda e, i=i, par=par: e.dma_start(
                            out=xt[par][:, :], in_=xh[128 * i:128 * (i + 1), :]),
                            wait=(p.at("vS", vmarks[i - 2]) if i >= 2 else p.w("tS", "cS")),
                            inc=[(f"dH{par}", 16)])
                        ld_mark = S[f"dH{par}"]
                        p.emit("vector", lambda e, par=par: e.tensor_tensor(
                            tmpf[:, :], xt[par][:, :], nrbc[:, :], ALU.mult),
                            wait=p.at(f"dH{par}", ld_mark), inc=[("vS", 1)])
                        p.emit("vector", lambda e, i=i: e.tensor_tensor(
                            arena_t(i)[:, :], tmpf[:, :], nmbc[:, :], ALU.subtract),
                            inc=[("vS", 1)])
                        vmarks[i] = S["vS"]
                dbg_dump_fm(p, "xn1_dbg", arena_t, ND)
                p.run()

        if upto >= 9:
            p = Phase(nc, sems, S)
            with ExitStack() as c9:
                wq_sb2 = [sbuf(c9, f"p9wq{i}", [128, ND * 1024], BF16) for i in range(2)]
                hh_sb = sbuf(c9, "p9hh", [128, D], F32)
                gw_sb = sbuf(c9, "p9gw", [128, D], F32)
                gb_sb = sbuf(c9, "p9gb", [128, D], F32)
                c32_sb = sbuf(c9, "p9c32", [128, NT * 32], F32)
                s32_sb = sbuf(c9, "p9s32", [128, NT * 32], F32)
                h_sb = [sbuf(c9, f"p9h{i}", [128, 1024], F32) for i in range(2)]
                h2_sb = [sbuf(c9, f"p9h2{i}", [128, 1024], F32) for i in range(2)]
                gn_f = [sbuf(c9, f"p9gn{i}", [128, 1024], F32) for i in range(2)]
                sq_f = sbuf(c9, "p9sqf", [128, 1024], F32)
                gst = sbuf(c9, "p9gst", [128, 64], F32)
                rt = sbuf(c9, "p9rt", [128, 64], F32)
                psG = [psum(c9, f"p9g{i}", [128, 1024]) for i in range(2)]

                for dst, src in [(hh_sb, hhb_in), (gw_sb, gnw_in), (gb_sb, gnb_in)]:
                    p.emit("sync", lambda e, d=dst, s=src: e.dma_start(out=d[:, :], in_=s),
                           inc=[("dS", 16)])
                for i in range(NT):
                    p.emit("sync", lambda e, i=i: e.dma_start(
                        out=c32_sb[:, 32 * i:32 * (i + 1)],
                        in_=c32_in[128 * i:128 * (i + 1), :]), inc=[("dS", 16)])
                    p.emit("sync", lambda e, i=i: e.dma_start(
                        out=s32_sb[:, 32 * i:32 * (i + 1)],
                        in_=s32_in[128 * i:128 * (i + 1), :]), inc=[("dS", 16)])
                gi = 0
                relu_mark = {}
                acc_done = {}
                og_dgm = {}
                for og in range(2):
                    p.emit("gpsimd", lambda e, og=og: e.dma_start(
                        out=wq_sb2[og][:, :].rearrange("p (k c) -> p k c", c=1024),
                        in_=WQm[:, 1024 * og:1024 * (og + 1)].rearrange(
                            "(k p) c -> p k c", p=128)),
                        inc=[(f"dW{og}", 16)])
                    og_dgm[og] = S[f"dW{og}"]
                for og in range(2):
                    wq_sb = wq_sb2[og]
                    dgm = og_dgm[og]
                    for t in range(NT):
                        gi += 1
                        par = gi % 2
                        wv = p.at(f"dW{og}", dgm) + p.w("dS")
                        if gi > 2:
                            wv = wv + p.at("vS", relu_mark[gi - 2])
                        for k in range(ND):
                            for c in range(2):
                                p.emit("tensor", lambda e, k=k, c=c, t=t, par=par, wq_sb=wq_sb: e.matmul(
                                    psG[par][:, 512 * c:512 * (c + 1)],
                                    arena_t(k)[:, 128 * t:128 * (t + 1)],
                                    wq_sb[:, 1024 * k + 512 * c:1024 * k + 512 * (c + 1)],
                                    start=(k == 0), stop=(k == ND - 1)),
                                    wait=(wv if (k == 0 and c == 0) else ()),
                                    inc=([("tS", 1)] if (k == ND - 1 and c == 1) else ()))
                        tm = S["tS"]
                        hv, h2v, gv = h_sb[par], h2_sb[par], gn_f[par]
                        if og == 0:
                            def rview(par=par, t=t, hv=hv):
                                pg = psG[par]
                                A = pg[:, 0:64].rearrange("p (a b) -> p a b", b=2)[:, :, 0:1]
                                B = pg[:, 0:64].rearrange("p (a b) -> p a b", b=2)[:, :, 1:2]
                                ct = c32_sb[:, 32 * t:32 * (t + 1)].rearrange("p (a o) -> p a o", o=1)
                                st_ = s32_sb[:, 32 * t:32 * (t + 1)].rearrange("p (a o) -> p a o", o=1)
                                hA = hv[:, 0:64].rearrange("p (a b) -> p a b", b=2)[:, :, 0:1]
                                hB = hv[:, 0:64].rearrange("p (a b) -> p a b", b=2)[:, :, 1:2]
                                r0 = rt[:, 0:32].rearrange("p (a o) -> p a o", o=1)
                                r1_ = rt[:, 32:64].rearrange("p (a o) -> p a o", o=1)
                                return A, B, ct, st_, hA, hB, r0, r1_
                            A, B, ct_, st_, hA, hB, r0, r1_ = rview()
                            p.emit("vector", lambda e, A=A, ct_=ct_, r0=r0: e.tensor_tensor(
                                r0, A, ct_, ALU.mult), wait=p.at("tS", tm))
                            p.emit("vector", lambda e, B=B, st_=st_, r1_=r1_: e.tensor_tensor(
                                r1_, B, st_, ALU.mult))
                            p.emit("vector", lambda e, hA=hA, r0=r0, r1_=r1_: e.tensor_tensor(
                                hA, r0, r1_, ALU.subtract))
                            p.emit("vector", lambda e, B=B, ct_=ct_, r0=r0: e.tensor_tensor(
                                r0, B, ct_, ALU.mult))
                            p.emit("vector", lambda e, A=A, st_=st_, r1_=r1_: e.tensor_tensor(
                                r1_, A, st_, ALU.mult))
                            p.emit("vector", lambda e, hB=hB, r0=r0, r1_=r1_: e.tensor_tensor(
                                hB, r0, r1_, ALU.add))
                            p.emit("vector", lambda e, hv=hv: e.tensor_relu(
                                hv[:, 0:64], hv[:, 0:64]))
                            p.emit("vector", lambda e, par=par, hv=hv: e.tensor_relu(
                                hv[:, 64:1024], psG[par][:, 64:1024]), inc=[("vS", 1)])
                        else:
                            p.emit("vector", lambda e, par=par, hv=hv: e.tensor_relu(
                                hv[:, :], psG[par][:, :]), wait=p.at("tS", tm),
                                inc=[("vS", 1)])
                        relu_mark[gi] = S["vS"]
                        p.emit("vector", lambda e, og=og, hv=hv: e.tensor_tensor(
                            hv[:, :], hv[:, :], hh_sb[:, 1024 * og:1024 * (og + 1)], ALU.add),
                            inc=[("vS", 1)])
                        p.emit("scalar", lambda e, hv=hv, h2v=h2v: e.activation(
                            h2v[:, :], hv[:, :], AF.Tanh), wait=p.w("vS"), inc=[("cS", 1)])
                        p.emit("scalar", lambda e, h2v=h2v: e.square(sq_f[:, :], h2v[:, :]),
                               inc=[("cS", 1)])
                        p.emit("vector", lambda e, h2v=h2v: e.tensor_reduce(
                            gst[:, 0:16].rearrange("p (g o) -> p g o", o=1),
                            h2v[:, :].rearrange("p (g d) -> p g d", d=64), AX.X, ALU.add),
                            wait=p.w("cS"))
                        p.emit("vector", lambda e: e.tensor_reduce(
                            gst[:, 16:32].rearrange("p (g o) -> p g o", o=1),
                            sq_f[:, :].rearrange("p (g d) -> p g d", d=64), AX.X, ALU.add))
                        p.emit("vector", lambda e: e.tensor_scalar(
                            gst[:, 0:16], gst[:, 0:16], 1.0 / 64, None, ALU.mult))
                        p.emit("vector", lambda e: e.tensor_scalar(
                            gst[:, 16:32], gst[:, 16:32], 1.0 / 64, None, ALU.mult))
                        p.emit("vector", lambda e: e.tensor_tensor(
                            gst[:, 32:48], gst[:, 0:16], gst[:, 0:16], ALU.mult))
                        p.emit("vector", lambda e: e.tensor_tensor(
                            gst[:, 16:32], gst[:, 16:32], gst[:, 32:48], ALU.subtract),
                            inc=[("vS", 1)])
                        p.emit("scalar", lambda e: e.activation(
                            gst[:, 16:32], gst[:, 16:32], AF.Sqrt, bias=cst[:, 2:3]),
                            wait=p.w("vS"), inc=[("cS", 1)])
                        p.emit("vector", lambda e: e.reciprocal(gst[:, 16:32], gst[:, 16:32]),
                               wait=p.w("cS"))
                        for g in range(16):
                            p.emit("vector", lambda e, g=g, h2v=h2v, gv=gv: e.tensor_scalar(
                                gv[:, 64 * g:64 * (g + 1)], h2v[:, 64 * g:64 * (g + 1)],
                                gst[:, g:g + 1], gst[:, 16 + g:17 + g],
                                ALU.subtract, ALU.mult),
                                wait=(p.at(f"dA{par}", acc_done[gi - 2]) if (g == 0 and gi > 2)
                                      else ()))
                        p.emit("vector", lambda e, og=og, gv=gv: e.tensor_tensor(
                            gv[:, :], gv[:, :], gw_sb[:, 1024 * og:1024 * (og + 1)], ALU.mult))
                        p.emit("vector", lambda e, og=og, gv=gv: e.tensor_tensor(
                            gv[:, :], gv[:, :], gb_sb[:, 1024 * og:1024 * (og + 1)], ALU.add),
                            inc=[("vS", 1)])
                        vmark = S["vS"]
                        p.emit("gpsimd", lambda e, t=t, og=og, gv=gv: e.dma_start(
                            out=xh[128 * t:128 * (t + 1), 1024 * og:1024 * (og + 1)],
                            in_=gv[:, :], accum_op=ALU.add),
                            wait=p.at("vS", vmark), inc=[(f"dA{par}", 16)])
                        acc_done[gi] = S[f"dA{par}"]
                p.run()

        if upto >= 10:
            p = Phase(nc, sems, S)
            with ExitStack() as ca:
                wf_sb = sbuf(ca, "pawf", [128, ND * 512], F32)
                wf2_sb = sbuf(ca, "pawf2", [128, ND * 512], F32)
                wq1 = [sbuf(ca, f"pawq1{i}", [128, ND * 512], BF16) for i in range(2)]
                wq2 = [sbuf(ca, f"pawq2{i}", [128, ND * 512], BF16) for i in range(2)]
                sgt = sbuf(ca, "pasg", [128, T], BF16)
                hstr = [sbuf(ca, f"pah{i}", [128, T], BF16) for i in range(2)]
                psB1 = psum(ca, "pab1", [128, T])
                psB2 = psum(ca, "pab2", [128, T])
                mm_marks = {}
                q_marks = {}
                ld_marks = {}

                def emit_loads(og):
                    c0 = 512 * og
                    lw = p.w("tS", "vS") if og == 0 else p.at("vS", q_marks[og - 1])
                    p.emit("sync", lambda e, c0=c0: e.dma_start(
                        out=wf_sb[:, 0:ND * 512].rearrange("p (k c) -> p k c", c=512),
                        in_=w1T[:, c0:c0 + 512].rearrange("(k p) c -> p k c", p=128)),
                        wait=lw, inc=[("dL", 16)])
                    ld_marks[(og, 1)] = S["dL"]
                    p.emit("sync", lambda e, c0=c0: e.dma_start(
                        out=wf2_sb[:, 0:ND * 512].rearrange("p (k c) -> p k c", c=512),
                        in_=w2T[:, c0:c0 + 512].rearrange("(k p) c -> p k c", p=128)),
                        inc=[("dP0", 16)])
                    ld_marks[(og, 2)] = S["dP0"]

                def emit_quant(og):
                    par = og % 2
                    for wq, wf, ldm, lds, ti in ((wq1, wf_sb, ld_marks[(og, 1)], "dL", 0),
                                                 (wq2, wf2_sb, ld_marks[(og, 2)], "dP0", 2)):
                        for k in range(ND):
                            p.emit("vector", lambda e, k=k, par=par, wq=wq, wf=wf, ti=ti: e.tensor_scalar(
                                wq[par][:, 512 * k:512 * (k + 1)], wf[:, 512 * k:512 * (k + 1)],
                                thr[:, ti:ti + 1], 1.0, ALU.is_ge, ALU.subtract),
                                wait=((p.at(lds, ldm) +
                                       (p.at("tS", mm_marks[og - 2]) if og >= 2 else []))
                                      if k == 0 else ()))
                            p.emit("vector", lambda e, k=k, par=par, wq=wq, wf=wf, ti=ti: e.scalar_tensor_tensor(
                                wq[par][:, 512 * k:512 * (k + 1)], wf[:, 512 * k:512 * (k + 1)],
                                thr[:, ti + 1:ti + 2], wq[par][:, 512 * k:512 * (k + 1)],
                                ALU.is_gt, ALU.add))
                    p.emit("vector", lambda e: e.engine_nop(), inc=[("vS", 1)])
                    q_marks[og] = S["vS"]

                emit_loads(0)
                emit_quant(0)
                for og in range(11):
                    c0 = 512 * og
                    par = og % 2
                    if og + 1 < 11:
                        emit_loads(og + 1)
                        emit_quant(og + 1)
                    for m in range(4):
                        s_par = m % 2
                        wv = p.at("vS", q_marks[og])
                        if not (og == 0 and m == 0):
                            wv = wv + p.w("cS")
                        for k in range(ND):
                            for c in range(4):
                                p.emit("tensor", lambda e, k=k, c=c, m=m, par=par: e.matmul(
                                    psB1[:, 512 * c:512 * (c + 1)],
                                    wq1[par][:, 512 * k + 128 * m:512 * k + 128 * (m + 1)],
                                    arena_t(k)[:, 512 * c:512 * (c + 1)],
                                    start=(k == 0), stop=(k == ND - 1)),
                                    wait=(wv if (k == 0 and c == 0) else ()),
                                    inc=([("tS", 1)] if (k == ND - 1 and c == 3) else ()))
                        b1m = S["tS"]
                        wv2 = p.w("vS") if not (og == 0 and m == 0) else ()
                        for k in range(ND):
                            for c in range(4):
                                p.emit("tensor", lambda e, k=k, c=c, m=m, par=par: e.matmul(
                                    psB2[:, 512 * c:512 * (c + 1)],
                                    wq2[par][:, 512 * k + 128 * m:512 * k + 128 * (m + 1)],
                                    arena_t(k)[:, 512 * c:512 * (c + 1)],
                                    start=(k == 0), stop=(k == ND - 1)),
                                    wait=(wv2 if (k == 0 and c == 0) else ()),
                                    inc=([("tS", 1)] if (k == ND - 1 and c == 3) else ()))
                        b2m = S["tS"]
                        p.emit("scalar", lambda e: e.activation(
                            sgt[:, :], psB1[:, :], AF.Sigmoid),
                            wait=p.at("tS", b1m) + p.w("vS"), inc=[("cS", 1)])
                        p.emit("vector", lambda e, s_par=s_par: e.tensor_tensor(
                            hstr[s_par][:, :], sgt[:, :], psB2[:, :], ALU.mult),
                            wait=p.at("tS", b2m) + p.w("cS") +
                            (p.w("dS") if not (og == 0 and m <= 1) else []),
                            inc=[("vS", 1)])
                        p.emit("sync", lambda e, c0=c0, m=m, s_par=s_par: e.dma_start(
                            out=hid_hbm[c0 + 128 * m:c0 + 128 * (m + 1), :],
                            in_=hstr[s_par][:, :]),
                            wait=p.w("vS"), inc=[("dS", 16)])
                    mm_marks[og] = S["tS"]
                if "hid_dbg" in dbg_out:
                    p.emit("sync", lambda e: e.dma_start(
                        out=dbg_out["hid_dbg"], in_=hid_hbm),
                        wait=p.w("dS"), inc=[("dS", 16)])
                p.run()

            p = Phase(nc, sems, S)
            with ExitStack() as cb:
                w3f = [sbuf(cb, f"pbw3f{i}", [128, ND * 128], F32) for i in range(2)]
                w3q = [sbuf(cb, f"pbw3q{i}", [128, ND * 128], BF16) for i in range(2)]
                dtmp = [sbuf(cb, f"pbdt{i}", [128, T], F32) for i in range(2)]
                xr = sbuf(cb, "pbxr", [128, T], F32)
                fin = sbuf(cb, "pbfin", [128, T], F32)
                otmp = sbuf(cb, "pbot", [128, T], F32)
                slabs = [(0, 16), (2048, 16), (4096, 12)]
                acc_done = {}
                ev_mark = {}
                mm_marks = {}
                fin_marks = {}
                tr_marks = {}
                ot_marks = {}
                gi = 0
                units = []
                for si, (r0, nk) in enumerate(slabs):
                    for mo in range(ND):
                        units.append((si, r0, nk, mo))
                q_marks = {}
                band_mark = {}

                def emit_band10(u):
                    si, r0, nk, mo = units[u]
                    par = (u + 1) % 2
                    p.emit("gpsimd", lambda e, r0=r0, nk=nk, mo=mo, par=par: e.dma_start(
                        out=w3f[par][:, 0:nk * 128].rearrange("p (k c) -> p k c", c=128),
                        in_=w3T[r0:r0 + nk * 128, 128 * mo:128 * (mo + 1)].rearrange(
                            "(k p) c -> p k c", p=128)),
                        wait=(p.at("vS", q_marks[u - 2]) if u >= 2 else ()),
                        inc=[(f"dW{par}", 16)])
                    return S[f"dW{par}"]

                def emit_quant10(u):
                    si, r0, nk, mo = units[u]
                    par = (u + 1) % 2
                    p.emit("vector", lambda e, nk=nk, par=par: e.tensor_scalar(
                        w3q[par][:, 0:nk * 128], w3f[par][:, 0:nk * 128],
                        thr[:, 4:5], 1.0, ALU.is_ge, ALU.subtract),
                        wait=p.at(f"dW{par}", band_mark[u]) +
                        (p.at("tS", mm_marks[u - 2]) if u >= 2 else []))
                    p.emit("vector", lambda e, nk=nk, par=par: e.scalar_tensor_tensor(
                        w3q[par][:, 0:nk * 128], w3f[par][:, 0:nk * 128],
                        thr[:, 5:6], w3q[par][:, 0:nk * 128], ALU.is_gt, ALU.add),
                        inc=[("vS", 1)])
                    q_marks[u] = S["vS"]

                cur_slab = -1
                band_mark[0] = emit_band10(0)
                band_mark[1] = emit_band10(1)
                emit_quant10(0)
                emit_quant10(1)
                with ExitStack() as cb1:
                    psD = [psum(cb1, f"pbd{i}", [128, T]) for i in range(2)]
                    for u, (si, r0, nk, mo) in enumerate(units):
                        last = (si == len(slabs) - 1)
                        gi += 1
                        par = gi % 2
                        if si != cur_slab:
                            cur_slab = si
                            p.emit("sync", None, wait=p.w("tS"))
                            for k in range(nk):
                                p.emit("sync", lambda e, r0=r0, k=k: e.dma_start(
                                    out=arena[:, T * k:T * (k + 1)],
                                    in_=hid_hbm[r0 + 128 * k:r0 + 128 * (k + 1), :]),
                                    inc=[("dL", 16)])
                            ds_mark = S["dL"]
                        wv = p.at("dL", ds_mark) + p.at("vS", q_marks[u])
                        if last:
                            wv = wv + p.at("vS", max(fin_marks.get(gi - 1, 0),
                                                     ev_mark.get(31, 0), ev_mark.get(32, 0)))
                        elif gi > 2:
                            wv = wv + p.at("vS", ev_mark.get(gi - 2, 0))
                        pd = psD[par if not last else 0]
                        for k in range(nk):
                            for c in range(4):
                                p.emit("tensor", lambda e, k=k, c=c, pd=pd, nk=nk, par=par: e.matmul(
                                    pd[:, 512 * c:512 * (c + 1)],
                                    w3q[par][:, 128 * k:128 * (k + 1)],
                                    arena[:, T * k + 512 * c:T * k + 512 * (c + 1)],
                                    start=(k == 0), stop=(k == nk - 1)),
                                    wait=(wv if (k == 0 and c == 0) else ()),
                                    inc=([("tS", 1)] if (k == nk - 1 and c == 3) else ()))
                        mm_mark = S["tS"]
                        mm_marks[u] = mm_mark
                        if u + 2 < len(units):
                            band_mark[u + 2] = emit_band10(u + 2)
                            emit_quant10(u + 2)
                        if not last:
                            p.emit("vector", lambda e, par=par, pd=pd: e.tensor_copy(
                                dtmp[par][:, :], pd[:, :]),
                                wait=p.at("tS", mm_mark) +
                                (p.at(f"dA{par}", acc_done[gi - 2]) if gi > 2 else []),
                                inc=[("vS", 1)])
                            ev_mark[gi] = S["vS"]
                            p.emit("gpsimd", lambda e, mo=mo, par=par: e.dma_start(
                                out=xh[128 * mo:128 * (mo + 1), :], in_=dtmp[par][:, :],
                                accum_op=ALU.add),
                                wait=p.w("vS"), inc=[(f"dA{par}", 16)])
                            acc_done[gi] = S[f"dA{par}"]
                        else:
                            p.emit("sync", lambda e, mo=mo: e.dma_start(
                                out=xr[:, :], in_=xh[128 * mo:128 * (mo + 1), :]),
                                wait=p.at(f"dA{(mo + 17) % 2}", acc_done[16 + mo + 1]) + p.w("vS"),
                                inc=[("dS", 16)])
                            p.emit("vector", lambda e, pd0=psD[0]: e.tensor_tensor(
                                fin[:, :], pd0[:, :], xr[:, :], ALU.add),
                                wait=p.at("tS", mm_mark) + p.w("dS") +
                                (p.at("tS", tr_marks.get(gi - 1, 0))), inc=[("vS", 1)])
                            fmark = S["vS"]
                            fin_marks[gi] = fmark
                            for tb in range(NT):
                                p.emit("tensor", lambda e, tb=tb, pF=psD[1]: e.transpose(
                                    pF[:, 128 * tb:128 * (tb + 1)],
                                    fin[:, 128 * tb:128 * (tb + 1)], ident_f[:, :]),
                                    wait=(p.at("vS", fmark) +
                                          p.at("vS", ot_marks.get(gi - 1, 0))
                                          if tb == 0 else ()),
                                    inc=([("tS", 1)] if tb == NT - 1 else ()))
                            tr_marks[gi] = S["tS"]
                            p.emit("vector", lambda e, pF=psD[1]: e.tensor_copy(
                                otmp[:, :], pF[:, :]),
                                wait=p.at("tS", tr_marks[gi]) + (p.w("dS") if mo > 0 else []),
                                inc=[("vS", 1)])
                            ot_marks[gi] = S["vS"]
                            p.emit("sync", lambda e, mo=mo: e.dma_start(
                                out=o_out[:, 128 * mo:128 * (mo + 1)].rearrange(
                                    "(tb p) n -> p tb n", p=128),
                                in_=otmp[:, :].rearrange("p (tb n) -> p tb n", n=128)),
                                wait=p.w("vS"), inc=[("dS", 16)])
                p.run()

    return nc



def _host_prep(inputs):
    f32 = np.float32
    asf = lambda a: np.ascontiguousarray(np.asarray(a), dtype=f32)

    _ = inputs["L1_attn_S_n"]
    w_qb = np.asarray(inputs["L0_attn_w_qb"])
    perm_q = []
    for h in range(H):
        base = h * 192
        perm_q += list(range(base, base + 128))
        perm_q += list(range(base + 128, base + 192))
        perm_q += [base + 128 + ((j + 32) % 64) for j in range(64)]
    wqbT = asf(w_qb[perm_q].T)

    w_kva = np.asarray(inputs["L0_attn_w_kva"])
    perm_kv = list(range(512)) + list(range(512, 576)) + \
        [512 + ((j + 32) % 64) for j in range(64)]
    wkvaT = asf(w_kva[perm_kv].T)

    inv = 1.0 / (10000.0 ** (np.arange(0, 64, 2, dtype=np.float64) / 64))
    frq = np.outer(np.arange(T, dtype=np.float64), inv)
    emb = np.concatenate([frq, frq], axis=1)
    cosT = np.cos(emb).T
    sinT = np.sin(emb).T
    sgn = np.where(np.arange(64) < 32, -1.0, 1.0)[:, None]
    cs_qk = asf(np.concatenate([cosT, sgn * sinT], axis=0))

    f1 = np.arange(T, dtype=np.float64)[:, None] * inv[None, :]
    c32 = asf(np.cos(f1))
    s32 = asf(np.sin(f1))

    mask = np.where(np.arange(128)[None, :] <= np.arange(128)[:, None], 0.0, NEG)

    return {
        "wqaT": asf(np.asarray(inputs["L0_attn_w_qa"]).T),
        "lnq": asf(np.asarray(inputs["L0_attn_q_a_ln_w"]).reshape(NQ, 128).T),
        "wqbT": wqbT,
        "wkvaT": wkvaT,
        "lnkv": asf(np.asarray(inputs["L0_attn_kv_a_ln_w"]).reshape(NKV, 128).T),
        "wkvbT": asf(np.asarray(inputs["L0_attn_w_kvb"]).T),
        "woT": asf(np.asarray(inputs["L0_attn_w_o"]).T),
        "rwT": asf(np.asarray(inputs["L0_ffn_router_weight"]).T),
        "wgT": asf(np.asarray(inputs["L0_ffn_sh_gate"]).T),
        "wuT": asf(np.asarray(inputs["L0_ffn_sh_up"]).T),
        "wrgT": asf(np.asarray(inputs["L0_ffn_r_gate"]).T),
        "wruT": asf(np.asarray(inputs["L0_ffn_r_up"]).T),
        "wdsT": asf(np.concatenate([np.asarray(inputs["L0_ffn_sh_down"]).T,
                                    np.asarray(inputs["L0_ffn_r_down"]).T], axis=0)),
        "WQm": asf(np.asarray(inputs["L1_attn_W_Q"])),
        "hhb": asf(np.broadcast_to(np.asarray(inputs["L1_attn_hh_b"]), (128, D))),
        "gnw": asf(np.broadcast_to(np.asarray(inputs["L1_attn_gn_weight"]), (128, D))),
        "gnb": asf(np.broadcast_to(np.asarray(inputs["L1_attn_gn_bias"]), (128, D))),
        "w1T": asf(np.asarray(inputs["L1_ffn_w1"]).T),
        "w2T": asf(np.asarray(inputs["L1_ffn_w2"]).T),
        "w3T": asf(np.asarray(inputs["L1_ffn_w3"]).T),
        "identf": np.eye(128, dtype=f32),
        "identb": np.eye(128).astype(ml_dtypes.bfloat16),
        "onescol": np.ones((128, 1), dtype=f32),
        "onesrow": np.ones((1, 128), dtype=f32),
        "maskd": asf(mask),
        "cs_qk": cs_qk,
        "c32": c32,
        "s32": s32,
    }


def kernel(**inputs):
    common = _host_prep(inputs)
    x = np.asarray(inputs["L0_attn_hidden_states"], dtype=np.float32)
    nc = build_program()
    in_maps = []
    for i in range(8):
        m = dict(common)
        m["x"] = np.ascontiguousarray(x[i])
        in_maps.append(m)
    res = run_bass_kernel_spmd(nc, in_maps, core_ids=list(range(8)))
    return np.stack([res.results[i]["o"] for i in range(8)], axis=0)


if __name__ == "__main__":
    import time
    t0 = time.time()
    nc = build_program()
    print(f"built ok in {time.time() - t0:.1f}s")


# version 09 (final)
# speedup vs baseline: 1.5682x; 1.0872x over previous
import numpy as np
import ml_dtypes
import concourse.bass as bass
import concourse.mybir as mybir
from contextlib import ExitStack
from concourse.bass_utils import run_bass_kernel_spmd

F32 = mybir.dt.float32
BF16 = mybir.dt.bfloat16
AF = mybir.ActivationFunctionType
ALU = mybir.AluOpType
AX = mybir.AxisListType

D = 2048
T = 2048
NT = T // 128
ND = D // 128
QLR, NQ = 1536, 12
KVLR, NKV = 512, 4
H = 16
DQ = 192
INTER = 8192
MOE = 1024
SDIM = INTER + MOE
HID1 = 5632
EPS_RMS = 1e-6
EPS_LN = 1e-5
EPS_GN = 1e-5
EPS_BIT = 1e-8
NEG = -1e30
QSC = float(DQ) ** -0.5

SEMS = ("dS", "dG", "tS", "vS", "cS", "vq", "cq", "gq", "dA0", "dA1", "dW0", "dW1", "dL", "dH0", "dH1", "dP0", "dP1")


class Phase:

    def __init__(self, nc, sems, S):
        self.nc, self.sems, self.S = nc, sems, S
        self.q = {k: [] for k in ("sync", "gpsimd", "tensor", "vector", "scalar")}

    CHAIN = {"vector": "vS", "scalar": "cS", "gpsimd": "gq"}

    def emit(self, eng, fn, wait=(), inc=()):
        wait = list(wait)
        inc = list(inc)
        ch = self.CHAIN.get(eng)
        is_dma = any(c.startswith(("dS", "dG", "dA", "dW")) and k >= 16 for c, k in inc)
        if ch is not None and fn is not None and not is_dma:
            inc = [(c, k) for c, k in inc if c != ch]
            wait.append((ch, self.S[ch]))
            inc.append((ch, 1))
            assert len(inc) == 1, f"multi-inc on {eng}: {inc}"
        for c, k in inc:
            self.S[c] += k
        self.q[eng].append((wait, fn, inc))

    def w(self, *ctrs):
        return [(c, self.S[c]) for c in ctrs]

    def at(self, ctr, val):
        return [(ctr, val)]

    def run(self):
        self.emit("sync", None, wait=self.w("dS", "dL", "dH0", "dH1", "dP0", "dP1"))
        self.emit("gpsimd", None, wait=self.w("dG", "dA0", "dA1", "dW0", "dW1"))
        nc = self.nc
        with nc.Block() as block:
            def mk(name):
                ops = self.q[name]

                def body(eng):
                    for waits, fn, incs in ops:
                        for c, v in waits:
                            if v > 0:
                                eng.wait_ge(self.sems[c], v)
                        if fn is None:
                            continue
                        ins = fn(eng)
                        for c, k in incs:
                            ins.then_inc(self.sems[c], k)
                return body

            block.sync(mk("sync"))
            block.gpsimd(mk("gpsimd"))
            block.tensor(mk("tensor"))
            block.vector(mk("vector"))
            block.scalar(mk("scalar"))
        nc.all_engine_barrier()


def ncj(j):
    return (128 * (j + 1) + 511) // 512


def build_program(upto=99, dbg=()):
    nc = bass.Bass("TRN2", target_bir_lowering=False)
    dbg = set(dbg)

    def din(name, shape, dt=F32):
        return nc.dram_tensor(name, shape, dt, kind="ExternalInput").ap()

    x_in = din("x", [T, D])
    wqaT = din("wqaT", [D, QLR])
    lnq_in = din("lnq", [128, NQ])
    wqbT = din("wqbT", [QLR, H * 256])
    wkvaT = din("wkvaT", [D, 640])
    lnkv_in = din("lnkv", [128, NKV])
    wkvbT = din("wkvbT", [KVLR, H * 256])
    woT = din("woT", [D, D])
    rwT = din("rwT", [D, 8])
    wgT = din("wgT", [D, INTER])
    wuT = din("wuT", [D, INTER])
    wrgT = din("wrgT", [D, MOE])
    wruT = din("wruT", [D, MOE])
    wdsT = din("wdsT", [SDIM, D])
    WQm = din("WQm", [D, D])
    hhb_in = din("hhb", [128, D])
    gnw_in = din("gnw", [128, D])
    gnb_in = din("gnb", [128, D])
    w1T = din("w1T", [D, HID1])
    w2T = din("w2T", [D, HID1])
    w3T = din("w3T", [HID1, D])
    identf = din("identf", [128, 128])
    onescol = din("onescol", [128, 1])
    onesrow = din("onesrow", [1, 128])
    maskd = din("maskd", [128, 128])
    cs_qk_in = din("cs_qk", [128, T])
    c32_in = din("c32", [T, 32])
    s32_in = din("s32", [T, 32])
    identb_in = din("identb", [128, 128], BF16)

    o_out = nc.dram_tensor("o", [T, D], F32, kind="ExternalOutput").ap()
    dbg_out = {}
    for name, shape in [
        ("xh_dbg", [D, T]), ("xn_dbg", [D, T]), ("cqn_dbg", [QLR, T]),
        ("ckv_dbg", [KVLR, T]), ("kpe_dbg", [64, T]), ("cat_dbg", [D, T]),
        ("comb_dbg", [128, NT]), ("thr_dbg", [128, 6]), ("xn1_dbg", [D, T]),
    ]:
        if name in dbg:
            dbg_out[name] = nc.dram_tensor(name, shape, F32, kind="ExternalOutput").ap()
    for name, shape in [("s_dbg", [SDIM, T]), ("hid_dbg", [HID1, T])]:
        if name in dbg:
            dbg_out[name] = nc.dram_tensor(name, shape, BF16, kind="ExternalOutput").ap()

    xh = nc.dram_tensor("xh", [D, T], F32, kind="Internal").ap()
    s_hbm = nc.dram_tensor("s_hbm", [SDIM, T], BF16, kind="Internal").ap()
    hid_hbm = nc.dram_tensor("hid_hbm", [HID1, T], BF16, kind="Internal").ap()
    bounce = nc.dram_tensor("bounce", [16, 128], F32, kind="Internal").ap()

    S = {k: 0 for k in SEMS}

    with ExitStack() as top:
        sems = {k: top.enter_context(nc.semaphore(k)) for k in SEMS}

        def sbuf(stack, name, shape, dt):
            return stack.enter_context(nc.sbuf_tensor(name, shape, dt))

        def psum(stack, name, shape, dt=F32):
            return stack.enter_context(nc.psum_tensor(name, shape, dt))

        arena = sbuf(top, "arena", [128, NT * T], BF16)
        ident_f = sbuf(top, "ident_f", [128, 128], F32)
        ident_b = sbuf(top, "ident_b", [128, 128], BF16)
        ones_c = sbuf(top, "ones_c", [128, 1], F32)
        ones_r = sbuf(top, "ones_r", [1, 128], F32)
        thr = sbuf(top, "thr", [128, 6], F32)
        cst = sbuf(top, "cst", [128, 4], F32)

        def arena_t(i):
            return arena[:, T * i:T * (i + 1)]

        def arena3(i):
            return arena[:, :].rearrange("p (c t) -> p c t", t=T)[:, :, 128 * i:128 * (i + 1)]

        def dbg_dump_fm(ph, name, src_fn, ntiles):
            if name not in dbg_out:
                return
            dst = dbg_out[name]
            for i in range(ntiles):
                ph.emit("gpsimd", lambda e, i=i: e.dma_start(
                    out=dst[128 * i:128 * (i + 1), :], in_=src_fn(i)),
                    wait=ph.w("vS", "tS", "cS"), inc=[("dG", 16)])

        p = Phase(nc, sems, S)
        for dst, src in [(ident_f, identf), (ident_b, identb_in),
                         (ones_c, onescol), (ones_r, onesrow)]:
            p.emit("sync", lambda e, d=dst, s=src: e.dma_start(out=d[:, :], in_=s),
                   inc=[("dS", 16)])
        for col, val in [(0, EPS_LN), (1, EPS_RMS), (2, EPS_GN)]:
            p.emit("vector", lambda e, col=col, val=val: e.memset(cst[:, col:col + 1], val),
                   inc=[("vS", 1)])
        p.run()

        if upto >= 1:
            p = Phase(nc, sems, S)
            with ExitStack() as c1:
                xt = [sbuf(c1, f"p1x{i}", [128, D], F32) for i in range(2)]
                xsq = sbuf(c1, "p1sq", [128, D], F32)
                st = sbuf(c1, "p1st", [128, 8], F32)
                xn_tm = sbuf(c1, "p1xn", [128, D], BF16)
                xh_str = sbuf(c1, "p1xh", [128, D], F32)
                wb = [sbuf(c1, f"p0w{i}", [128, HID1], F32) for i in range(2)]
                trash = sbuf(c1, "p0tr", [128, HID1], F32)
                racc3 = sbuf(c1, "p0acc", [128, 80], F32)
                rtmp = sbuf(c1, "p0tmp", [128, 1], F32)
                sc1 = sbuf(c1, "p0s1", [1, 1], F32)
                ps_b = psum(c1, "p1psb", [128, D], BF16)
                ps_f = psum(c1, "p1psf", [128, D])
                ps1 = psum(c1, "p0ps", [1, 1])
                psr = psum(c1, "p0psr", [128, 1])
                strips = [(w1T, i, HID1, i) for i in range(16)]
                strips += [(w2T, i, HID1, 16 + i) for i in range(16)]
                strips += [(w3T, i, D, 32 + i) for i in range(44)]
                abs_marks = {}

                def emit_strip(s):
                    wt, i, cols, col = strips[s]
                    spar = s % 2
                    p.emit("sync", lambda e, wt=wt, i=i, cols=cols, spar=spar: e.dma_start(
                        out=wb[spar][:, 0:cols], in_=wt[128 * i:128 * (i + 1), :]),
                        wait=(p.at("cS", abs_marks[s - 2]) if s >= 2 else ()),
                        inc=[(f"dP{spar}", 16)])
                    dlm = S[f"dP{spar}"]
                    p.emit("scalar", lambda e, cols=cols, col=col, spar=spar: e.activation(
                        trash[:, 0:cols], wb[spar][:, 0:cols], AF.Abs,
                        accum_out=racc3[:, col:col + 1]),
                        wait=p.at(f"dP{spar}", dlm), inc=[("cS", 1)])
                    abs_marks[s] = S["cS"]

                marks = {}
                for i in range(NT):
                    for s in range(5 * i, min(5 * (i + 1), len(strips))):
                        emit_strip(s)
                    par = i % 2
                    xv = xt[par]
                    wv = (p.at("tS", marks[i - 2][0]) + p.at("vS", marks[i - 2][1])
                          if i >= 2 else ())
                    p.emit("sync", lambda e, i=i, xv=xv: e.dma_start(
                        out=xv[:, :], in_=x_in[128 * i:128 * (i + 1), :]),
                        wait=wv, inc=[(f"dH{par}", 16)])
                    ld_mark = S[f"dH{par}"]
                    p.emit("scalar", lambda e, xv=xv: e.square(xsq[:, :], xv[:, :]),
                           wait=p.at(f"dH{par}", ld_mark) + p.w("vS"), inc=[("cS", 1)])
                    p.emit("vector", lambda e, xv=xv: e.reduce_sum(
                        st[:, 0:1], xv[:, :], AX.X), wait=p.at(f"dH{par}", ld_mark))
                    p.emit("vector", lambda e: e.reduce_sum(st[:, 1:2], xsq[:, :], AX.X),
                           wait=p.w("cS"))
                    p.emit("vector", lambda e: e.tensor_scalar(
                        st[:, 0:1], st[:, 0:1], 1.0 / D, None, ALU.mult))
                    p.emit("vector", lambda e: e.tensor_tensor(
                        st[:, 2:3], st[:, 0:1], st[:, 0:1], ALU.mult))
                    p.emit("vector", lambda e: e.tensor_scalar(
                        st[:, 3:4], st[:, 1:2], 1.0 / D, None, ALU.mult))
                    p.emit("vector", lambda e: e.tensor_tensor(
                        st[:, 4:5], st[:, 3:4], st[:, 2:3], ALU.subtract), inc=[("vS", 1)])
                    p.emit("scalar", lambda e: e.activation(
                        st[:, 5:6], st[:, 4:5], AF.Sqrt, bias=cst[:, 0:1]),
                        wait=p.w("vS"), inc=[("cS", 1)])
                    p.emit("vector", lambda e: e.reciprocal(st[:, 6:7], st[:, 5:6]),
                           wait=p.w("cS"))
                    p.emit("vector", lambda e, xv=xv: e.tensor_scalar(
                        xn_tm[:, :], xv[:, :], st[:, 0:1], st[:, 6:7],
                        ALU.subtract, ALU.mult), inc=[("vS", 1)])
                    xn_mark = S["vS"]
                    for c in range(ND):
                        p.emit("tensor", lambda e, c=c: e.transpose(
                            ps_b[:, 128 * c:128 * (c + 1)], xn_tm[:, 128 * c:128 * (c + 1)],
                            ident_b[:, :]),
                            wait=(p.at("vS", xn_mark) if c == 0 else ()),
                            inc=([("tS", 1)] if c == ND - 1 else ()))
                    bT = S["tS"]
                    for c in range(ND):
                        p.emit("tensor", lambda e, c=c, xv=xv: e.transpose(
                            ps_f[:, 128 * c:128 * (c + 1)], xv[:, 128 * c:128 * (c + 1)],
                            ident_f[:, :]),
                            inc=([("tS", 1)] if c == ND - 1 else ()))
                    fT = S["tS"]
                    p.emit("vector", lambda e, i=i: e.tensor_copy(
                        arena3(i), ps_b[:, :].rearrange("p (c n) -> p c n", n=128)),
                        wait=p.at("tS", bT), inc=[("vS", 1)])
                    p.emit("vector", lambda e: e.tensor_copy(xh_str[:, :], ps_f[:, :]),
                           wait=p.at("tS", fT) + (p.w("dS") if i > 0 else []),
                           inc=[("vS", 1)])
                    fEv = S["vS"]
                    p.emit("sync", lambda e, i=i: e.dma_start(
                        out=xh[:, 128 * i:128 * (i + 1)].rearrange("(c p) n -> p c n", p=128),
                        in_=xh_str[:, :].rearrange("p (c n) -> p c n", n=128)),
                        wait=p.at("vS", fEv), inc=[("dS", 16)])
                    marks[i] = (fT, fEv)
                for s in range(5 * NT, len(strips)):
                    emit_strip(s)
                for mi, (a, b, nelem) in enumerate([(0, 16, D * HID1), (16, 32, D * HID1),
                                                    (32, 76, HID1 * D)]):
                    p.emit("vector", lambda e, a=a, b=b: e.tensor_reduce(
                        rtmp[:, :], racc3[:, a:b], AX.X, ALU.add),
                        wait=p.at("cS", abs_marks[len(strips) - 1]))
                    p.emit("vector", lambda e: e.engine_nop(), inc=[("vS", 1)])
                    p.emit("tensor", lambda e: e.matmul(
                        ps1[:, :], ones_c[:, :], rtmp[:, :], start=True, stop=True),
                        wait=p.w("vS"), inc=[("tS", 1)])
                    p.emit("vector", lambda e, nelem=nelem: e.tensor_scalar(
                        sc1[:, :], ps1[:, :], 0.5 / nelem, 0.5 * EPS_BIT, ALU.mult, ALU.add),
                        wait=p.w("tS"), inc=[("vS", 1)])
                    p.emit("tensor", lambda e: e.matmul(
                        psr[:, :], ones_r[:, :], sc1[:, :], start=True, stop=True),
                        wait=p.w("vS"), inc=[("tS", 1)])
                    p.emit("vector", lambda e, mi=mi: e.tensor_copy(
                        thr[:, 2 * mi:2 * mi + 1], psr[:, :]), wait=p.w("tS"))
                    p.emit("vector", lambda e, mi=mi: e.tensor_scalar(
                        thr[:, 2 * mi + 1:2 * mi + 2], psr[:, :], -1.0, None, ALU.mult),
                        inc=[("vS", 1)])
                if "thr_dbg" in dbg_out:
                    p.emit("sync", lambda e: e.dma_start(
                        out=dbg_out["thr_dbg"], in_=thr[:, :]),
                        wait=p.w("vS"), inc=[("dS", 16)])
                if "xh_dbg" in dbg_out and upto == 1:
                    p.emit("sync", lambda e: e.dma_start(out=dbg_out["xh_dbg"], in_=xh),
                           wait=p.w("dS"), inc=[("dS", 16)])
                dbg_dump_fm(p, "xn_dbg", arena_t, ND)
                p.run()

        if upto >= 2:
            p = Phase(nc, sems, S)
            with ExitStack() as c2:
                comb_bc = sbuf(c2, "comb_bc", [128, T], F32)
                with ExitStack() as cr:
                    rw_sb = sbuf(cr, "rw_sb", [128, NT * 8], BF16)
                    t8 = sbuf(cr, "p2t8", [128, 8], F32)
                    cb = sbuf(cr, "p2cb", [128, 8], F32)
                    combc = sbuf(cr, "p2cc", [128, NT], F32)
                    ct16 = sbuf(cr, "p2ct", [16, 128], F32)
                    crow = sbuf(cr, "p2crow", [1, T], F32)
                    ps_r = psum(cr, "p2psr", [128, 8])
                    ps_t = psum(cr, "p2pst", [16, 128])
                    ps_c = psum(cr, "p2psc", [128, T])
                    for k in range(ND):
                        p.emit("gpsimd", lambda e, k=k: e.dma_start(
                            out=rw_sb[:, 8 * k:8 * (k + 1)], in_=rwT[128 * k:128 * (k + 1), :]),
                            inc=[("dG", 16)])
                    for i in range(NT):
                        for k in range(ND):
                            p.emit("tensor", lambda e, i=i, k=k: e.matmul(
                                ps_r[:, :],
                                arena_t(k)[:, 128 * i:128 * (i + 1)],
                                rw_sb[:, 8 * k:8 * (k + 1)],
                                start=(k == 0), stop=(k == ND - 1)),
                                wait=(p.w("dG", "vS") if k == 0 else ()),
                                inc=([("tS", 1)] if k == ND - 1 else ()))
                        p.emit("vector", lambda e: e.reduce_max(cb[:, 0:1], ps_r[:, :], AX.X),
                               wait=p.w("tS"))
                        p.emit("vector", lambda e: e.tensor_scalar(
                            t8[:, :], ps_r[:, :], cb[:, 0:1], NEG, ALU.is_ge, ALU.mult))
                        p.emit("vector", lambda e: e.tensor_tensor(
                            t8[:, :], t8[:, :], ps_r[:, :], ALU.add))
                        p.emit("vector", lambda e: e.reduce_max(cb[:, 1:2], t8[:, :], AX.X))
                        p.emit("vector", lambda e: e.tensor_tensor(
                            cb[:, 2:3], cb[:, 1:2], cb[:, 0:1], ALU.subtract), inc=[("vS", 1)])
                        p.emit("scalar", lambda e: e.activation(
                            cb[:, 3:4], cb[:, 2:3], AF.Exp), wait=p.w("vS"), inc=[("cS", 1)])
                        p.emit("vector", lambda e: e.tensor_scalar(
                            cb[:, 4:5], cb[:, 3:4], 1.0, None, ALU.add), wait=p.w("cS"))
                        p.emit("vector", lambda e: e.reciprocal(cb[:, 5:6], cb[:, 4:5]))
                        p.emit("vector", lambda e, i=i: e.tensor_tensor(
                            combc[:, i:i + 1], cb[:, 4:5], cb[:, 5:6], ALU.mult),
                            inc=[("vS", 1)])
                    p.emit("tensor", lambda e: e.transpose(
                        ps_t[:, :], combc[:, :], ident_f[:, :]),
                        wait=p.w("vS"), inc=[("tS", 1)])
                    p.emit("vector", lambda e: e.tensor_copy(ct16[:, :], ps_t[:, :]),
                           wait=p.w("tS"), inc=[("vS", 1)])
                    p.emit("sync", lambda e: e.dma_start(out=bounce, in_=ct16[:, :]),
                           wait=p.w("vS"), inc=[("dS", 16)])
                    p.emit("sync", lambda e: e.dma_start(
                        out=crow[:, :], in_=bounce.rearrange("a b -> (a b)").rearrange("(o n) -> o n", o=1)),
                        wait=p.w("dS"), inc=[("dS", 16)])
                    for c in range(4):
                        p.emit("tensor", lambda e, c=c: e.matmul(
                            ps_c[:, 512 * c:512 * (c + 1)], ones_r[:, :],
                            crow[:, 512 * c:512 * (c + 1)], start=True, stop=True),
                            wait=(p.w("dS") if c == 0 else ()),
                            inc=([("tS", 1)] if c == 3 else ()))
                    p.emit("vector", lambda e: e.tensor_copy(comb_bc[:, :], ps_c[:, :]),
                           wait=p.w("tS"), inc=[("vS", 1)])
                    if "comb_dbg" in dbg_out:
                        p.emit("sync", lambda e: e.dma_start(
                            out=dbg_out["comb_dbg"], in_=combc[:, :]),
                            wait=p.w("vS"), inc=[("dS", 16)])
                wg2 = [sbuf(c2, f"wg2_{i}", [128, ND * 512], BF16) for i in range(2)]
                wu2 = [sbuf(c2, f"wu2_{i}", [128, ND * 512], BF16) for i in range(2)]
                t1 = [sbuf(c2, f"p2t1{i}", [128, 1024], BF16) for i in range(2)]
                s_str = [sbuf(c2, f"p2s{i}", [128, T], BF16) for i in range(2)]
                psG = [psum(c2, f"p2g{i}", [128, 1024]) for i in range(2)]
                psU = [psum(c2, f"p2u{i}", [128, 1024]) for i in range(2)]
                ogs = [(wgT, wuT, 512 * og, 512 * og, False) for og in range(16)]
                ogs += [(wrgT, wruT, 512 * r, INTER + 512 * r, True) for r in range(2)]
                gi = 0
                ev_mark = {}
                og_mm_marks = {}
                for og, (gsrc, usrc, col0, srow0, routed) in enumerate(ogs):
                    wpar = og % 2
                    lw = p.w("tS", "vS", "cS", "dS") if og == 0 else (
                        p.at("tS", og_mm_marks[og - 2]) if og >= 2 else ())
                    p.emit("gpsimd", lambda e, gsrc=gsrc, col0=col0, wpar=wpar: e.dma_start(
                        out=wg2[wpar][:, :].rearrange("p (k c) -> p k c", c=512),
                        in_=gsrc[:, col0:col0 + 512].rearrange("(k p) c -> p k c", p=128)),
                        wait=lw, inc=[(f"dW{wpar}", 16)])
                    p.emit("gpsimd", lambda e, usrc=usrc, col0=col0, wpar=wpar: e.dma_start(
                        out=wu2[wpar][:, :].rearrange("p (k c) -> p k c", c=512),
                        in_=usrc[:, col0:col0 + 512].rearrange("(k p) c -> p k c", p=128)),
                        inc=[(f"dW{wpar}", 16)])
                    dg_mark = S[f"dW{wpar}"]
                    for m in range(4):
                        s_par = m % 2
                        for hf in range(2):
                            par = gi % 2
                            gi += 1
                            wv = p.at(f"dW{wpar}", dg_mark)
                            if gi > 2:
                                wv = wv + p.at("vS", ev_mark[gi - 2])
                            for k in range(ND):
                                for c in range(2):
                                    p.emit("tensor", lambda e, k=k, c=c, m=m, hf=hf, par=par, wpar=wpar: e.matmul(
                                        psG[par][:, 512 * c:512 * (c + 1)],
                                        wg2[wpar][:, 512 * k + 128 * m:512 * k + 128 * (m + 1)],
                                        arena_t(k)[:, 1024 * hf + 512 * c:1024 * hf + 512 * (c + 1)],
                                        start=(k == 0), stop=(k == ND - 1)),
                                        wait=(wv if (k == 0 and c == 0) else ()),
                                        inc=([("tS", 1)] if (k == ND - 1 and c == 1) else ()))
                            g_mark = S["tS"]
                            for k in range(ND):
                                for c in range(2):
                                    p.emit("tensor", lambda e, k=k, c=c, m=m, hf=hf, par=par, wpar=wpar: e.matmul(
                                        psU[par][:, 512 * c:512 * (c + 1)],
                                        wu2[wpar][:, 512 * k + 128 * m:512 * k + 128 * (m + 1)],
                                        arena_t(k)[:, 1024 * hf + 512 * c:1024 * hf + 512 * (c + 1)],
                                        start=(k == 0), stop=(k == ND - 1)),
                                        inc=([("tS", 1)] if (k == ND - 1 and c == 1) else ()))
                            u_mark = S["tS"]
                            p.emit("scalar", lambda e, par=par: e.activation(
                                t1[par][:, :], psG[par][:, :], AF.Silu),
                                wait=p.at("tS", g_mark), inc=[("cS", 1)])
                            if routed:
                                p.emit("vector", lambda e, par=par, hf=hf: e.tensor_tensor(
                                    t1[par][:, :], t1[par][:, :],
                                    comb_bc[:, 1024 * hf:1024 * (hf + 1)], ALU.mult),
                                    wait=p.w("cS"))
                            p.emit("vector", lambda e, par=par, s_par=s_par, hf=hf: e.tensor_tensor(
                                s_str[s_par][:, 1024 * hf:1024 * (hf + 1)],
                                t1[par][:, :], psU[par][:, :], ALU.mult),
                                wait=p.at("tS", u_mark) + p.w("cS") +
                                (p.w("dS") if (hf == 0 and gi > 4) else []),
                                inc=[("vS", 1)])
                            ev_mark[gi] = S["vS"]
                        p.emit("sync", lambda e, srow0=srow0, m=m, s_par=s_par: e.dma_start(
                            out=s_hbm[srow0 + 128 * m:srow0 + 128 * (m + 1), :],
                            in_=s_str[s_par][:, :]),
                            wait=p.w("vS"), inc=[("dS", 16)])
                    og_mm_marks[og] = S["tS"]
                if "s_dbg" in dbg_out:
                    p.emit("sync", lambda e: e.dma_start(out=dbg_out["s_dbg"], in_=s_hbm),
                           wait=p.w("dS"), inc=[("dS", 16)])
                p.run()

        if upto >= 3:
            p = Phase(nc, sems, S)
            with ExitStack() as c3:
                ss_sb = sbuf(c3, "ss_sb", [128, ND * T], BF16)
                wd_sb = [sbuf(c3, f"wd{i}", [128, ND * 128], BF16) for i in range(2)]
                dtmp = [sbuf(c3, f"dt{i}", [128, T], F32) for i in range(2)]
                psD = [psum(c3, f"p3d{i}", [128, T]) for i in range(2)]
                slabs = [(0, 16), (2048, 16), (4096, 16), (6144, 16), (8192, 8)]
                acc_done = {}
                ev_mark = {}
                mm_marks = {}
                gi = 0
                units = []
                for si, (r0, nk) in enumerate(slabs):
                    for mo in range(ND):
                        units.append((si, r0, nk, mo))

                def emit_band(u):
                    si, r0, nk, mo = units[u]
                    par = (u + 1) % 2
                    p.emit("gpsimd", lambda e, r0=r0, nk=nk, mo=mo, par=par: e.dma_start(
                        out=wd_sb[par][:, 0:nk * 128].rearrange("p (k c) -> p k c", c=128),
                        in_=wdsT[r0:r0 + nk * 128, 128 * mo:128 * (mo + 1)].rearrange(
                            "(k p) c -> p k c", p=128)),
                        wait=(p.at("tS", mm_marks[u - 2]) if u >= 2 else ()),
                        inc=[(f"dW{par}", 16)])
                    return S[f"dW{par}"]

                band_mark = {}
                band_mark[0] = emit_band(0)
                band_mark[1] = emit_band(1)
                cur_slab = -1
                for u, (si, r0, nk, mo) in enumerate(units):
                    gi += 1
                    par = gi % 2
                    if si != cur_slab:
                        cur_slab = si
                        p.emit("sync", None, wait=p.w("tS"))
                        for k in range(nk):
                            p.emit("sync", lambda e, r0=r0, k=k: e.dma_start(
                                out=ss_sb[:, T * k:T * (k + 1)],
                                in_=s_hbm[r0 + 128 * k:r0 + 128 * (k + 1), :]),
                                inc=[("dS", 16)])
                        ds_mark = S["dS"]
                    wv = p.at("dS", ds_mark) + p.at(f"dW{par}", band_mark[u])
                    if gi > 2:
                        wv = wv + p.at("vS", ev_mark[gi - 2])
                    for k in range(nk):
                        for c in range(4):
                            p.emit("tensor", lambda e, k=k, c=c, par=par, nk=nk: e.matmul(
                                psD[par][:, 512 * c:512 * (c + 1)],
                                wd_sb[par][:, 128 * k:128 * (k + 1)],
                                ss_sb[:, T * k + 512 * c:T * k + 512 * (c + 1)],
                                start=(k == 0), stop=(k == nk - 1)),
                                wait=(wv if (k == 0 and c == 0) else ()),
                                inc=([("tS", 1)] if (k == nk - 1 and c == 3) else ()))
                    mm_marks[u] = S["tS"]
                    if u + 2 < len(units):
                        band_mark[u + 2] = emit_band(u + 2)
                    p.emit("vector", lambda e, par=par: e.tensor_copy(
                        dtmp[par][:, :], psD[par][:, :]),
                        wait=p.w("tS") + (p.at(f"dA{par}", acc_done[gi - 2]) if gi > 2 else []),
                        inc=[("vS", 1)])
                    ev_mark[gi] = S["vS"]
                    p.emit("gpsimd", lambda e, mo=mo, par=par: e.dma_start(
                        out=xh[128 * mo:128 * (mo + 1), :], in_=dtmp[par][:, :],
                        accum_op=ALU.add),
                        wait=p.w("vS") + (p.at(f"dA{par}", acc_done.get(gi - 16, 0))
                                          if gi > 16 else []),
                        inc=[(f"dA{par}", 16)])
                    acc_done[gi] = S[f"dA{par}"]
                if "xh_dbg" in dbg_out and upto == 3:
                    p.emit("sync", lambda e: e.dma_start(out=dbg_out["xh_dbg"], in_=xh),
                           wait=p.w("dG", "dA0", "dA1"), inc=[("dS", 16)])
                p.run()

        b_scope = ExitStack()
        cqn = sbuf(b_scope, "cqn", [128, NQ * T], BF16)
        ckv = sbuf(b_scope, "ckv", [128, NKV * T], BF16)
        kpe = sbuf(b_scope, "kpe", [64, T], BF16)
        cs_sb = sbuf(b_scope, "cs_sb", [128, T], BF16)

        def cqn_t(i):
            return cqn[:, T * i:T * (i + 1)]

        def ckv_t(i):
            return ckv[:, T * i:T * (i + 1)]

        if upto >= 4:
            p = Phase(nc, sems, S)
            with ExitStack() as c4:
                wa_sb = [sbuf(c4, f"p4w{i}", [128, ND * 128], BF16) for i in range(2)]
                lnq_sb = sbuf(c4, "lnq_sb", [128, NQ], F32)
                lnkv_sb = sbuf(c4, "lnkv_sb", [128, NKV], F32)
                sqf = sbuf(c4, "p4sq", [128, T], F32)
                srow = sbuf(c4, "p4srow", [1, T], F32)
                rstd_bc = sbuf(c4, "p4rstd", [128, T], F32)
                t1f = sbuf(c4, "p4t1", [128, T], F32)
                t64 = sbuf(c4, "p4t64", [64, T], F32)
                psC2 = [psum(c4, f"p4c{i}", [128, 1024]) for i in range(2)]
                psS = psum(c4, "p4s", [1, T])

                p.emit("sync", lambda e: e.dma_start(out=lnq_sb[:, :], in_=lnq_in),
                       inc=[("dS", 16)])
                p.emit("sync", lambda e: e.dma_start(out=lnkv_sb[:, :], in_=lnkv_in),
                       inc=[("dS", 16)])
                p.emit("gpsimd", lambda e: e.dma_start(out=cs_sb[:, :], in_=cs_qk_in),
                       inc=[("dG", 16)])
                pmm_marks = []

                def proj_rms(wsrcT, nm, n_rms, dst_t, ln_sb, scale):
                    ev_marks = {}
                    for m in range(nm):
                        par = m % 2
                        p.emit("gpsimd", lambda e, m=m, par=par, wsrcT=wsrcT: e.dma_start(
                            out=wa_sb[par][:, :].rearrange("p (k c) -> p k c", c=128),
                            in_=wsrcT[:, 128 * m:128 * (m + 1)].rearrange(
                                "(k p) c -> p k c", p=128)),
                            wait=(p.at("tS", pmm_marks[-2]) if len(pmm_marks) >= 2
                                  else ()),
                            inc=[(f"dW{par}", 16)])
                        dwm = S[f"dW{par}"]
                        for h2 in range(2):
                            hpar = (2 * m + h2) % 2
                            wv = p.at(f"dW{par}", dwm) + \
                                p.at("vS", ev_marks.get(2 * m + h2 - 2, 0)) + \
                                p.at("cS", ev_marks.get((2 * m + h2 - 2, "c"), 0))
                            for k in range(ND):
                                for c in range(2):
                                    p.emit("tensor", lambda e, k=k, c=c, h2=h2, par=par,
                                           hpar=hpar: e.matmul(
                                        psC2[hpar][:, 512 * c:512 * (c + 1)],
                                        wa_sb[par][:, 128 * k:128 * (k + 1)],
                                        arena_t(k)[:, 1024 * h2 + 512 * c:1024 * h2 + 512 * (c + 1)],
                                        start=(k == 0), stop=(k == ND - 1)),
                                        wait=(wv if (k == 0 and c == 0) else ()),
                                        inc=([("tS", 1)] if (k == ND - 1 and c == 1) else ()))
                            hm = S["tS"]
                            if m < n_rms:
                                p.emit("vector", lambda e, m=m, h2=h2, hpar=hpar: e.tensor_copy(
                                    dst_t(m)[:, 1024 * h2:1024 * (h2 + 1)], psC2[hpar][:, :]),
                                    wait=p.at("tS", hm), inc=[("vS", 1)])
                                ev_marks[2 * m + h2] = S["vS"]
                                p.emit("scalar", lambda e, m=m, h2=h2: e.square(
                                    sqf[:, 1024 * h2:1024 * (h2 + 1)],
                                    dst_t(m)[:, 1024 * h2:1024 * (h2 + 1)]),
                                    wait=p.w("vS"), inc=[("cS", 1)])
                                ev_marks[(2 * m + h2, "c")] = S["cS"]
                                for c in range(2):
                                    p.emit("tensor", lambda e, c=c, h2=h2, m=m,
                                           n_rms=n_rms: e.matmul(
                                        psS[:, 1024 * h2 + 512 * c:1024 * h2 + 512 * (c + 1)],
                                        ones_c[:, :],
                                        sqf[:, 1024 * h2 + 512 * c:1024 * h2 + 512 * (c + 1)],
                                        start=(m == 0), stop=(m == n_rms - 1)),
                                        wait=(p.w("cS") if c == 0 else ()),
                                        inc=([("tS", 1)] if c == 1 else ()))
                            else:
                                p.emit("vector", lambda e, h2=h2, hpar=hpar: e.tensor_tensor(
                                    t1f[:, 1024 * h2:1024 * (h2 + 1)], psC2[hpar][:, :],
                                    cs_sb[:, 1024 * h2:1024 * (h2 + 1)], ALU.mult),
                                    wait=p.at("tS", hm) + p.w("dG"), inc=[("vS", 1)])
                                ev_marks[2 * m + h2] = S["vS"]
                        pmm_marks.append(S["tS"])
                        if m >= n_rms:
                            p.emit("gpsimd", lambda e: e.dma_start(
                                out=t64[:, :], in_=t1f[64:128, :]),
                                wait=p.w("vS"), inc=[("dG", 16)])
                            p.emit("vector", lambda e: e.tensor_tensor(
                                kpe[:, :], t1f[0:64, :], t64[:, :], ALU.add),
                                wait=p.w("dG"), inc=[("vS", 1)])
                    p.emit("vector", lambda e: e.tensor_copy(srow[:, :], psS[:, :]),
                           wait=p.w("tS", "dS"))
                    p.emit("vector", lambda e, n_rms=n_rms: e.tensor_scalar(
                        srow[:, :], srow[:, :], 1.0 / (n_rms * 128), None, ALU.mult),
                        inc=[("vS", 1)])
                    p.emit("scalar", lambda e: e.activation(
                        srow[:, :], srow[:, :], AF.Sqrt, bias=cst[0:1, 1:2]),
                        wait=p.w("vS"), inc=[("cS", 1)])
                    p.emit("vector", lambda e: e.reciprocal(srow[:, :], srow[:, :]),
                           wait=p.w("cS"))
                    if scale != 1.0:
                        p.emit("vector", lambda e, scale=scale: e.tensor_scalar(
                            srow[:, :], srow[:, :], scale, None, ALU.mult))
                    p.emit("vector", lambda e: e.engine_nop(), inc=[("vS", 1)])
                    for h2 in range(2):
                        for c in range(2):
                            p.emit("tensor", lambda e, c=c, h2=h2: e.matmul(
                                psC2[h2][:, 512 * c:512 * (c + 1)], ones_r[:, :],
                                srow[:, 1024 * h2 + 512 * c:1024 * h2 + 512 * (c + 1)],
                                start=True, stop=True),
                                wait=(p.w("vS") if c == 0 else ()),
                                inc=([("tS", 1)] if c == 1 else ()))
                        p.emit("vector", lambda e, h2=h2: e.tensor_copy(
                            rstd_bc[:, 1024 * h2:1024 * (h2 + 1)], psC2[h2][:, :]),
                            wait=p.w("tS"), inc=[("vS", 1)])
                    for m in range(n_rms):
                        p.emit("vector", lambda e, m=m, ln_sb=ln_sb: e.scalar_tensor_tensor(
                            dst_t(m)[:, :], dst_t(m)[:, :], ln_sb[:, m:m + 1],
                            rstd_bc[:, :], ALU.mult, ALU.mult))
                    p.emit("vector", lambda e: e.engine_nop(), inc=[("vS", 1)])

                proj_rms(wqaT, NQ, NQ, cqn_t, lnq_sb, QSC)
                proj_rms(wkvaT, 5, NKV, ckv_t, lnkv_sb, 1.0)
                dbg_dump_fm(p, "cqn_dbg", cqn_t, NQ)
                dbg_dump_fm(p, "ckv_dbg", ckv_t, NKV)
                if "kpe_dbg" in dbg_out:
                    p.emit("gpsimd", lambda e: e.dma_start(
                        out=dbg_out["kpe_dbg"], in_=kpe[:, :]),
                        wait=p.w("vS"), inc=[("dG", 16)])
                p.run()


        if upto >= 6:
            p = Phase(nc, sems, S)
            with ExitStack() as c6:
                wqb_sb = [sbuf(c6, f"wqb{i}", [128, NQ * 256], BF16) for i in range(2)]
                wkvb_sb = [sbuf(c6, f"wkvb{i}", [128, NKV * 256], BF16) for i in range(2)]
                qn2 = [sbuf(c6, f"qn{i}", [128, T], BF16) for i in range(2)]
                qpe2 = [sbuf(c6, f"qpe{i}", [64, T], BF16) for i in range(2)]
                kn2 = [sbuf(c6, f"kn{i}", [128, T], BF16) for i in range(2)]
                vfm1 = sbuf(c6, "vfm1", [128, T], BF16)
                vfm2 = [vfm1, vfm1]
                vtm2 = [sbuf(c6, f"vtm{i}", [128, T], BF16) for i in range(2)]
                probs = [sbuf(c6, f"pr{i}", [128, T], BF16) for i in range(2)]
                probsT = [sbuf(c6, "prT0", [128, T], BF16),
                          sbuf(c6, "prT1", [128, T], BF16)]
                t1f = sbuf(c6, "p6t1", [128, 256], F32)
                t64 = sbuf(c6, "p6t64", [64, 256], F32)
                mask_sb = sbuf(c6, "mask_sb", [128, 128], BF16)
                cmax = sbuf(c6, "cmax", [128, 4], F32)
                nmax = sbuf(c6, "nmax", [128, 2], F32)
                sums = sbuf(c6, "sums", [128, 2], F32)
                recip = sbuf(c6, "recip", [128, 1], F32)
                psA = psum(c6, "p6a", [128, T])
                psT = psum(c6, "p6t", [128, 1024], BF16)
                psO = psum(c6, "p6o", [128, 128])
                psP = psum(c6, "p6p", [128, 1024])

                p.emit("gpsimd", lambda e: e.dma_start(out=mask_sb[:, :], in_=maskd),
                       inc=[("dG", 16)])
                mask_mark = S["dG"]

                psT_guard = [[("vS", 0)]]
                slice_mark = {}
                hend_mark = {-1: 0, -2: 0}

                def emit_slices(h):
                    hp = h % 2
                    for k in range(NQ):
                        p.emit("gpsimd", lambda e, k=k, h=h, hp=hp: e.dma_start(
                            out=wqb_sb[hp][:, 256 * k:256 * (k + 1)],
                            in_=wqbT[128 * k:128 * (k + 1), 256 * h:256 * (h + 1)]),
                            wait=(p.at("tS", hend_mark[h - 2]) if k == 0 and h >= 2 else ()),
                            inc=[(f"dW{hp}", 16)])
                    for k in range(NKV):
                        p.emit("gpsimd", lambda e, k=k, h=h, hp=hp: e.dma_start(
                            out=wkvb_sb[hp][:, 256 * k:256 * (k + 1)],
                            in_=wkvbT[128 * k:128 * (k + 1), 256 * h:256 * (h + 1)]),
                            inc=[(f"dW{hp}", 16)])
                    slice_mark[h] = S[f"dW{hp}"]

                def proj_units(h):
                    hp = h % 2
                    units = []
                    half_ev = [[("cS", 0)], [("cS", 0)]]
                    uidx = [0]

                    def mm_unit(lhs_off, nkk, rhs_t, dst, tq8, post=None):
                        def go():
                            hh = uidx[0] % 2
                            uidx[0] += 1
                            pslice = psP[:, 512 * hh:512 * hh + 256]
                            wv = p.at(f"dW{h % 2}", slice_mark[h]) + \
                                p.at("tS", hend_mark[h - 2]) + half_ev[hh]
                            for k in range(nkk):
                                p.emit("tensor", lambda e, k=k, lhs_off=lhs_off, nkk=nkk,
                                       rhs_t=rhs_t, tq8=tq8, hp=hp, pslice=pslice: e.matmul(
                                    pslice, wqb_sb[hp][:, 256 * k + lhs_off:256 * k + lhs_off + 128]
                                    if nkk == NQ else
                                    wkvb_sb[hp][:, 256 * k + lhs_off:256 * k + lhs_off + 128],
                                    rhs_t(k)[:, 256 * tq8:256 * (tq8 + 1)],
                                    start=(k == 0), stop=(k == nkk - 1)),
                                    wait=(wv if k == 0 else ()),
                                    inc=([("tS", 1)] if k == nkk - 1 else ()))
                            um = S["tS"]
                            if post is None:
                                p.emit("scalar", lambda e, dst=dst, tq8=tq8, pslice=pslice: e.copy(
                                    dst[:, 256 * tq8:256 * (tq8 + 1)], pslice),
                                    wait=p.at("tS", um), inc=[("cS", 1)])
                                half_ev[hh] = p.at("cS", S["cS"])
                            else:
                                post(pslice)
                                half_ev[hh] = p.at("vS", S["vS"])
                        return go

                    for tq8 in range(8):
                        units.append(mm_unit(0, NQ, cqn_t, qn2[hp], tq8))
                    for tq8 in range(8):
                        def rope_post(pslice, tq8=tq8, hp=hp):
                            p.emit("vector", lambda e, tq8=tq8, pslice=pslice: e.tensor_tensor(
                                t1f[:, :], pslice,
                                cs_sb[:, 256 * tq8:256 * (tq8 + 1)], ALU.mult),
                                wait=p.w("tS", "dG"), inc=[("vS", 1)])
                            p.emit("gpsimd", lambda e: e.dma_start(
                                out=t64[:, :], in_=t1f[64:128, :]),
                                wait=p.w("vS"), inc=[("dG", 16)])
                            p.emit("vector", lambda e, tq8=tq8, hp=hp: e.tensor_tensor(
                                qpe2[hp][:, 256 * tq8:256 * (tq8 + 1)],
                                t1f[0:64, :],
                                t64[0:64, :], ALU.add),
                                wait=p.w("dG"), inc=[("vS", 1)])
                        units.append(mm_unit(128, NQ, cqn_t, None, tq8, post=rope_post))
                    for tq8 in range(8):
                        units.append(mm_unit(0, NKV, ckv_t, kn2[hp], tq8))
                    for tq8 in range(8):
                        units.append(mm_unit(128, NKV, ckv_t, vfm2[hp], tq8))

                    def vtmT(batch, hp=hp):
                        def go():
                            for kt in range(8 * batch, 8 * batch + 8):
                                p.emit("tensor", lambda e, kt=kt, batch=batch, hp=hp: e.transpose(
                                    psT[:, 128 * (kt - 8 * batch):128 * (kt - 8 * batch + 1)],
                                    vfm2[hp][:, 128 * kt:128 * (kt + 1)], ident_b[:, :]),
                                    wait=(psT_guard[0] + p.w("vS", "cS") if kt == 8 * batch else ()),
                                    inc=([("tS", 1)] if kt == 8 * batch + 7 else ()))
                            p.emit("vector", lambda e, batch=batch, hp=hp: e.tensor_copy(
                                vtm2[hp][:, 1024 * batch:1024 * (batch + 1)],
                                psT[:, 0:1024]),
                                wait=p.w("tS"), inc=[("vS", 1)])
                            psT_guard[0] = p.at("vS", S["vS"])
                        return go
                    units.append(vtmT(0))
                    units.append(vtmT(1))
                    return units

                emit_slices(0)
                for u in proj_units(0):
                    u()
                emit_slices(1)

                for h in range(H):
                    hp = h % 2
                    pend = list(proj_units(h + 1)) if h + 1 < H else []
                    if h + 2 < H:
                        pass
                    exp_mark, ptc_mark, av_mark, tT_mark, nrm_mark = {}, {}, {}, {}, {}
                    sc_marks = {}

                    def emit_scores(j, h=h, hp=hp):
                        wv = p.w("vS", "cS")
                        if j > 0:
                            wv = p.at("cS", exp_mark[j - 1])
                        for kc in range(ncj(j)):
                            n = min(512, 128 * (j + 1) - 512 * kc)
                            p.emit("tensor", lambda e, j=j, kc=kc, n=n, hp=hp: e.matmul(
                                psA[:, 512 * kc:512 * kc + n],
                                qn2[hp][:, 128 * j:128 * (j + 1)],
                                kn2[hp][:, 512 * kc:512 * kc + n],
                                start=True, stop=False),
                                wait=(wv if kc == 0 else ()))
                            p.emit("tensor", lambda e, j=j, kc=kc, n=n, hp=hp: e.matmul(
                                psA[:, 512 * kc:512 * kc + n],
                                qpe2[hp][:, 128 * j:128 * (j + 1)],
                                kpe[:, 512 * kc:512 * kc + n],
                                start=False, stop=True),
                                inc=[("tS", 1)])
                            sc_marks[(j, kc)] = S["tS"]

                    def emit_vec(j):
                        for kc in range(ncj(j)):
                            n = min(512, 128 * (j + 1) - 512 * kc)
                            if kc == ncj(j) - 1:
                                p.emit("vector", lambda e, j=j: e.tensor_tensor(
                                    psA[:, 128 * j:128 * (j + 1)],
                                    psA[:, 128 * j:128 * (j + 1)], mask_sb[:, :], ALU.add),
                                    wait=p.at("tS", sc_marks[(j, kc)]) + p.at("dG", mask_mark))
                                p.emit("vector", lambda e, kc=kc, n=n: e.reduce_max(
                                    cmax[:, kc:kc + 1], psA[:, 512 * kc:512 * kc + n], AX.X))
                            else:
                                p.emit("vector", lambda e, kc=kc, n=n: e.reduce_max(
                                    cmax[:, kc:kc + 1], psA[:, 512 * kc:512 * kc + n], AX.X),
                                    wait=p.at("tS", sc_marks[(j, kc)]))
                        p.emit("vector", lambda e, j=j: e.reduce_max(
                            nmax[:, j % 2:j % 2 + 1], cmax[:, 0:ncj(j)], AX.X, negate=True),
                            inc=[("vS", 1)])

                    def emit_T_ptc(j, h=h):
                        nb = (j + 8) // 8
                        for b in range(nb):
                            k0, k1 = 8 * b, min(8 * b + 8, j + 1)
                            for kt in range(k0, k1):
                                p.emit("tensor", lambda e, j=j, kt=kt, k0=k0: e.transpose(
                                    psT[:, 128 * (kt - k0):128 * (kt - k0 + 1)],
                                    probs[j % 2][:, 128 * kt:128 * (kt + 1)], ident_b[:, :]),
                                    wait=((p.at("cS", exp_mark[j]) if b == 0 else
                                           p.at("cS", ptc_mark[j])) + psT_guard[0]
                                          if kt == k0 else ()),
                                    inc=([("tS", 1)] if kt == k1 - 1 else ()))
                            tT_mark[j] = S["tS"]
                            p.emit("scalar", lambda e, j=j, k0=k0, k1=k1: e.copy(
                                probsT[j % 2][:, 128 * k0:128 * k1], psT[:, 0:128 * (k1 - k0)]),
                                wait=p.at("tS", tT_mark[j]), inc=[("cS", 1)])
                            ptc_mark[j] = S["cS"]
                            psT_guard[0] = p.at("cS", S["cS"])

                    def emit_AV(j, hp=hp):
                        wv = p.at("cS", ptc_mark[j])
                        if j >= 1:
                            wv = wv + p.at("vS", nrm_mark[j - 1])
                        for kt in range(j + 1):
                            p.emit("tensor", lambda e, j=j, kt=kt, hp=hp: e.matmul(
                                psO[:, :], probsT[j % 2][:, 128 * kt:128 * (kt + 1)],
                                vtm2[hp][:, 128 * kt:128 * (kt + 1)],
                                start=(kt == 0), stop=(kt == j)),
                                wait=(wv if kt == 0 else ()),
                                inc=([("tS", 1)] if kt == j else ()))
                        av_mark[j] = S["tS"]

                    def emit_recip_norm(j):
                        p.emit("vector", lambda e, j=j: e.reciprocal(
                            recip[:, :], sums[:, j % 2:j % 2 + 1]),
                            wait=p.at("cS", exp_mark[j]), inc=[("vS", 1)])
                        p.emit("vector", lambda e, j=j, h=h: e.tensor_scalar(
                            arena_t(h)[:, 128 * j:128 * (j + 1)], psO[:, :], recip[:, 0:1],
                            None, ALU.mult), wait=p.at("tS", av_mark[j]), inc=[("vS", 1)])
                        nrm_mark[j] = S["vS"]

                    def emit_exp(j):
                        kj = 128 * (j + 1)
                        wv = p.w("vS")
                        if j >= 2:
                            wv = wv + p.at("tS", tT_mark[j - 2])
                        p.emit("scalar", lambda e, j=j, kj=kj: e.activation(
                            probs[j % 2][:, 0:kj], psA[:, 0:kj], AF.Exp,
                            bias=nmax[:, j % 2:j % 2 + 1], accum_out=sums[:, j % 2:j % 2 + 1]),
                            wait=wv, inc=[("cS", 1)])
                        exp_mark[j] = S["cS"]

                    ui = 0
                    for j in range(NT):
                        emit_scores(j)
                        emit_vec(j)
                        if j > 0:
                            emit_T_ptc(j - 1)
                        if j > 1:
                            emit_AV(j - 2)
                            emit_recip_norm(j - 2)
                        emit_exp(j)
                        want = ((j + 1) * len(pend)) // NT
                        while ui < want:
                            pend[ui]()
                            ui += 1
                    emit_T_ptc(NT - 1)
                    emit_AV(NT - 2)
                    emit_recip_norm(NT - 2)
                    emit_AV(NT - 1)
                    emit_recip_norm(NT - 1)
                    while ui < len(pend):
                        pend[ui]()
                        ui += 1

                    nm15 = nrm_mark[NT - 1]
                    for b in range(2):
                        for j in range(8 * b, 8 * b + 8):
                            p.emit("tensor", lambda e, j=j, b=b, h=h: e.transpose(
                                psT[:, 128 * (j - 8 * b):128 * (j - 8 * b + 1)],
                                arena_t(h)[:, 128 * j:128 * (j + 1)], ident_b[:, :]),
                                wait=((p.at("vS", nm15) if b == 0 else p.w("vS")) +
                                      psT_guard[0] if j == 8 * b else ()),
                                inc=([("tS", 1)] if j == 8 * b + 7 else ()))
                        p.emit("vector", lambda e, h=h, b=b: e.tensor_copy(
                            arena_t(h)[:, 1024 * b:1024 * (b + 1)], psT[:, 0:1024]),
                            wait=p.w("tS"), inc=[("vS", 1)])
                        psT_guard[0] = p.at("vS", S["vS"])
                    hend_mark[h] = S["tS"]
                    if h + 2 < H:
                        emit_slices(h + 2)
                dbg_dump_fm(p, "cat_dbg", arena_t, ND)
                p.run()

        b_scope.close()

        if upto >= 7:
            p = Phase(nc, sems, S)
            with ExitStack() as c7:
                wo_sb = [sbuf(c7, f"wo{i}", [128, ND * 128], BF16) for i in range(2)]
                dtmp = [sbuf(c7, f"p7dt{i}", [128, T], F32) for i in range(2)]
                psW = [psum(c7, f"p7w{i}", [128, T]) for i in range(2)]
                acc_done = {}
                ev_mark = {}
                mm_marks = {}
                band_mark = {}

                def emit_band7(u):
                    par = u % 2
                    p.emit("gpsimd", lambda e, u=u, par=par: e.dma_start(
                        out=wo_sb[par][:, :].rearrange("p (k c) -> p k c", c=128),
                        in_=woT[:, 128 * u:128 * (u + 1)].rearrange(
                            "(k p) c -> p k c", p=128)),
                        wait=(p.at("tS", mm_marks[u - 2]) if u >= 2 else ()),
                        inc=[(f"dW{par}", 16)])
                    return S[f"dW{par}"]

                band_mark[0] = emit_band7(0)
                band_mark[1] = emit_band7(1)
                for mo in range(ND):
                    par = mo % 2
                    wv = p.at(f"dW{par}", band_mark[mo]) + (
                        p.at("vS", ev_mark[mo - 2]) if mo >= 2 else [])
                    for k in range(ND):
                        for c in range(4):
                            p.emit("tensor", lambda e, k=k, c=c, par=par: e.matmul(
                                psW[par][:, 512 * c:512 * (c + 1)],
                                wo_sb[par][:, 128 * k:128 * (k + 1)],
                                arena_t(k)[:, 512 * c:512 * (c + 1)],
                                start=(k == 0), stop=(k == ND - 1)),
                                wait=(wv if (k == 0 and c == 0) else ()),
                                inc=([("tS", 1)] if (k == ND - 1 and c == 3) else ()))
                    mm_marks[mo] = S["tS"]
                    if mo + 2 < ND:
                        band_mark[mo + 2] = emit_band7(mo + 2)
                    p.emit("vector", lambda e, par=par: e.tensor_copy(
                        dtmp[par][:, :], psW[par][:, :]),
                        wait=p.w("tS") + (p.at(f"dA{par}", acc_done[mo - 2]) if mo >= 2 else []),
                        inc=[("vS", 1)])
                    ev_mark[mo] = S["vS"]
                    p.emit("gpsimd", lambda e, mo=mo, par=par: e.dma_start(
                        out=xh[128 * mo:128 * (mo + 1), :], in_=dtmp[par][:, :],
                        accum_op=ALU.add),
                        wait=p.w("vS"), inc=[(f"dA{par}", 16)])
                    acc_done[mo] = S[f"dA{par}"]
                if "xh_dbg" in dbg_out and upto == 7:
                    p.emit("sync", lambda e: e.dma_start(out=dbg_out["xh_dbg"], in_=xh),
                           wait=p.w("dG", "dA0", "dA1"), inc=[("dS", 16)])
                p.run()

        if upto >= 8:
            p = Phase(nc, sems, S)
            with ExitStack() as c8:
                xt = [sbuf(c8, f"p8x{i}", [128, T], F32) for i in range(2)]
                sqf = sbuf(c8, "p8sq", [128, T], F32)
                r1 = sbuf(c8, "p8r1", [1, T], F32)
                r2 = sbuf(c8, "p8r2", [1, T], F32)
                nrbc = sbuf(c8, "p8nr", [128, T], F32)
                nmbc = sbuf(c8, "p8nm", [128, T], F32)
                tmpf = sbuf(c8, "p8tmp", [128, T], F32)
                with ExitStack() as c8s:
                    psS1 = psum(c8s, "p8s1", [1, T])
                    psS2 = psum(c8s, "p8s2", [1, T])
                    for i in range(NT):
                        par = i % 2
                        p.emit("sync", lambda e, i=i, par=par: e.dma_start(
                            out=xt[par][:, :], in_=xh[128 * i:128 * (i + 1), :]),
                            wait=(p.at("tS", S["tS"] - 1) if i >= 2 else ()),
                            inc=[(f"dH{par}", 16)])
                        ld_mark = S[f"dH{par}"]
                        p.emit("scalar", lambda e, par=par: e.square(
                            sqf[:, :], xt[par][:, :]),
                            wait=p.at(f"dH{par}", ld_mark) + (p.w("tS") if i > 0 else []),
                            inc=[("cS", 1)])
                        for c in range(4):
                            p.emit("tensor", lambda e, c=c, i=i, par=par: e.matmul(
                                psS1[:, 512 * c:512 * (c + 1)], ones_c[:, :],
                                xt[par][:, 512 * c:512 * (c + 1)],
                                start=(i == 0), stop=(i == NT - 1)),
                                wait=(p.at(f"dH{par}", ld_mark) if c == 0 else ()),
                                inc=([("tS", 1)] if c == 3 else ()))
                        for c in range(4):
                            p.emit("tensor", lambda e, c=c, i=i: e.matmul(
                                psS2[:, 512 * c:512 * (c + 1)], ones_c[:, :],
                                sqf[:, 512 * c:512 * (c + 1)],
                                start=(i == 0), stop=(i == NT - 1)),
                                wait=(p.w("cS") if c == 0 else ()),
                                inc=([("tS", 1)] if c == 3 else ()))
                    p.emit("vector", lambda e: e.tensor_copy(r1[:, :], psS1[:, :]),
                           wait=p.w("tS"))
                    p.emit("vector", lambda e: e.tensor_copy(r2[:, :], psS2[:, :]))
                    p.emit("vector", lambda e: e.tensor_scalar(
                        r1[:, :], r1[:, :], 1.0 / D, None, ALU.mult))
                    p.emit("vector", lambda e: e.tensor_scalar(
                        r2[:, :], r2[:, :], 1.0 / D, None, ALU.mult))
                    p.emit("vector", lambda e: e.tensor_tensor(
                        sqf[0:1, 0:T], r1[:, :], r1[:, :], ALU.mult))
                    p.emit("vector", lambda e: e.tensor_tensor(
                        r2[:, :], r2[:, :], sqf[0:1, 0:T], ALU.subtract), inc=[("vS", 1)])
                    p.emit("scalar", lambda e: e.activation(
                        r2[:, :], r2[:, :], AF.Sqrt, bias=cst[0:1, 0:1]),
                        wait=p.w("vS"), inc=[("cS", 1)])
                    p.emit("vector", lambda e: e.reciprocal(r2[:, :], r2[:, :]),
                           wait=p.w("cS"))
                    p.emit("vector", lambda e: e.tensor_tensor(
                        r1[:, :], r1[:, :], r2[:, :], ALU.mult), inc=[("vS", 1)])
                with ExitStack() as c8r:
                    psR = psum(c8r, "p8pr", [128, T])
                    for c in range(4):
                        p.emit("tensor", lambda e, c=c: e.matmul(
                            psR[:, 512 * c:512 * (c + 1)], ones_r[:, :],
                            r2[:, 512 * c:512 * (c + 1)], start=True, stop=True),
                            wait=(p.w("vS") if c == 0 else ()),
                            inc=([("tS", 1)] if c == 3 else ()))
                    p.emit("vector", lambda e: e.tensor_copy(nrbc[:, :], psR[:, :]),
                           wait=p.w("tS"), inc=[("vS", 1)])
                    for c in range(4):
                        p.emit("tensor", lambda e, c=c: e.matmul(
                            psR[:, 512 * c:512 * (c + 1)], ones_r[:, :],
                            r1[:, 512 * c:512 * (c + 1)], start=True, stop=True),
                            wait=(p.w("vS") if c == 0 else ()),
                            inc=([("tS", 1)] if c == 3 else ()))
                    p.emit("vector", lambda e: e.tensor_copy(nmbc[:, :], psR[:, :]),
                           wait=p.w("tS"), inc=[("vS", 1)])
                    vmarks = {}
                    for i in range(NT):
                        par = i % 2
                        p.emit("sync", lambda e, i=i, par=par: e.dma_start(
                            out=xt[par][:, :], in_=xh[128 * i:128 * (i + 1), :]),
                            wait=(p.at("vS", vmarks[i - 2]) if i >= 2 else p.w("tS", "cS")),
                            inc=[(f"dH{par}", 16)])
                        ld_mark = S[f"dH{par}"]
                        p.emit("vector", lambda e, par=par: e.tensor_tensor(
                            tmpf[:, :], xt[par][:, :], nrbc[:, :], ALU.mult),
                            wait=p.at(f"dH{par}", ld_mark), inc=[("vS", 1)])
                        p.emit("vector", lambda e, i=i: e.tensor_tensor(
                            arena_t(i)[:, :], tmpf[:, :], nmbc[:, :], ALU.subtract),
                            inc=[("vS", 1)])
                        vmarks[i] = S["vS"]
                dbg_dump_fm(p, "xn1_dbg", arena_t, ND)
                p.run()

        if upto >= 9:
            p = Phase(nc, sems, S)
            with ExitStack() as c9:
                wq_sb2 = [sbuf(c9, f"p9wq{i}", [128, ND * 1024], BF16) for i in range(2)]
                hh_sb = sbuf(c9, "p9hh", [128, D], F32)
                gw_sb = sbuf(c9, "p9gw", [128, D], F32)
                gb_sb = sbuf(c9, "p9gb", [128, D], F32)
                c32_sb = sbuf(c9, "p9c32", [128, NT * 32], F32)
                s32_sb = sbuf(c9, "p9s32", [128, NT * 32], F32)
                h_sb = [sbuf(c9, f"p9h{i}", [128, 1024], F32) for i in range(2)]
                h2_sb = [sbuf(c9, f"p9h2{i}", [128, 1024], F32) for i in range(2)]
                gn_f = [sbuf(c9, f"p9gn{i}", [128, 1024], F32) for i in range(2)]
                sq_f = sbuf(c9, "p9sqf", [128, 1024], F32)
                gst = sbuf(c9, "p9gst", [128, 64], F32)
                rt = sbuf(c9, "p9rt", [128, 64], F32)
                psG = [psum(c9, f"p9g{i}", [128, 1024]) for i in range(2)]

                for dst, src in [(hh_sb, hhb_in), (gw_sb, gnw_in), (gb_sb, gnb_in)]:
                    p.emit("sync", lambda e, d=dst, s=src: e.dma_start(out=d[:, :], in_=s),
                           inc=[("dS", 16)])
                for i in range(NT):
                    p.emit("sync", lambda e, i=i: e.dma_start(
                        out=c32_sb[:, 32 * i:32 * (i + 1)],
                        in_=c32_in[128 * i:128 * (i + 1), :]), inc=[("dS", 16)])
                    p.emit("sync", lambda e, i=i: e.dma_start(
                        out=s32_sb[:, 32 * i:32 * (i + 1)],
                        in_=s32_in[128 * i:128 * (i + 1), :]), inc=[("dS", 16)])
                gi = 0
                relu_mark = {}
                acc_done = {}
                og_dgm = {}
                for og in range(2):
                    p.emit("gpsimd", lambda e, og=og: e.dma_start(
                        out=wq_sb2[og][:, :].rearrange("p (k c) -> p k c", c=1024),
                        in_=WQm[:, 1024 * og:1024 * (og + 1)].rearrange(
                            "(k p) c -> p k c", p=128)),
                        inc=[(f"dW{og}", 16)])
                    og_dgm[og] = S[f"dW{og}"]
                for og in range(2):
                    wq_sb = wq_sb2[og]
                    dgm = og_dgm[og]
                    for t in range(NT):
                        gi += 1
                        par = gi % 2
                        wv = p.at(f"dW{og}", dgm) + p.w("dS")
                        if gi > 2:
                            wv = wv + p.at("vS", relu_mark[gi - 2])
                        for k in range(ND):
                            for c in range(2):
                                p.emit("tensor", lambda e, k=k, c=c, t=t, par=par, wq_sb=wq_sb: e.matmul(
                                    psG[par][:, 512 * c:512 * (c + 1)],
                                    arena_t(k)[:, 128 * t:128 * (t + 1)],
                                    wq_sb[:, 1024 * k + 512 * c:1024 * k + 512 * (c + 1)],
                                    start=(k == 0), stop=(k == ND - 1)),
                                    wait=(wv if (k == 0 and c == 0) else ()),
                                    inc=([("tS", 1)] if (k == ND - 1 and c == 1) else ()))
                        tm = S["tS"]
                        hv, h2v, gv = h_sb[par], h2_sb[par], gn_f[par]
                        if og == 0:
                            def rview(par=par, t=t, hv=hv):
                                pg = psG[par]
                                A = pg[:, 0:64].rearrange("p (a b) -> p a b", b=2)[:, :, 0:1]
                                B = pg[:, 0:64].rearrange("p (a b) -> p a b", b=2)[:, :, 1:2]
                                ct = c32_sb[:, 32 * t:32 * (t + 1)].rearrange("p (a o) -> p a o", o=1)
                                st_ = s32_sb[:, 32 * t:32 * (t + 1)].rearrange("p (a o) -> p a o", o=1)
                                hA = hv[:, 0:64].rearrange("p (a b) -> p a b", b=2)[:, :, 0:1]
                                hB = hv[:, 0:64].rearrange("p (a b) -> p a b", b=2)[:, :, 1:2]
                                r0 = rt[:, 0:32].rearrange("p (a o) -> p a o", o=1)
                                r1_ = rt[:, 32:64].rearrange("p (a o) -> p a o", o=1)
                                return A, B, ct, st_, hA, hB, r0, r1_
                            A, B, ct_, st_, hA, hB, r0, r1_ = rview()
                            p.emit("vector", lambda e, A=A, ct_=ct_, r0=r0: e.tensor_tensor(
                                r0, A, ct_, ALU.mult), wait=p.at("tS", tm))
                            p.emit("vector", lambda e, B=B, st_=st_, r1_=r1_: e.tensor_tensor(
                                r1_, B, st_, ALU.mult))
                            p.emit("vector", lambda e, hA=hA, r0=r0, r1_=r1_: e.tensor_tensor(
                                hA, r0, r1_, ALU.subtract))
                            p.emit("vector", lambda e, B=B, ct_=ct_, r0=r0: e.tensor_tensor(
                                r0, B, ct_, ALU.mult))
                            p.emit("vector", lambda e, A=A, st_=st_, r1_=r1_: e.tensor_tensor(
                                r1_, A, st_, ALU.mult))
                            p.emit("vector", lambda e, hB=hB, r0=r0, r1_=r1_: e.tensor_tensor(
                                hB, r0, r1_, ALU.add))
                            p.emit("vector", lambda e, hv=hv: e.tensor_relu(
                                hv[:, 0:64], hv[:, 0:64]))
                            p.emit("vector", lambda e, par=par, hv=hv: e.tensor_relu(
                                hv[:, 64:1024], psG[par][:, 64:1024]), inc=[("vS", 1)])
                        else:
                            p.emit("vector", lambda e, par=par, hv=hv: e.tensor_relu(
                                hv[:, :], psG[par][:, :]), wait=p.at("tS", tm),
                                inc=[("vS", 1)])
                        relu_mark[gi] = S["vS"]
                        p.emit("vector", lambda e, og=og, hv=hv: e.tensor_tensor(
                            hv[:, :], hv[:, :], hh_sb[:, 1024 * og:1024 * (og + 1)], ALU.add),
                            inc=[("vS", 1)])
                        p.emit("scalar", lambda e, hv=hv, h2v=h2v: e.activation(
                            h2v[:, :], hv[:, :], AF.Tanh), wait=p.w("vS"), inc=[("cS", 1)])
                        p.emit("scalar", lambda e, h2v=h2v: e.square(sq_f[:, :], h2v[:, :]),
                               inc=[("cS", 1)])
                        p.emit("vector", lambda e, h2v=h2v: e.tensor_reduce(
                            gst[:, 0:16].rearrange("p (g o) -> p g o", o=1),
                            h2v[:, :].rearrange("p (g d) -> p g d", d=64), AX.X, ALU.add),
                            wait=p.w("cS"))
                        p.emit("vector", lambda e: e.tensor_reduce(
                            gst[:, 16:32].rearrange("p (g o) -> p g o", o=1),
                            sq_f[:, :].rearrange("p (g d) -> p g d", d=64), AX.X, ALU.add))
                        p.emit("vector", lambda e: e.tensor_scalar(
                            gst[:, 0:16], gst[:, 0:16], 1.0 / 64, None, ALU.mult))
                        p.emit("vector", lambda e: e.tensor_scalar(
                            gst[:, 16:32], gst[:, 16:32], 1.0 / 64, None, ALU.mult))
                        p.emit("vector", lambda e: e.tensor_tensor(
                            gst[:, 32:48], gst[:, 0:16], gst[:, 0:16], ALU.mult))
                        p.emit("vector", lambda e: e.tensor_tensor(
                            gst[:, 16:32], gst[:, 16:32], gst[:, 32:48], ALU.subtract),
                            inc=[("vS", 1)])
                        p.emit("scalar", lambda e: e.activation(
                            gst[:, 16:32], gst[:, 16:32], AF.Sqrt, bias=cst[:, 2:3]),
                            wait=p.w("vS"), inc=[("cS", 1)])
                        p.emit("vector", lambda e: e.reciprocal(gst[:, 16:32], gst[:, 16:32]),
                               wait=p.w("cS"))
                        for g in range(16):
                            p.emit("vector", lambda e, g=g, h2v=h2v, gv=gv: e.tensor_scalar(
                                gv[:, 64 * g:64 * (g + 1)], h2v[:, 64 * g:64 * (g + 1)],
                                gst[:, g:g + 1], gst[:, 16 + g:17 + g],
                                ALU.subtract, ALU.mult),
                                wait=(p.at(f"dA{par}", acc_done[gi - 2]) if (g == 0 and gi > 2)
                                      else ()))
                        p.emit("vector", lambda e, og=og, gv=gv: e.tensor_tensor(
                            gv[:, :], gv[:, :], gw_sb[:, 1024 * og:1024 * (og + 1)], ALU.mult))
                        p.emit("vector", lambda e, og=og, gv=gv: e.tensor_tensor(
                            gv[:, :], gv[:, :], gb_sb[:, 1024 * og:1024 * (og + 1)], ALU.add),
                            inc=[("vS", 1)])
                        vmark = S["vS"]
                        p.emit("gpsimd", lambda e, t=t, og=og, gv=gv: e.dma_start(
                            out=xh[128 * t:128 * (t + 1), 1024 * og:1024 * (og + 1)],
                            in_=gv[:, :], accum_op=ALU.add),
                            wait=p.at("vS", vmark), inc=[(f"dA{par}", 16)])
                        acc_done[gi] = S[f"dA{par}"]
                p.run()

        if upto >= 10:
            p = Phase(nc, sems, S)
            with ExitStack() as ca:
                wf_sb = sbuf(ca, "pawf", [128, ND * 512], F32)
                wf2_sb = sbuf(ca, "pawf2", [128, ND * 512], F32)
                wq1 = [sbuf(ca, f"pawq1{i}", [128, ND * 512], BF16) for i in range(2)]
                wq2 = [sbuf(ca, f"pawq2{i}", [128, ND * 512], BF16) for i in range(2)]
                sgt = sbuf(ca, "pasg", [128, T], BF16)
                hstr = [sbuf(ca, f"pah{i}", [128, T], BF16) for i in range(2)]
                psB1 = psum(ca, "pab1", [128, T])
                psB2 = psum(ca, "pab2", [128, T])
                mm_marks = {}
                q_marks = {}
                ld_marks = {}

                def emit_loads(og):
                    c0 = 512 * og
                    lw = p.w("tS", "vS") if og == 0 else p.at("vS", q_marks[og - 1])
                    p.emit("sync", lambda e, c0=c0: e.dma_start(
                        out=wf_sb[:, 0:ND * 512].rearrange("p (k c) -> p k c", c=512),
                        in_=w1T[:, c0:c0 + 512].rearrange("(k p) c -> p k c", p=128)),
                        wait=lw, inc=[("dL", 16)])
                    ld_marks[(og, 1)] = S["dL"]
                    p.emit("sync", lambda e, c0=c0: e.dma_start(
                        out=wf2_sb[:, 0:ND * 512].rearrange("p (k c) -> p k c", c=512),
                        in_=w2T[:, c0:c0 + 512].rearrange("(k p) c -> p k c", p=128)),
                        inc=[("dP0", 16)])
                    ld_marks[(og, 2)] = S["dP0"]

                def emit_quant(og):
                    par = og % 2
                    for wq, wf, ldm, lds, ti in ((wq1, wf_sb, ld_marks[(og, 1)], "dL", 0),
                                                 (wq2, wf2_sb, ld_marks[(og, 2)], "dP0", 2)):
                        for k in range(ND):
                            p.emit("vector", lambda e, k=k, par=par, wq=wq, wf=wf, ti=ti: e.tensor_scalar(
                                wq[par][:, 512 * k:512 * (k + 1)], wf[:, 512 * k:512 * (k + 1)],
                                thr[:, ti:ti + 1], 1.0, ALU.is_ge, ALU.subtract),
                                wait=((p.at(lds, ldm) +
                                       (p.at("tS", mm_marks[og - 2]) if og >= 2 else []))
                                      if k == 0 else ()))
                            p.emit("vector", lambda e, k=k, par=par, wq=wq, wf=wf, ti=ti: e.scalar_tensor_tensor(
                                wq[par][:, 512 * k:512 * (k + 1)], wf[:, 512 * k:512 * (k + 1)],
                                thr[:, ti + 1:ti + 2], wq[par][:, 512 * k:512 * (k + 1)],
                                ALU.is_gt, ALU.add))
                    p.emit("vector", lambda e: e.engine_nop(), inc=[("vS", 1)])
                    q_marks[og] = S["vS"]

                emit_loads(0)
                emit_quant(0)
                for og in range(11):
                    c0 = 512 * og
                    par = og % 2
                    if og + 1 < 11:
                        emit_loads(og + 1)
                        emit_quant(og + 1)
                    for m in range(4):
                        s_par = m % 2
                        wv = p.at("vS", q_marks[og])
                        if not (og == 0 and m == 0):
                            wv = wv + p.w("cS")
                        for k in range(ND):
                            for c in range(4):
                                p.emit("tensor", lambda e, k=k, c=c, m=m, par=par: e.matmul(
                                    psB1[:, 512 * c:512 * (c + 1)],
                                    wq1[par][:, 512 * k + 128 * m:512 * k + 128 * (m + 1)],
                                    arena_t(k)[:, 512 * c:512 * (c + 1)],
                                    start=(k == 0), stop=(k == ND - 1)),
                                    wait=(wv if (k == 0 and c == 0) else ()),
                                    inc=([("tS", 1)] if (k == ND - 1 and c == 3) else ()))
                        b1m = S["tS"]
                        wv2 = p.w("vS") if not (og == 0 and m == 0) else ()
                        for k in range(ND):
                            for c in range(4):
                                p.emit("tensor", lambda e, k=k, c=c, m=m, par=par: e.matmul(
                                    psB2[:, 512 * c:512 * (c + 1)],
                                    wq2[par][:, 512 * k + 128 * m:512 * k + 128 * (m + 1)],
                                    arena_t(k)[:, 512 * c:512 * (c + 1)],
                                    start=(k == 0), stop=(k == ND - 1)),
                                    wait=(wv2 if (k == 0 and c == 0) else ()),
                                    inc=([("tS", 1)] if (k == ND - 1 and c == 3) else ()))
                        b2m = S["tS"]
                        p.emit("scalar", lambda e: e.activation(
                            sgt[:, :], psB1[:, :], AF.Sigmoid),
                            wait=p.at("tS", b1m) + p.w("vS"), inc=[("cS", 1)])
                        p.emit("vector", lambda e, s_par=s_par: e.tensor_tensor(
                            hstr[s_par][:, :], sgt[:, :], psB2[:, :], ALU.mult),
                            wait=p.at("tS", b2m) + p.w("cS") +
                            (p.w("dS") if not (og == 0 and m <= 1) else []),
                            inc=[("vS", 1)])
                        p.emit("sync", lambda e, c0=c0, m=m, s_par=s_par: e.dma_start(
                            out=hid_hbm[c0 + 128 * m:c0 + 128 * (m + 1), :],
                            in_=hstr[s_par][:, :]),
                            wait=p.w("vS"), inc=[("dS", 16)])
                    mm_marks[og] = S["tS"]
                if "hid_dbg" in dbg_out:
                    p.emit("sync", lambda e: e.dma_start(
                        out=dbg_out["hid_dbg"], in_=hid_hbm),
                        wait=p.w("dS"), inc=[("dS", 16)])
                p.run()

            p = Phase(nc, sems, S)
            with ExitStack() as cb:
                w3f = [sbuf(cb, f"pbw3f{i}", [128, ND * 128], F32) for i in range(2)]
                w3q = [sbuf(cb, f"pbw3q{i}", [128, ND * 128], BF16) for i in range(2)]
                dtmp = [sbuf(cb, f"pbdt{i}", [128, T], F32) for i in range(2)]
                xr = sbuf(cb, "pbxr", [128, T], F32)
                fin = sbuf(cb, "pbfin", [128, T], F32)
                otmp = sbuf(cb, "pbot", [128, T], F32)
                slabs = [(0, 16), (2048, 16), (4096, 12)]
                acc_done = {}
                ev_mark = {}
                mm_marks = {}
                fin_marks = {}
                tr_marks = {}
                ot_marks = {}
                gi = 0
                units = []
                for si, (r0, nk) in enumerate(slabs):
                    for mo in range(ND):
                        units.append((si, r0, nk, mo))
                q_marks = {}
                band_mark = {}

                def emit_band10(u):
                    si, r0, nk, mo = units[u]
                    par = (u + 1) % 2
                    p.emit("gpsimd", lambda e, r0=r0, nk=nk, mo=mo, par=par: e.dma_start(
                        out=w3f[par][:, 0:nk * 128].rearrange("p (k c) -> p k c", c=128),
                        in_=w3T[r0:r0 + nk * 128, 128 * mo:128 * (mo + 1)].rearrange(
                            "(k p) c -> p k c", p=128)),
                        wait=(p.at("vS", q_marks[u - 2]) if u >= 2 else ()),
                        inc=[(f"dW{par}", 16)])
                    return S[f"dW{par}"]

                def emit_quant10(u):
                    si, r0, nk, mo = units[u]
                    par = (u + 1) % 2
                    p.emit("vector", lambda e, nk=nk, par=par: e.tensor_scalar(
                        w3q[par][:, 0:nk * 128], w3f[par][:, 0:nk * 128],
                        thr[:, 4:5], 1.0, ALU.is_ge, ALU.subtract),
                        wait=p.at(f"dW{par}", band_mark[u]) +
                        (p.at("tS", mm_marks[u - 2]) if u >= 2 else []))
                    p.emit("vector", lambda e, nk=nk, par=par: e.scalar_tensor_tensor(
                        w3q[par][:, 0:nk * 128], w3f[par][:, 0:nk * 128],
                        thr[:, 5:6], w3q[par][:, 0:nk * 128], ALU.is_gt, ALU.add),
                        inc=[("vS", 1)])
                    q_marks[u] = S["vS"]

                cur_slab = -1
                band_mark[0] = emit_band10(0)
                band_mark[1] = emit_band10(1)
                emit_quant10(0)
                emit_quant10(1)
                with ExitStack() as cb1:
                    psD = [psum(cb1, f"pbd{i}", [128, T]) for i in range(2)]
                    for u, (si, r0, nk, mo) in enumerate(units):
                        last = (si == len(slabs) - 1)
                        gi += 1
                        par = gi % 2
                        if si != cur_slab:
                            cur_slab = si
                            p.emit("sync", None, wait=p.w("tS"))
                            for k in range(nk):
                                p.emit("sync", lambda e, r0=r0, k=k: e.dma_start(
                                    out=arena[:, T * k:T * (k + 1)],
                                    in_=hid_hbm[r0 + 128 * k:r0 + 128 * (k + 1), :]),
                                    inc=[("dL", 16)])
                            ds_mark = S["dL"]
                        wv = p.at("dL", ds_mark) + p.at("vS", q_marks[u])
                        if last:
                            wv = wv + p.at("vS", max(fin_marks.get(gi - 1, 0),
                                                     ev_mark.get(31, 0), ev_mark.get(32, 0)))
                        elif gi > 2:
                            wv = wv + p.at("vS", ev_mark.get(gi - 2, 0))
                        pd = psD[par if not last else 0]
                        for k in range(nk):
                            for c in range(4):
                                p.emit("tensor", lambda e, k=k, c=c, pd=pd, nk=nk, par=par: e.matmul(
                                    pd[:, 512 * c:512 * (c + 1)],
                                    w3q[par][:, 128 * k:128 * (k + 1)],
                                    arena[:, T * k + 512 * c:T * k + 512 * (c + 1)],
                                    start=(k == 0), stop=(k == nk - 1)),
                                    wait=(wv if (k == 0 and c == 0) else ()),
                                    inc=([("tS", 1)] if (k == nk - 1 and c == 3) else ()))
                        mm_mark = S["tS"]
                        mm_marks[u] = mm_mark
                        if u + 2 < len(units):
                            band_mark[u + 2] = emit_band10(u + 2)
                            emit_quant10(u + 2)
                        if not last:
                            p.emit("vector", lambda e, par=par, pd=pd: e.tensor_copy(
                                dtmp[par][:, :], pd[:, :]),
                                wait=p.at("tS", mm_mark) +
                                (p.at(f"dA{par}", acc_done[gi - 2]) if gi > 2 else []),
                                inc=[("vS", 1)])
                            ev_mark[gi] = S["vS"]
                            p.emit("gpsimd", lambda e, mo=mo, par=par: e.dma_start(
                                out=xh[128 * mo:128 * (mo + 1), :], in_=dtmp[par][:, :],
                                accum_op=ALU.add),
                                wait=p.w("vS"), inc=[(f"dA{par}", 16)])
                            acc_done[gi] = S[f"dA{par}"]
                        else:
                            p.emit("sync", lambda e, mo=mo: e.dma_start(
                                out=xr[:, :], in_=xh[128 * mo:128 * (mo + 1), :]),
                                wait=p.at(f"dA{(mo + 17) % 2}", acc_done[16 + mo + 1]) + p.w("vS"),
                                inc=[("dS", 16)])
                            p.emit("vector", lambda e, pd0=psD[0]: e.tensor_tensor(
                                fin[:, :], pd0[:, :], xr[:, :], ALU.add),
                                wait=p.at("tS", mm_mark) + p.w("dS") +
                                (p.at("tS", tr_marks.get(gi - 1, 0))), inc=[("vS", 1)])
                            fmark = S["vS"]
                            fin_marks[gi] = fmark
                            for tb in range(NT):
                                p.emit("tensor", lambda e, tb=tb, pF=psD[1]: e.transpose(
                                    pF[:, 128 * tb:128 * (tb + 1)],
                                    fin[:, 128 * tb:128 * (tb + 1)], ident_f[:, :]),
                                    wait=(p.at("vS", fmark) +
                                          p.at("vS", ot_marks.get(gi - 1, 0))
                                          if tb == 0 else ()),
                                    inc=([("tS", 1)] if tb == NT - 1 else ()))
                            tr_marks[gi] = S["tS"]
                            p.emit("vector", lambda e, pF=psD[1]: e.tensor_copy(
                                otmp[:, :], pF[:, :]),
                                wait=p.at("tS", tr_marks[gi]) + (p.w("dS") if mo > 0 else []),
                                inc=[("vS", 1)])
                            ot_marks[gi] = S["vS"]
                            p.emit("sync", lambda e, mo=mo: e.dma_start(
                                out=o_out[:, 128 * mo:128 * (mo + 1)].rearrange(
                                    "(tb p) n -> p tb n", p=128),
                                in_=otmp[:, :].rearrange("p (tb n) -> p tb n", n=128)),
                                wait=p.w("vS"), inc=[("dS", 16)])
                p.run()

    return nc



def _host_prep(inputs):
    f32 = np.float32
    asf = lambda a: np.ascontiguousarray(np.asarray(a), dtype=f32)

    _ = inputs["L1_attn_S_n"]
    w_qb = np.asarray(inputs["L0_attn_w_qb"])
    perm_q = []
    for h in range(H):
        base = h * 192
        perm_q += list(range(base, base + 128))
        perm_q += list(range(base + 128, base + 192))
        perm_q += [base + 128 + ((j + 32) % 64) for j in range(64)]
    wqbT = asf(w_qb[perm_q].T)

    w_kva = np.asarray(inputs["L0_attn_w_kva"])
    perm_kv = list(range(512)) + list(range(512, 576)) + \
        [512 + ((j + 32) % 64) for j in range(64)]
    wkvaT = asf(w_kva[perm_kv].T)

    inv = 1.0 / (10000.0 ** (np.arange(0, 64, 2, dtype=np.float64) / 64))
    frq = np.outer(np.arange(T, dtype=np.float64), inv)
    emb = np.concatenate([frq, frq], axis=1)
    cosT = np.cos(emb).T
    sinT = np.sin(emb).T
    sgn = np.where(np.arange(64) < 32, -1.0, 1.0)[:, None]
    cs_qk = asf(np.concatenate([cosT, sgn * sinT], axis=0))

    f1 = np.arange(T, dtype=np.float64)[:, None] * inv[None, :]
    c32 = asf(np.cos(f1))
    s32 = asf(np.sin(f1))

    mask = np.where(np.arange(128)[None, :] <= np.arange(128)[:, None], 0.0, NEG)

    return {
        "wqaT": asf(np.asarray(inputs["L0_attn_w_qa"]).T),
        "lnq": asf(np.asarray(inputs["L0_attn_q_a_ln_w"]).reshape(NQ, 128).T),
        "wqbT": wqbT,
        "wkvaT": wkvaT,
        "lnkv": asf(np.asarray(inputs["L0_attn_kv_a_ln_w"]).reshape(NKV, 128).T),
        "wkvbT": asf(np.asarray(inputs["L0_attn_w_kvb"]).T),
        "woT": asf(np.asarray(inputs["L0_attn_w_o"]).T),
        "rwT": asf(np.asarray(inputs["L0_ffn_router_weight"]).T),
        "wgT": asf(np.asarray(inputs["L0_ffn_sh_gate"]).T),
        "wuT": asf(np.asarray(inputs["L0_ffn_sh_up"]).T),
        "wrgT": asf(np.asarray(inputs["L0_ffn_r_gate"]).T),
        "wruT": asf(np.asarray(inputs["L0_ffn_r_up"]).T),
        "wdsT": asf(np.concatenate([np.asarray(inputs["L0_ffn_sh_down"]).T,
                                    np.asarray(inputs["L0_ffn_r_down"]).T], axis=0)),
        "WQm": asf(np.asarray(inputs["L1_attn_W_Q"])),
        "hhb": asf(np.broadcast_to(np.asarray(inputs["L1_attn_hh_b"]), (128, D))),
        "gnw": asf(np.broadcast_to(np.asarray(inputs["L1_attn_gn_weight"]), (128, D))),
        "gnb": asf(np.broadcast_to(np.asarray(inputs["L1_attn_gn_bias"]), (128, D))),
        "w1T": asf(np.asarray(inputs["L1_ffn_w1"]).T),
        "w2T": asf(np.asarray(inputs["L1_ffn_w2"]).T),
        "w3T": asf(np.asarray(inputs["L1_ffn_w3"]).T),
        "identf": np.eye(128, dtype=f32),
        "identb": np.eye(128).astype(ml_dtypes.bfloat16),
        "onescol": np.ones((128, 1), dtype=f32),
        "onesrow": np.ones((1, 128), dtype=f32),
        "maskd": asf(mask),
        "cs_qk": cs_qk,
        "c32": c32,
        "s32": s32,
    }


def kernel(**inputs):
    common = _host_prep(inputs)
    x = np.asarray(inputs["L0_attn_hidden_states"], dtype=np.float32)
    nc = build_program()
    in_maps = []
    for i in range(8):
        m = dict(common)
        m["x"] = np.ascontiguousarray(x[i])
        in_maps.append(m)
    res = run_bass_kernel_spmd(nc, in_maps, core_ids=list(range(8)))
    return np.stack([res.results[i]["o"] for i in range(8)], axis=0)


if __name__ == "__main__":
    import time
    t0 = time.time()
    nc = build_program()
    print(f"built ok in {time.time() - t0:.1f}s")
